# version 07
# baseline (speedup 1.0000x reference)
import numpy as np
from contextlib import ExitStack

import concourse.bass as bass
import concourse.mybir as mybir
from concourse.bass_utils import run_bass_kernel_spmd
from concourse.aot_env import allow_ici_all_to_all

F32 = mybir.dt.float32
BF16 = mybir.dt.bfloat16
AF = mybir.ActivationFunctionType
ALU = mybir.AluOpType
AX = mybir.AxisListType.X

D = 2048
NCH = 16
TOK = 1024
HID2 = 16384
DK, DV, M = 64, 128, 64
NTC = 64
EPS = 1e-5

STAGE_FFN1 = True
STAGE_ATTN = True
STAGE_FFN2 = True
ATTN_SUB = 3
SCAN_LVL = 8
DEBUG_DUMP = False
DEBUG_SCAN = False


class Emit:
    def __init__(self, nc, ctx):
        self.nc = nc
        self.t, self.v, self.s, self.q, self.g = nc.tensor, nc.vector, nc.scalar, nc.sync, nc.gpsimd
        self.sem, self.cnt = {}, {}
        for name in ["sd", "sg", "sm", "sv", "sa", "sc"]:
            self.sem[name] = ctx.enter_context(nc.semaphore(name))
            self.cnt[name] = 0
        for name in ["wf0", "wf1", "wp0", "wp1", "wp2", "at0", "at1", "at2",
                     "xt0", "xt1", "wo0", "wo1", "ofm", "xj0", "xj1", "as0", "as1"]:
            self.sem[name] = ctx.enter_context(nc.semaphore(name))
            self.cnt[name] = 0

    def inc(self, instr, name, amt=1):
        if name in ("sd", "sg") and amt == 1:
            amt = 16
        self.cnt[name] += amt
        instr.then_inc(self.sem[name], amt)
        return self.cnt[name]

    def wait(self, eng, name, value=None):
        v = self.cnt[name] if value is None else value
        if v > 0:
            eng.wait_ge(self.sem[name], v)

    def barrier(self):
        engs = {"sd": self.q, "sg": self.g, "sm": self.t, "sv": self.v, "sa": self.s}
        allsems = list(self.sem.keys())
        for ename, eng in engs.items():
            for sname in allsems:
                if sname != ename:
                    self.wait(eng, sname)


def build_program():
    nc = bass.Bass("TRN2", target_bir_lowering=False)

    x_in = nc.dram_tensor("x_in", [D, TOK], F32, kind="ExternalInput").ap()
    wfcT = nc.dram_tensor("wfcT", [D, HID2], F32, kind="ExternalInput").ap()
    wprojT = nc.dram_tensor("wprojT", [HID2, D], F32, kind="ExternalInput").ap()
    wrT = nc.dram_tensor("wrT", [D, 2], F32, kind="ExternalInput").ap()
    wq = nc.dram_tensor("wq", [D, 128], F32, kind="ExternalInput").ap()
    wk = nc.dram_tensor("wk", [D, 128], F32, kind="ExternalInput").ap()
    ws = nc.dram_tensor("ws", [D, 128], F32, kind="ExternalInput").ap()
    wv = nc.dram_tensor("wv", [D, 256], F32, kind="ExternalInput").ap()
    wg = nc.dram_tensor("wg", [D, 256], F32, kind="ExternalInput").ap()
    wo = nc.dram_tensor("wo", [D, D], F32, kind="ExternalInput").ap()
    gnb = nc.dram_tensor("gnb", [128, 256], F32, kind="ExternalInput").ap()
    triu_in = nc.dram_tensor("triu_in", [128, 128], F32, kind="ExternalInput").ap()
    triu2_in = nc.dram_tensor("triu2_in", [128, 256], F32, kind="ExternalInput").ap()
    ident_in = nc.dram_tensor("ident_in", [128, 128], F32, kind="ExternalInput").ap()

    y_out = nc.dram_tensor("y", [D, TOK], F32, kind="ExternalOutput").ap()
    if DEBUG_DUMP:
        dbg_ai = nc.dram_tensor("dbg_ai", [D, TOK], F32, kind="ExternalOutput").ap()
        dbg_ao = nc.dram_tensor("dbg_ao", [D, TOK], F32, kind="ExternalOutput").ap()
        dbg_ah = nc.dram_tensor("dbg_ah", [256, TOK], BF16, kind="ExternalOutput").ap()
        dbg_al = nc.dram_tensor("dbg_al", [256, TOK], BF16, kind="ExternalOutput").ap()
        dbg_xlh = nc.dram_tensor("dbg_xlh", [256, TOK], BF16, kind="ExternalOutput").ap()
        dbg_w0 = nc.dram_tensor("dbg_w0", [128, TOK], F32, kind="ExternalOutput").ap()
    if DEBUG_SCAN:
        dbg_sc = nc.dram_tensor("dbg_sc", [2, 22, 128, 256], F32, kind="ExternalOutput").ap()

    a_bufH = nc.dram_tensor("a_bufH", [HID2, TOK], BF16).ap()
    a_bufL = nc.dram_tensor("a_bufL", [HID2, TOK], BF16).ap()
    ag_in = nc.dram_tensor("ag_in", [D, TOK], F32)
    ag_out = nc.dram_tensor("ag_out", [8 * D, TOK], F32)
    a2a_in = nc.dram_tensor("a2a_in", [D, TOK], F32)
    a2a_out = nc.dram_tensor("a2a_out", [D, TOK], F32)

    with ExitStack() as ctx:
        e = Emit(nc, ctx)
        t, v, s, q, g = e.t, e.v, e.s, e.q, e.g
        _uid = [0]

        def uname(base):
            _uid[0] += 1
            return f"{base}_{_uid[0]}"

        def sb(name, shape):
            return ctx.enter_context(nc.sbuf_tensor(name, shape, F32))

        XACC = sb("XACC", [128, NCH, TOK])
        TRIU = sb("TRIU", [128, 128])
        TRIU2 = sb("TRIU2", [128, 256])
        IDENT = sb("IDENT", [128, 128])
        ONESC = sb("ONESC", [128, 1])
        ONESR = sb("ONESR", [1, 512])
        GNB = sb("GNB", [128, 256])
        MU_BC = sb("MU_BC", [128, TOK])
        R_BC = sb("R_BC", [128, TOK])
        CONSTB = sb("CONSTB", [128, 4])
        e.sq_guard = [0, 0]

        block = ctx.enter_context(nc.Block())

        @block.sync
        def _(_sync):
            e.inc(q.dma_start(out=XACC[:, :, :], in_=x_in.rearrange("(j p) t -> p j t", p=128)), "sd")
            e.inc(q.dma_start(out=TRIU[:, :], in_=triu_in[:, :]), "sd")
            e.inc(q.dma_start(out=TRIU2[:, :], in_=triu2_in[:, :]), "sd")
            e.inc(q.dma_start(out=IDENT[:, :], in_=ident_in[:, :]), "sd")
            e.inc(q.dma_start(out=GNB[:, :], in_=gnb[:, :]), "sd")
            e.wait(v, "sd")
            e.inc(v.memset(ONESC[:, :], 1.0), "sv")
            e.inc(v.memset(ONESR[:, :], 1.0), "sv")
            e.inc(g.memset(CONSTB[:, 0:1], -32.0), "sg", 1)
            e.inc(g.memset(CONSTB[:, 1:2], EPS), "sg", 1)
            e.inc(g.memset(CONSTB[:, 2:3], 1.5), "sg", 1)
            e.barrier()

            def emit_ln_stage():
                with ExitStack() as lctx, nc.named_scope(uname("LN")):
                    pR0 = lctx.enter_context(nc.psum_tensor(uname("pR0"), [1, 512], F32))
                    pR1 = lctx.enter_context(nc.psum_tensor(uname("pR1"), [1, 512], F32))
                    pBC = lctx.enter_context(nc.psum_tensor(uname("pBC"), [128, 512], F32))
                    XSQ = lctx.enter_context(nc.sbuf_tensor(uname("XSQ"), [128, 2, 512], F32))
                    MU = lctx.enter_context(nc.sbuf_tensor(uname("MU"), [1, TOK], F32))
                    RR = lctx.enter_context(nc.sbuf_tensor(uname("RR"), [1, TOK], F32))
                    T0 = lctx.enter_context(nc.sbuf_tensor(uname("T0"), [1, TOK], F32))
                    T1 = lctx.enter_context(nc.sbuf_tensor(uname("T1"), [1, TOK], F32))
                    T2 = lctx.enter_context(nc.sbuf_tensor(uname("T2"), [1, TOK], F32))
                    for vt in range(2):
                        sl = slice(512 * vt, 512 * (vt + 1))
                        for j in range(NCH):
                            if j >= 2:
                                e.wait(s, "sm", e.sq_guard[j % 2])
                            sa_c = e.inc(s.activation(XSQ[:, j % 2, :], XACC[:, j, sl], AF.Square), "sa")
                            t.matmul(pR0[:, :], ONESC[:, 0:1], XACC[:, j, sl],
                                     start=(j == 0), stop=(j == NCH - 1))
                            e.wait(t, "sa", sa_c)
                            mm2 = t.matmul(pR1[:, :], ONESC[:, 0:1], XSQ[:, j % 2, :],
                                           start=(j == 0), stop=(j == NCH - 1))
                            e.sq_guard[j % 2] = e.inc(mm2, "sm")
                        e.wait(s, "sm")
                        s.activation(MU[:, sl], pR0[:, :], AF.Copy, scale=1.0 / D)
                        sa_c = e.inc(s.activation(T0[:, sl], pR1[:, :], AF.Copy, scale=1.0 / D), "sa")
                        e.wait(t, "sa", sa_c)
                    e.wait(v, "sa")
                    e.inc(v.tensor_tensor(T1[:, :], MU[:, :], MU[:, :], ALU.mult), "sv")
                    sv_c = e.inc(v.tensor_tensor(T0[:, :], T0[:, :], T1[:, :], ALU.subtract), "sv")
                    e.wait(s, "sv", sv_c)
                    e.inc(s.activation(T2[:, :], T0[:, :], AF.Copy, bias=EPS), "sa")
                    sa_c = e.inc(s.activation(T1[:, :], T2[:, :], AF.Sqrt), "sa")
                    e.wait(v, "sa", sa_c)
                    e.inc(v.reciprocal(T0[:, :], T1[:, :]), "sv")
                    e.inc(v.tensor_tensor(T1[:, :], T0[:, :], T0[:, :], ALU.mult), "sv")
                    sv_c = e.inc(v.tensor_tensor(T1[:, :], T1[:, :], T2[:, :], ALU.mult), "sv")
                    e.wait(s, "sv", sv_c)
                    sa_c = e.inc(s.activation(T1[:, :], T1[:, :], AF.Copy, scale=-0.5, bias=1.5), "sa")
                    e.wait(v, "sa", sa_c)
                    e.inc(v.tensor_tensor(RR[:, :], T0[:, :], T1[:, :], ALU.mult), "sv")
                    e.wait(t, "sv")
                    for (row, dstbc) in ((MU, MU_BC), (RR, R_BC)):
                        for vt in range(2):
                            sl = slice(512 * vt, 512 * (vt + 1))
                            c1 = e.inc(t.matmul(pBC[:, :], ONESR[0:1, 0:128], row[:, sl],
                                                start=True, stop=True), "sm")
                            e.wait(v, "sm", c1)
                            cv = e.inc(v.tensor_copy(dstbc[:, sl], pBC[:, :]), "sv")
                            e.wait(t, "sv", cv)

            def norm_chunk(dst_ap, j):
                e.inc(v.tensor_tensor(dst_ap, XACC[:, j, :], MU_BC[:, :], ALU.subtract), "sv")
                return e.inc(v.tensor_tensor(dst_ap, dst_ap, R_BC[:, :], ALU.mult), "sv")

            def emit_ffn():
                with ExitStack() as fctx:
                    XLH = fctx.enter_context(nc.sbuf_tensor(uname("XLH"), [128, NCH, TOK], BF16))
                    XLL = fctx.enter_context(nc.sbuf_tensor(uname("XLL"), [128, NCH, TOK], BF16))
                    TMPF = fctx.enter_context(nc.sbuf_tensor(uname("TMPF"), [128, TOK], F32))
                    W0BC = fctx.enter_context(nc.sbuf_tensor(uname("W0BC"), [128, TOK], F32))
                    W1BC = fctx.enter_context(nc.sbuf_tensor(uname("W1BC"), [128, TOK], F32))
                    with ExitStack() as rctx, nc.named_scope(uname("ROUTER")):
                        pDLr = [rctx.enter_context(nc.psum_tensor(uname(f"pDLr{i}"), [1, 512], F32)) for i in range(2)]
                        pBC2 = rctx.enter_context(nc.psum_tensor(uname("pBC2"), [128, 512], F32))
                        WRT = rctx.enter_context(nc.sbuf_tensor(uname("WRT"), [128, NCH, 2], F32))
                        WRDF = rctx.enter_context(nc.sbuf_tensor(uname("WRDF"), [128, NCH], F32))
                        SR0 = rctx.enter_context(nc.sbuf_tensor(uname("SR0"), [1, TOK], F32))
                        SR1 = rctx.enter_context(nc.sbuf_tensor(uname("SR1"), [1, TOK], F32))
                        MR = rctx.enter_context(nc.sbuf_tensor(uname("MR"), [1, TOK], F32))
                        DL = rctx.enter_context(nc.sbuf_tensor(uname("DL"), [1, TOK], F32))
                        W0R = rctx.enter_context(nc.sbuf_tensor(uname("W0R"), [1, TOK], F32))
                        W1R = rctx.enter_context(nc.sbuf_tensor(uname("W1R"), [1, TOK], F32))
                        cdw = e.inc(q.dma_start(out=WRT[:, :, :],
                                                in_=wrT.rearrange("(j p) c -> p j c", p=128)), "sd")
                        e.wait(v, "sd", cdw)
                        e.inc(v.tensor_tensor(WRDF[:, :], WRT[:, :, 1], WRT[:, :, 0], ALU.subtract), "sv")
                        tmp_mm = 0
                        for j in range(NCH):
                            e.wait(v, "sm", tmp_mm)
                            e.inc(v.tensor_tensor(TMPF[:, :], XACC[:, j, :], MU_BC[:, :], ALU.subtract), "sv")
                            sv_t = e.inc(v.tensor_tensor(TMPF[:, :], TMPF[:, :], R_BC[:, :], ALU.mult), "sv")
                            e.inc(v.tensor_copy(XLH[:, j, :], TMPF[:, :]), "sv")
                            e.inc(v.tensor_tensor(XLL[:, j, :], TMPF[:, :], XLH[:, j, :], ALU.subtract), "sv")
                            e.wait(t, "sv", sv_t)
                            t.matmul(pDLr[0][:, :], WRDF[:, j:j + 1], TMPF[:, 0:512],
                                     start=(j == 0), stop=(j == NCH - 1))
                            mm = t.matmul(pDLr[1][:, :], WRDF[:, j:j + 1], TMPF[:, 512:1024],
                                          start=(j == 0), stop=(j == NCH - 1))
                            tmp_mm = e.inc(mm, "sm")
                        e.wait(s, "sm", tmp_mm)
                        s.copy(DL[:, 0:512], pDLr[0][:, :])
                        sa_c = e.inc(s.copy(DL[:, 512:1024], pDLr[1][:, :]), "sa")
                        e.wait(s, "sa", sa_c)
                        e.inc(s.activation(SR1[:, :], DL[:, :], AF.Sigmoid), "sa")
                        sa_c = e.inc(s.activation(SR0[:, :], DL[:, :], AF.Sigmoid, scale=-1.0), "sa")
                        e.wait(v, "sa", sa_c)
                        e.inc(v.tensor_scalar(MR[:, :], DL[:, :], 0.0, None, ALU.is_gt), "sv")
                        e.inc(v.tensor_tensor(W1R[:, :], MR[:, :], SR1[:, :], ALU.mult), "sv")
                        e.inc(v.tensor_scalar(W1R[:, :], W1R[:, :], 0.5, None, ALU.mult), "sv")
                        e.inc(v.tensor_scalar(MR[:, :], MR[:, :], -0.5, 0.5, ALU.mult, ALU.add), "sv")
                        e.inc(v.tensor_tensor(W0R[:, :], MR[:, :], SR0[:, :], ALU.mult), "sv")
                        e.wait(t, "sv")
                        for (row, dstbc) in ((W0R, W0BC), (W1R, W1BC)):
                            for vt in range(2):
                                sl = slice(512 * vt, 512 * (vt + 1))
                                c1 = e.inc(t.matmul(pBC2[:, :], ONESR[0:1, 0:128], row[:, sl],
                                                    start=True, stop=True), "sm")
                                e.wait(v, "sm", c1)
                                cv = e.inc(v.tensor_copy(dstbc[:, sl], pBC2[:, :]), "sv")
                                e.wait(t, "sv", cv)
                    e.barrier()
                    with ExitStack() as actx, nc.named_scope(uname("FFNA")):
                        WFF = actx.enter_context(nc.sbuf_tensor(uname("WFF"), [128, 2, NCH, 128], F32))
                        WFH = actx.enter_context(nc.sbuf_tensor(uname("WFH"), [128, 2, NCH, 128], BF16))
                        WFL = actx.enter_context(nc.sbuf_tensor(uname("WFL"), [128, 2, NCH, 128], BF16))
                        AFP = actx.enter_context(nc.sbuf_tensor(uname("AFP"), [128, 2, 512], F32))
                        ASLH = actx.enter_context(nc.sbuf_tensor(uname("ASLH"), [128, 2, TOK], BF16))
                        ASLL = actx.enter_context(nc.sbuf_tensor(uname("ASLL"), [128, 2, TOK], BF16))
                        AB1 = actx.enter_context(nc.sbuf_tensor(uname("AB1"), [128, 2, 512], F32))
                        TT = actx.enter_context(nc.sbuf_tensor(uname("TT"), [128, 2, 512], F32))
                        pH = [actx.enter_context(nc.psum_tensor(uname(f"pH{i}"), [128, 512], F32)) for i in range(4)]
                        wf_ready = [0, 0]
                        wfh_ready = [0, 0]
                        wfl_done = [0, 0]
                        wf_guard = [0, 0]
                        ph_free = [0, 0, 0, 0]
                        asl_guard = [0, 0]
                        for k in range(128):
                            kb = k % 2
                            wbc = W0BC if k < 64 else W1BC
                            e.wait(q, "sm", wf_guard[kb])
                            wf_ready[kb] = e.inc(
                                q.dma_start(out=WFF[:, kb, :, :],
                                            in_=wfcT[:, 128 * k:128 * (k + 1)]
                                            .rearrange("(j p) h -> p j h", p=128)), f"wf{kb}", 16)
                            e.wait(g, "sm", wf_guard[kb])
                            wfh_ready[kb] = e.inc(
                                g.dma_start(out=WFH[:, kb, :, :],
                                            in_=wfcT[:, 128 * k:128 * (k + 1)]
                                            .rearrange("(j p) h -> p j h", p=128)), f"wo{kb}", 16)
                            e.wait(v, f"wf{kb}", wf_ready[kb])
                            e.wait(v, f"wo{kb}", wfh_ready[kb])
                            e.wait(v, "sm", wf_guard[kb])
                            wfl_done[kb] = e.inc(v.tensor_tensor(WFL[:, kb, :, :], WFF[:, kb, :, :],
                                                                 WFH[:, kb, :, :], ALU.subtract), "sv")
                            for vt in range(2):
                                pb = 2 * kb + vt
                                sl = slice(512 * vt, 512 * (vt + 1))
                                e.wait(t, "sv", wfl_done[kb])
                                e.wait(t, "sv", ph_free[pb])
                                for j in range(NCH):
                                    t.matmul(pH[pb][:, :], WFH[:, kb, j, :], XLH[:, j, sl],
                                             start=(j == 0), stop=False)
                                    t.matmul(pH[pb][:, :], WFH[:, kb, j, :], XLL[:, j, sl],
                                             start=False, stop=False)
                                    mm = t.matmul(pH[pb][:, :], WFL[:, kb, j, :], XLH[:, j, sl],
                                                  start=False, stop=(j == NCH - 1))
                                ph_mm = e.inc(mm, "sm")
                                if vt == 1:
                                    wf_guard[kb] = ph_mm
                                e.wait(s, "sm", ph_mm)
                                sa1 = e.inc(s.activation(AB1[:, vt, :], pH[pb][:, :], AF.Abs, scale=0.25), "sa")
                                e.wait(v, "sa", sa1)
                                e.wait(v, f"as{kb}", asl_guard[kb])
                                sv1 = e.inc(v.scalar_tensor_tensor(TT[:, vt, :], pH[pb][:, :], 0.75,
                                                                   AB1[:, vt, :], ALU.mult, ALU.add), "sv")
                                ph_free[pb] = sv1
                                e.wait(s, "sv", sv1)
                                sa2 = e.inc(s.activation(TT[:, vt, :], TT[:, vt, :], AF.Square), "sa")
                                e.wait(v, "sa", sa2)
                                e.inc(v.tensor_tensor(AFP[:, vt, :], TT[:, vt, :], wbc[:, sl], ALU.mult), "sv")
                                e.inc(v.tensor_copy(ASLH[:, kb, sl], AFP[:, vt, :]), "sv")
                                sv2 = e.inc(v.tensor_tensor(ASLL[:, kb, sl], AFP[:, vt, :],
                                                            ASLH[:, kb, sl], ALU.subtract), "sv")
                            e.wait(g, "sv", sv2)
                            e.inc(g.dma_start(out=a_bufH[128 * k:128 * (k + 1), :], in_=ASLH[:, kb, :]),
                                  f"as{kb}", 16)
                            asl_guard[kb] = e.inc(
                                g.dma_start(out=a_bufL[128 * k:128 * (k + 1), :], in_=ASLL[:, kb, :]),
                                f"as{kb}", 16)
                    e.barrier()
                    with ExitStack() as bctx, nc.named_scope(uname("FFNB")):
                        WPF = bctx.enter_context(nc.sbuf_tensor(uname("WPF"), [128, 3, 1024], F32))
                        WPH = bctx.enter_context(nc.sbuf_tensor(uname("WPH"), [128, 3, 1024], BF16))
                        WPL = bctx.enter_context(nc.sbuf_tensor(uname("WPL"), [128, 3, 1024], BF16))
                        ATH = bctx.enter_context(nc.sbuf_tensor(uname("ATH"), [128, 3, 512], BF16))
                        ATL = bctx.enter_context(nc.sbuf_tensor(uname("ATL"), [128, 3, 512], BF16))
                        pO = [bctx.enter_context(nc.psum_tensor(uname(f"pO{i}"), [128, 512], F32)) for i in range(8)]
                        wp_ready = [0, 0, 0]
                        wph_ready = [0, 0, 0]
                        wpl_done = [0, 0, 0]
                        at_ready = [0, 0, 0]
                        buf_guard = [0, 0, 0]
                        po_free = 0
                        for d2h in range(2):
                            for th in range(2):
                                sl = slice(512 * th, 512 * (th + 1))
                                for k in range(128):
                                    kb = k % 3
                                    e.wait(q, "sm", buf_guard[kb])
                                    wp_ready[kb] = e.inc(
                                        q.dma_start(out=WPF[:, kb, :],
                                                    in_=wprojT[128 * k:128 * (k + 1),
                                                               1024 * d2h:1024 * (d2h + 1)]), f"wp{kb}", 16)
                                    e.wait(g, "sm", buf_guard[kb])
                                    e.inc(g.dma_start(out=WPH[:, kb, :],
                                                      in_=wprojT[128 * k:128 * (k + 1),
                                                                 1024 * d2h:1024 * (d2h + 1)]), f"at{kb}", 16)
                                    e.inc(g.dma_start(out=ATH[:, kb, :],
                                                      in_=a_bufH[128 * k:128 * (k + 1), sl]), f"at{kb}", 16)
                                    at_ready[kb] = e.inc(
                                        g.dma_start(out=ATL[:, kb, :],
                                                    in_=a_bufL[128 * k:128 * (k + 1), sl]), f"at{kb}", 16)
                                    e.wait(v, f"wp{kb}", wp_ready[kb])
                                    e.wait(v, f"at{kb}", at_ready[kb])
                                    e.wait(v, "sm", buf_guard[kb])
                                    wpl_done[kb] = e.inc(v.tensor_tensor(WPL[:, kb, :], WPF[:, kb, :],
                                                                         WPH[:, kb, :], ALU.subtract), "sv")
                                    e.wait(t, "sv", wpl_done[kb])
                                    e.wait(t, f"at{kb}", at_ready[kb])
                                    if k == 0:
                                        e.wait(t, "sv", po_free)
                                    for c8 in range(8):
                                        cs = slice(128 * c8, 128 * (c8 + 1))
                                        t.matmul(pO[c8][:, :], WPH[:, kb, cs], ATH[:, kb, :],
                                                 start=(k == 0), stop=False)
                                        t.matmul(pO[c8][:, :], WPH[:, kb, cs], ATL[:, kb, :],
                                                 start=False, stop=False)
                                        mm = t.matmul(pO[c8][:, :], WPL[:, kb, cs], ATH[:, kb, :],
                                                      start=False, stop=(k == 127))
                                    buf_guard[kb] = e.inc(mm, "sm")
                                e.wait(v, "sm")
                                for c8 in range(8):
                                    j = 8 * d2h + c8
                                    iv = v.tensor_tensor(XACC[:, j, sl], XACC[:, j, sl], pO[c8][:, :], ALU.add)
                                    po_free = e.inc(iv, "sv")
                e.barrier()

            def emit_attn():
                with ExitStack() as nctx, nc.named_scope(uname("LN2AG")):
                    XLJ = nctx.enter_context(nc.sbuf_tensor(uname("XLJ"), [128, 2, TOK], F32))
                    dma_guard = [0, 0]
                    for j in range(NCH):
                        jb = j % 2
                        e.wait(v, f"xj{jb}", dma_guard[jb])
                        cv = norm_chunk(XLJ[:, jb, :], j)
                        e.wait(q, "sv", cv)
                        dma_guard[jb] = e.inc(
                            q.dma_start(out=ag_in.ap()[128 * j:128 * (j + 1), :], in_=XLJ[:, jb, :]), f"xj{jb}", 16)
                    e.wait(g, "xj0")
                    e.wait(g, "xj1")
                cc1 = e.inc(g.collective_compute(
                    "AllGather", ALU.bypass, replica_groups=[list(range(8))],
                    ins=[ag_in.ap().opt()], outs=[ag_out.ap().opt()]), "sc")
                e.wait(g, "sc", cc1)
                e.barrier()
                if ATTN_SUB < 2:
                    return

                with ExitStack() as sctx, nc.named_scope(uname("SCAN")):
                    def ssb(name, shape):
                        return sctx.enter_context(nc.sbuf_tensor(name, shape, F32))
                    WQb, WKb, WSb = ssb("WQb", [128, NCH, 128]), ssb("WKb", [128, NCH, 128]), ssb("WSb", [128, NCH, 128])
                    WVb, WGb = ssb("WVb", [128, NCH, 256]), ssb("WGb", [128, NCH, 256])
                    XLT = ssb("XLT", [128, 2, NCH, 128])
                    QF = ssb(uname("QF"), [64, 2, 2, 128])
                    KT = ssb(uname("KT"), [128, 2, 128])
                    VT = ssb(uname("VT"), [128, 2, 256])
                    GT = ssb(uname("GT"), [128, 2, 256])
                    PT = ssb(uname("PT"), [128, 2, 128])
                    KF = ssb(uname("KF"), [64, 2, 128])
                    PF = ssb(uname("PF"), [64, 2, 128])
                    SM, W2M = ssb(uname("SM"), [128, 256]), ssb(uname("W2M"), [128, 256])
                    OKN, EXPT = ssb(uname("OKN"), [128, 128]), ssb(uname("EXPT"), [128, 128])
                    PV2 = ssb(uname("PV2"), [128, 128])
                    PV2F = ssb(uname("PV2F"), [64, 2, 128])
                    CD, RD = ssb(uname("CD"), [128, 128]), ssb(uname("RD"), [128, 128])
                    ON = ssb(uname("ON"), [128, 2, 256])
                    OF = ssb(uname("OF"), [128, 256])
                    SQG, SG = ssb(uname("SQG"), [128, 256]), ssb(uname("SG"), [128, 256])
                    NM, RSm = ssb(uname("NM"), [128, 2]), ssb(uname("RSm"), [128, 2])
                    GSm, MSm = ssb(uname("GSm"), [128, 2]), ssb(uname("MSm"), [128, 2])
                    RN = ssb(uname("RN"), [128, 2])
                    RSW, RSR = ssb(uname("RSW"), [128, 64]), ssb(uname("RSR"), [128, 64])
                    HST = ssb(uname("HST"), [64, 2, 64])
                    HVST = ssb(uname("HVST"), [64, 2, 128])
                    D0R = ssb(uname("D0R"), [1, 128])
                    pAp = [sctx.enter_context(nc.psum_tensor(uname(f"pA{i}"), [128, 512], F32)) for i in range(2)]
                    pBp = [sctx.enter_context(nc.psum_tensor(uname(f"pB{i}"), [128, 512], F32)) for i in range(2)]
                    pC = sctx.enter_context(nc.psum_tensor(uname("pC"), [128, 512], F32))
                    pD = sctx.enter_context(nc.psum_tensor(uname("pD"), [128, 512], F32))
                    pE = sctx.enter_context(nc.psum_tensor(uname("pE"), [128, 512], F32))
                    pF = sctx.enter_context(nc.psum_tensor(uname("pF"), [128, 512], F32))
                    for (buf, wsrc) in ((WQb, wq), (WKb, wk), (WSb, ws), (WVb, wv), (WGb, wg)):
                        e.inc(q.dma_start(out=buf[:, :, :], in_=wsrc.rearrange("(j p) c -> p j c", p=128)), "sd")
                    e.wait(v, "sd")
                    e.inc(v.memset(RSW[:, :], 1.0), "sv")
                    xlt_ready = [0, 0]
                    proj_mm = [0, 0]
                    ev_b = [0, 0]
                    pt_c = [0, 0]
                    bank_free = [0, 0]
                    tile_free = [0, 0]
                    of_guard = 0
                    st_upd_prev = 0
                    d0_ev_prev = 0
                    sm_ev_prev = 0
                    w2_ev_prev = 0
                    of_c_prev = 0

                    def emit_xlt(tcx):
                        cbx = tcx % 2
                        blkx, subx = tcx // 8, tcx % 8
                        e.wait(q, "sm", proj_mm[cbx])
                        xlt_ready[cbx] = e.inc(
                            q.dma_start(out=XLT[:, cbx, :, :],
                                        in_=ag_out.ap()[2048 * blkx:2048 * (blkx + 1),
                                                        128 * subx:128 * (subx + 1)]
                                        .rearrange("(j p) t -> p j t", p=128)), f"xt{cbx}", 16)

                    def proj_gen(tcx):
                        par = tcx % 2
                        pA, pB = pAp[par], pBp[par]
                        e.wait(t, f"xt{par}", xlt_ready[par])
                        e.wait(t, "sv", bank_free[par])
                        for j in range(NCH):
                            st0 = (j == 0)
                            lst = (j == NCH - 1)
                            t.matmul(pA[:, 256:384], XLT[:, par, j, :], WKb[:, j, :], start=st0, stop=lst)
                            t.matmul(pA[:, 384:512], XLT[:, par, j, :], WSb[:, j, :], start=False, stop=lst)
                            t.matmul(pA[0:64, 0:128], WQb[:, j, 0:64], XLT[:, par, j, :], start=False, stop=lst)
                            t.matmul(pA[0:64, 128:256], WQb[:, j, 64:128], XLT[:, par, j, :], start=False, stop=lst)
                            t.matmul(pB[:, 0:256], XLT[:, par, j, :], WVb[:, j, :], start=st0, stop=lst)
                            mm = t.matmul(pB[:, 256:512], XLT[:, par, j, :], WGb[:, j, :], start=False, stop=lst)
                            if lst:
                                proj_mm[par] = e.inc(mm, "sm")
                            if j % 2 == 1:
                                yield

                    def emit_evacs(tcx):
                        par = tcx % 2
                        pA, pB = pAp[par], pBp[par]
                        e.wait(v, "sm", proj_mm[par])
                        e.wait(v, "sm", tile_free[par])
                        e.inc(v.tensor_scalar(QF[:, par, :, :], pA[0:64, 0:256], 0.125, None, ALU.mult), "sv")
                        e.inc(v.tensor_copy(KT[:, par, :], pA[:, 256:384]), "sv")
                        sv_cl = e.inc(v.tensor_scalar(PT[:, par, :], pA[:, 384:512], -32.0, 32.0,
                                                      ALU.max, ALU.min), "sv")
                        e.inc(v.tensor_copy(VT[:, par, :], pB[:, 0:256]), "sv")
                        ev_b[par] = e.inc(v.tensor_copy(GT[:, par, :], pB[:, 256:512]), "sv")
                        e.wait(s, "sv", sv_cl)
                        pt_c[par] = e.inc(s.activation(PT[:, par, :], PT[:, par, :], AF.Exp,
                                                       bias=CONSTB[:, 0:1]), "sa")

                    def emit_ofinal(tc):
                        nonlocal of_guard, of_c_prev
                        if tc < 0:
                            return
                        par = tc % 2
                        pA = pAp[par]
                        blk, sub = tc // 8, tc % 8
                        e.wait(t, "sv", ov_free_arr[par])
                        t.matmul(pF[:, 256:384], ON[:, par, 0:128], IDENT[:, :], start=False, stop=True,
                                 is_transpose=True)
                        otr = e.inc(t.matmul(pF[:, 384:512], ON[:, par, 128:256], IDENT[:, :], start=False,
                                             stop=True, is_transpose=True), "sm")
                        e.wait(v, "sm", otr)
                        e.wait(v, "sg", of_guard)
                        of_c = e.inc(v.tensor_copy(OF[:, :], pF[:, 256:512]), "sv")
                        of_c_prev = of_c
                        e.wait(g, "sv", of_c)
                        for h in range(2):
                            of_guard = e.inc(g.dma_start(
                                out=a2a_in.ap()[256 * blk + 128 * h:256 * blk + 128 * (h + 1),
                                                128 * sub:128 * (sub + 1)],
                                in_=OF[:, 128 * h:128 * (h + 1)]), "sg", 16)

                    def emit_tail(tc, projit):
                        nonlocal of_guard, st_upd_prev, d0_ev_prev, sm_ev_prev, w2_ev_prev, of_c_prev
                        par = tc % 2
                        pA = pAp[par]
                        blk, sub = tc // 8, tc % 8

                        def pj(n=1):
                            for _ in range(n):
                                try:
                                    next(projit)
                                except StopIteration:
                                    return
                        if tc % 32 == 0:
                            e.wait(v, "sm", st_guard_mm[0])
                            e.inc(v.memset(HST[:, :, :], 0.0), "sv")
                            e.inc(v.memset(HVST[:, :, :], 0.0), "sv")
                            sv_c = e.inc(v.memset(D0R[:, :], 0.0), "sv")
                            e.wait(t, "sv", sv_c)
                        pj(1)
                        e.wait(t, "sv", st_upd_prev)
                        e.wait(t, "sv", d0_ev_prev)
                        e.wait(t, "sv", ev_b[par])
                        e.wait(t, "sa", pt_c[par])
                        t.matmul(pA[0:64, 0:128], KT[:, par, 0:64], IDENT[:, :], start=True, stop=True,
                                 is_transpose=True)
                        t.matmul(pA[0:64, 128:256], KT[:, par, 64:128], IDENT[:, :], start=False, stop=True,
                                 is_transpose=True)
                        t.matmul(pA[0:64, 256:384], PT[:, par, 0:64], IDENT[:, :], start=False, stop=True,
                                 is_transpose=True)
                        tr_mm = e.inc(t.matmul(pA[0:64, 384:512], PT[:, par, 64:128], IDENT[:, :],
                                               start=False, stop=True, is_transpose=True), "sm")
                        e.wait(v, "sm", tr_mm)
                        e.inc(v.tensor_copy(KF[:, :, :], pA[0:64, 0:256]), "sv")
                        tr_ev = e.inc(v.tensor_copy(PF[:, :, :], pA[0:64, 256:512]), "sv")
                        emit_ofinal(tc - 1)
                        pj(1)
                        e.wait(t, "sv", sm_ev_prev)
                        e.wait(t, "sv", tr_ev)
                        for h in range(2):
                            hs = slice(64 * h, 64 * (h + 1))
                            t.matmul(pC[:, hs], TRIU[:, :], PT[:, par, hs], start=(h == 0), stop=False)
                            t.matmul(pC[:, hs], ONESR[0:1, 0:128], D0R[0:1, hs], start=False, stop=True)
                        for h in range(2):
                            hs = slice(64 * h, 64 * (h + 1))
                            d0s = slice(128 + 64 * h, 128 + 64 * (h + 1))
                            t.matmul(pC[0:1, d0s], ONESC[:, 0:1], PT[:, par, hs], start=False, stop=False)
                            mm = t.matmul(pC[0:1, d0s], ONESR[0:1, 0:1], D0R[0:1, hs], start=False, stop=True)
                        cumd_mm = e.inc(mm, "sm")
                        e.wait(v, "sm", cumd_mm)
                        e.inc(v.tensor_copy(CD[:, :], pC[:, 0:128]), "sv")
                        e.inc(v.tensor_copy(D0R[0:1, :], pC[0:1, 128:256]), "sv")
                        rd_ev = e.inc(v.reciprocal(RD[:, :], CD[:, :]), "sv")
                        d0_ev = rd_ev
                        pj(2)
                        e.wait(t, "sv", rd_ev)
                        for h in range(2):
                            mm = t.matmul(pC[:, 256 + 128 * h:256 + 128 * (h + 1)],
                                          KF[:, h, :], QF[:, par, h, :], start=False, stop=True)
                        st_mm = e.inc(mm, "sm")
                        e.wait(v, "sm", st_mm)
                        sm_ev = e.inc(v.tensor_tensor(SM[:, :], pC[:, 256:512], TRIU2[:, :], ALU.mult), "sv")
                        sm_ev_prev = sm_ev
                        pj(1)
                        e.wait(t, "sv", w2_ev_prev)
                        e.wait(t, "sv", sm_ev)
                        for h in range(2):
                            hs = slice(64 * h, 64 * (h + 1))
                            t.matmul(pD[:, hs], QF[:, par, h, :], HST[:, h, :], start=(h == 0), stop=False)
                            mm = t.matmul(pD[:, hs], SM[:, 128 * h:128 * (h + 1)], PT[:, par, hs],
                                          start=False, stop=True)
                        ok_mm = e.inc(mm, "sm")
                        for h in range(2):
                            hs = slice(64 * h, 64 * (h + 1))
                            mm = t.matmul(pE[0:64, hs], KT[:, par, hs], PT[:, par, hs],
                                          start=(h == 0), stop=True)
                        h_mm = e.inc(mm, "sm")
                        e.wait(v, "sm", ok_mm)
                        e.inc(v.tensor_tensor(OKN[:, :], pD[:, 0:128], RD[:, :], ALU.mult), "sv")
                        nm_c = e.inc(v.reduce_max(NM[:, :], OKN.rearrange("p (g x) -> p g x", g=2),
                                                  axis=AX, negate=True), "sv")
                        e.wait(s, "sv", nm_c)
                        sa_c = 0
                        for h in range(2):
                            hs = slice(64 * h, 64 * (h + 1))
                            sa_c = e.inc(s.activation(EXPT[:, hs], OKN[:, hs], AF.Exp,
                                                      bias=NM[:, h:h + 1]), "sa")
                        e.wait(v, "sa", sa_c)
                        rs_c = e.inc(v.reduce_sum(RSm[:, :], EXPT.rearrange("p (g x) -> p g x", g=2),
                                                  axis=AX), "sv")
                        e.wait(s, "sv", rs_c)
                        sa_c = e.inc(s.copy(RSW[:, 0:2], RSm[:, :]), "sa")
                        e.wait(v, "sa", sa_c)
                        e.inc(v.reciprocal(RSR[:, :], RSW[:, :]), "sv")
                        e.inc(v.tensor_tensor(PV2.rearrange("p (g x) -> p g x", g=2),
                                              EXPT.rearrange("p (g x) -> p g x", g=2),
                                              RSR[:, 0:2].rearrange("p (g x) -> p g x", x=1)
                                              .broadcast_to((128, 2, 64)), ALU.mult), "sv")
                        pv2_c = e.inc(v.tensor_tensor(PV2[:, :], PV2[:, :], RD[:, :], ALU.mult), "sv")
                        pj(2)
                        e.wait(t, "sv", pv2_c)
                        t.matmul(pA[0:64, 0:128], PV2[:, 0:64], IDENT[:, :], start=True, stop=True,
                                 is_transpose=True)
                        tr2 = e.inc(t.matmul(pA[0:64, 128:256], PV2[:, 64:128], IDENT[:, :],
                                             start=False, stop=True, is_transpose=True), "sm")
                        e.wait(v, "sm", tr2)
                        pv2f_c = e.inc(v.tensor_copy(PV2F[:, :, :], pA[0:64, 0:256]), "sv")
                        bank_free[par] = pv2f_c
                        pj(1)
                        e.wait(t, "sv", pv2f_c)
                        for h in range(2):
                            mm = t.matmul(pD[:, 128 + 128 * h:128 + 128 * (h + 1)],
                                          PF[:, h, :], PV2F[:, h, :], start=False, stop=True)
                        w2_mm = e.inc(mm, "sm")
                        e.wait(v, "sm", w2_mm)
                        w2_ev = e.inc(v.tensor_tensor(W2M[:, :], pD[:, 128:384], TRIU2[:, :], ALU.mult), "sv")
                        w2_ev_prev = w2_ev
                        pj(1)
                        e.wait(t, "sv", of_c_prev)
                        e.wait(t, "sv", w2_ev)
                        for h in range(2):
                            vs = slice(128 * h, 128 * (h + 1))
                            t.matmul(pF[:, vs], PV2F[:, h, :], HVST[:, h, :], start=(h == 0), stop=False)
                            mm = t.matmul(pF[:, vs], W2M[:, vs], VT[:, par, vs], start=False, stop=True)
                        ov_mm = e.inc(mm, "sm")
                        for h in range(2):
                            vs = slice(128 + 128 * h, 128 + 128 * (h + 1))
                            mm = t.matmul(pE[0:64, vs], PT[:, par, 64 * h:64 * (h + 1)],
                                          VT[:, par, 128 * h:128 * (h + 1)], start=False, stop=True)
                        hv_mm = e.inc(mm, "sm")
                        tile_free[par] = hv_mm
                        st_guard_mm[0] = hv_mm
                        e.wait(v, "sm", hv_mm)
                        e.inc(v.tensor_tensor(HST[:, :, :], HST[:, :, :], pE[0:64, 0:128], ALU.add), "sv")
                        st_upd = e.inc(v.tensor_tensor(HVST[:, :, :], HVST[:, :, :], pE[0:64, 128:384],
                                                       ALU.add), "sv")
                        st_upd_prev = st_upd
                        e.wait(s, "sm", ov_mm)
                        sq_c = e.inc(s.activation(SQG[:, :], pF[:, 0:256], AF.Square), "sa")
                        e.wait(v, "sa", sq_c)
                        gs_c = e.inc(v.reduce_sum(GSm[:, :], SQG.rearrange("p (g x) -> p g x", g=2),
                                                  axis=AX), "sv")
                        e.wait(s, "sv", gs_c)
                        e.inc(s.activation(MSm[:, :], GSm[:, :], AF.Copy, scale=1.0 / DV, bias=EPS), "sa")
                        e.inc(s.activation(RN[:, :], MSm[:, :], AF.Sqrt), "sa")
                        rw_c = e.inc(s.copy(RSW[:, 0:2], RN[:, :]), "sa")
                        e.wait(v, "sa", rw_c)
                        rec_c = e.inc(v.reciprocal(RSR[:, :], RSW[:, :]), "sv")
                        e.wait(s, "sv", rec_c)
                        s.copy(RN[:, :], RSR[:, 0:2])
                        sg_c = e.inc(s.activation(SG[:, :], GT[:, par, :], AF.Silu), "sa")
                        e.wait(v, "sa", sg_c)
                        e.inc(v.tensor_tensor(ON[:, par, :].rearrange("p (g x) -> p g x", g=2),
                                              pF[:, 0:256].rearrange("p (g x) -> p g x", g=2),
                                              RN[:, 0:2].rearrange("p (g x) -> p g x", x=1)
                                              .broadcast_to((128, 2, 128)), ALU.mult), "sv")
                        e.inc(v.tensor_tensor(ON[:, par, :], ON[:, par, :], GNB[:, :], ALU.mult), "sv")
                        ov_free_arr[par] = e.inc(v.tensor_tensor(ON[:, par, :], ON[:, par, :], SG[:, :], ALU.mult), "sv")
                        pj(8)

                    st_guard_mm = [0]
                    ov_free_arr = [0, 0]
                    emit_xlt(0)
                    g0 = proj_gen(0)
                    for _ in g0:
                        pass
                    emit_evacs(0)
                    for tc in range(NTC):
                        if tc + 1 < NTC:
                            emit_xlt(tc + 1)
                            projit = proj_gen(tc + 1)
                        else:
                            projit = iter(())
                        emit_tail(tc, projit)
                        if tc + 1 < NTC:
                            emit_evacs(tc + 1)
                    emit_ofinal(NTC - 1)
                    e.wait(g, "sg")
                e.barrier()
                if SCAN_LVL < 8:
                    return
                with allow_ici_all_to_all():
                    cc2 = e.inc(g.collective_compute(
                        "AllToAll", ALU.bypass, replica_groups=[list(range(8))],
                        ins=[a2a_in.ap().opt()], outs=[a2a_out.ap().opt()]), "sc")
                e.wait(g, "sc", cc2)
                e.barrier()
                if ATTN_SUB < 3:
                    return
                with ExitStack() as wctx, nc.named_scope(uname("WO")):
                    OFM = wctx.enter_context(nc.sbuf_tensor(uname("OFM"), [128, NCH, TOK], F32))
                    WOB = wctx.enter_context(nc.sbuf_tensor(uname("WOB"), [128, 2, NCH, 128], F32))
                    pW = [wctx.enter_context(nc.psum_tensor(uname(f"pW{i}"), [128, 512], F32)) for i in range(4)]
                    cd = e.inc(q.dma_start(out=OFM[:, :, :],
                                           in_=a2a_out.ap().rearrange("(j p) t -> p j t", p=128)), "ofm", 16)
                    wo_ready = [0, 0]
                    wo_guard = [0, 0]
                    pw_free = [0, 0, 0, 0]
                    for d2 in range(NCH):
                        db = d2 % 2
                        e.wait(q, "sm", wo_guard[db])
                        wo_ready[db] = e.inc(
                            q.dma_start(out=WOB[:, db, :, :],
                                        in_=wo[:, 128 * d2:128 * (d2 + 1)]
                                        .rearrange("(j p) c -> p j c", p=128)), f"wo{db}", 16)
                        e.wait(t, f"wo{db}", wo_ready[db])
                        e.wait(t, "ofm", cd)
                        for th in range(2):
                            pb = 2 * db + th
                            sl = slice(512 * th, 512 * (th + 1))
                            e.wait(t, "sv", pw_free[pb])
                            for jv in range(NCH):
                                mm = t.matmul(pW[pb][:, :], WOB[:, db, jv, :], OFM[:, jv, sl],
                                              start=(jv == 0), stop=(jv == NCH - 1))
                            cm = e.inc(mm, "sm")
                            if th == 1:
                                wo_guard[db] = cm
                            e.wait(v, "sm", cm)
                            iv = v.tensor_tensor(XACC[:, d2, sl], XACC[:, d2, sl], pW[pb][:, :], ALU.add)
                            pw_free[pb] = e.inc(iv, "sv")
                e.barrier()

            if STAGE_FFN1:
                emit_ln_stage()
                emit_ffn()
            if STAGE_ATTN:
                emit_ln_stage()
                emit_attn()
            if STAGE_FFN2:
                emit_ln_stage()
                emit_ffn()

            e.barrier()
            cst = e.inc(q.dma_start(out=y_out.rearrange("(j p) t -> p j t", p=128),
                                    in_=XACC[:, :, :]), "sd")
            if DEBUG_DUMP:
                e.inc(q.dma_start(out=dbg_ai[:, :], in_=a2a_in.ap()[:, :]), "sd")
                e.inc(q.dma_start(out=dbg_ao[:, :], in_=a2a_out.ap()[:, :]), "sd")
                e.inc(q.dma_start(out=dbg_ah[:, :], in_=a_bufH[0:256, :]), "sd")
                e.inc(q.dma_start(out=dbg_al[:, :], in_=a_bufL[0:256, :]), "sd")
            e.wait(q, "sd")

    return nc


_NC_CACHE = {}


def _get_nc():
    key = (STAGE_FFN1, STAGE_ATTN, STAGE_FFN2, ATTN_SUB, SCAN_LVL, DEBUG_DUMP, DEBUG_SCAN)
    if key not in _NC_CACHE:
        _NC_CACHE[key] = build_program()
    return _NC_CACHE[key]


def kernel(L0_attn_hidden_states, L0_attn_Wq, L0_attn_Wk, L0_attn_Wv, L0_attn_Wg,
           L0_attn_Ws, L0_attn_Wo, L0_attn_g_norm_weight, L0_ffn_w_router,
           L0_ffn_w_fc, L0_ffn_w_proj):
    x = np.asarray(L0_attn_hidden_states, np.float32).reshape(8192, 2048)
    wfcT_h = np.ascontiguousarray(
        np.transpose(np.asarray(L0_ffn_w_fc, np.float32), (2, 0, 1)).reshape(2048, 16384))
    wprojT_h = np.ascontiguousarray(
        np.transpose(np.asarray(L0_ffn_w_proj, np.float32), (0, 2, 1)).reshape(16384, 2048))
    wrT_h = np.ascontiguousarray(np.asarray(L0_ffn_w_router, np.float32).T)
    Wq = np.asarray(L0_attn_Wq, np.float32)
    Wk = np.asarray(L0_attn_Wk, np.float32)
    Ws = np.asarray(L0_attn_Ws, np.float32)
    Wv = np.asarray(L0_attn_Wv, np.float32)
    Wg = np.asarray(L0_attn_Wg, np.float32)
    Wo_h = np.ascontiguousarray(np.asarray(L0_attn_Wo, np.float32))
    gn = np.asarray(L0_attn_g_norm_weight, np.float32)
    gnb_h = np.ascontiguousarray(np.broadcast_to(np.tile(gn, 2)[None, :], (128, 256)))
    triu_h = np.triu(np.ones((128, 128), np.float32))
    ident_h = np.eye(128, dtype=np.float32)

    in_maps = []
    for c in range(8):
        in_maps.append({
            "x_in": np.ascontiguousarray(x[1024 * c:1024 * (c + 1), :].T),
            "wfcT": wfcT_h, "wprojT": wprojT_h, "wrT": wrT_h,
            "wq": np.ascontiguousarray(Wq[:, 128 * c:128 * (c + 1)]),
            "wk": np.ascontiguousarray(Wk[:, 128 * c:128 * (c + 1)]),
            "ws": np.ascontiguousarray(Ws[:, 128 * c:128 * (c + 1)]),
            "wv": np.ascontiguousarray(Wv[:, 256 * c:256 * (c + 1)]),
            "wg": np.ascontiguousarray(Wg[:, 256 * c:256 * (c + 1)]),
            "wo": Wo_h, "gnb": gnb_h, "triu_in": triu_h, "triu2_in": np.ascontiguousarray(np.tile(triu_h, (1, 2))), "ident_in": ident_h,
        })

    nc = _get_nc()
    res = run_bass_kernel_spmd(nc, in_maps, core_ids=list(range(8)))
    out = np.empty((8192, 2048), np.float32)
    for c in range(8):
        out[1024 * c:1024 * (c + 1), :] = res.results[c]["y"].T
    return out.reshape(2, 4096, 2048)


# version 08
# speedup vs baseline: 1.0323x; 1.0323x over previous
import numpy as np
from contextlib import ExitStack

import concourse.bass as bass
import concourse.mybir as mybir
from concourse.bass_utils import run_bass_kernel_spmd
from concourse.aot_env import allow_ici_all_to_all

F32 = mybir.dt.float32
AF = mybir.ActivationFunctionType
ALU = mybir.AluOpType
AX = mybir.AxisListType.X

D = 2048
NCH = 16
TOK = 1024
HID2 = 16384
DK, DV, M = 64, 128, 64
NTC = 64
EPS = 1e-5

STAGE_FFN1 = True
STAGE_ATTN = True
STAGE_FFN2 = True
ATTN_SUB = 3
SCAN_LVL = 8
DEBUG_DUMP = False
DEBUG_SCAN = False


class Emit:
    def __init__(self, nc, ctx):
        self.nc = nc
        self.t, self.v, self.s, self.q, self.g = nc.tensor, nc.vector, nc.scalar, nc.sync, nc.gpsimd
        self.sem, self.cnt = {}, {}
        for name in ["sd", "sg", "sm", "sv", "sa", "sc"]:
            self.sem[name] = ctx.enter_context(nc.semaphore(name))
            self.cnt[name] = 0
        for name in ["wf0", "wf1", "wp0", "wp1", "wp2", "at0", "at1", "at2",
                     "xt0", "xt1", "wo0", "wo1", "ofm", "xj0", "xj1", "as0", "as1"]:
            self.sem[name] = ctx.enter_context(nc.semaphore(name))
            self.cnt[name] = 0

    def inc(self, instr, name, amt=1):
        if name in ("sd", "sg") and amt == 1:
            amt = 16
        self.cnt[name] += amt
        instr.then_inc(self.sem[name], amt)
        return self.cnt[name]

    def wait(self, eng, name, value=None):
        v = self.cnt[name] if value is None else value
        if v > 0:
            eng.wait_ge(self.sem[name], v)

    def barrier(self):
        engs = {"sd": self.q, "sg": self.g, "sm": self.t, "sv": self.v, "sa": self.s}
        allsems = list(self.sem.keys())
        for ename, eng in engs.items():
            for sname in allsems:
                if sname != ename:
                    self.wait(eng, sname)


def build_program():
    nc = bass.Bass("TRN2", target_bir_lowering=False)

    x_in = nc.dram_tensor("x_in", [D, TOK], F32, kind="ExternalInput").ap()
    wfcT = nc.dram_tensor("wfcT", [D, HID2], F32, kind="ExternalInput").ap()
    wprojT = nc.dram_tensor("wprojT", [HID2, D], F32, kind="ExternalInput").ap()
    wrT = nc.dram_tensor("wrT", [D, 2], F32, kind="ExternalInput").ap()
    wq = nc.dram_tensor("wq", [D, 128], F32, kind="ExternalInput").ap()
    wk = nc.dram_tensor("wk", [D, 128], F32, kind="ExternalInput").ap()
    ws = nc.dram_tensor("ws", [D, 128], F32, kind="ExternalInput").ap()
    wv = nc.dram_tensor("wv", [D, 256], F32, kind="ExternalInput").ap()
    wg = nc.dram_tensor("wg", [D, 256], F32, kind="ExternalInput").ap()
    wo = nc.dram_tensor("wo", [D, D], F32, kind="ExternalInput").ap()
    gnb = nc.dram_tensor("gnb", [128, 256], F32, kind="ExternalInput").ap()
    triu_in = nc.dram_tensor("triu_in", [128, 128], F32, kind="ExternalInput").ap()
    triu2_in = nc.dram_tensor("triu2_in", [128, 256], F32, kind="ExternalInput").ap()
    ident_in = nc.dram_tensor("ident_in", [128, 128], F32, kind="ExternalInput").ap()

    y_out = nc.dram_tensor("y", [D, TOK], F32, kind="ExternalOutput").ap()
    if DEBUG_DUMP:
        dbg_ai = nc.dram_tensor("dbg_ai", [D, TOK], F32, kind="ExternalOutput").ap()
        dbg_ao = nc.dram_tensor("dbg_ao", [D, TOK], F32, kind="ExternalOutput").ap()
    if DEBUG_SCAN:
        dbg_sc = nc.dram_tensor("dbg_sc", [2, 22, 128, 256], F32, kind="ExternalOutput").ap()

    a_buf = nc.dram_tensor("a_buf", [HID2, TOK], F32).ap()
    ag_in = nc.dram_tensor("ag_in", [D, TOK], F32)
    ag_out = nc.dram_tensor("ag_out", [8 * D, TOK], F32)
    a2a_in = nc.dram_tensor("a2a_in", [D, TOK], F32)
    a2a_out = nc.dram_tensor("a2a_out", [D, TOK], F32)

    with ExitStack() as ctx:
        e = Emit(nc, ctx)
        t, v, s, q, g = e.t, e.v, e.s, e.q, e.g
        _uid = [0]

        def uname(base):
            _uid[0] += 1
            return f"{base}_{_uid[0]}"

        def sb(name, shape):
            return ctx.enter_context(nc.sbuf_tensor(name, shape, F32))

        XACC = sb("XACC", [128, NCH, TOK])
        TRIU = sb("TRIU", [128, 128])
        TRIU2 = sb("TRIU2", [128, 256])
        IDENT = sb("IDENT", [128, 128])
        ONESC = sb("ONESC", [128, 1])
        ONESR = sb("ONESR", [1, 512])
        GNB = sb("GNB", [128, 256])
        MU_BC = sb("MU_BC", [128, TOK])
        R_BC = sb("R_BC", [128, TOK])
        CONSTB = sb("CONSTB", [128, 4])
        e.sq_guard = [0, 0]

        block = ctx.enter_context(nc.Block())

        @block.sync
        def _(_sync):
            e.inc(q.dma_start(out=XACC[:, :, :], in_=x_in.rearrange("(j p) t -> p j t", p=128)), "sd")
            e.inc(q.dma_start(out=TRIU[:, :], in_=triu_in[:, :]), "sd")
            e.inc(q.dma_start(out=TRIU2[:, :], in_=triu2_in[:, :]), "sd")
            e.inc(q.dma_start(out=IDENT[:, :], in_=ident_in[:, :]), "sd")
            e.inc(q.dma_start(out=GNB[:, :], in_=gnb[:, :]), "sd")
            e.wait(v, "sd")
            e.inc(v.memset(ONESC[:, :], 1.0), "sv")
            e.inc(v.memset(ONESR[:, :], 1.0), "sv")
            e.inc(g.memset(CONSTB[:, 0:1], -32.0), "sg", 1)
            e.inc(g.memset(CONSTB[:, 1:2], EPS), "sg", 1)
            e.inc(g.memset(CONSTB[:, 2:3], 1.5), "sg", 1)
            e.barrier()

            def emit_ln_stage():
                with ExitStack() as lctx, nc.named_scope(uname("LN")):
                    pR0 = lctx.enter_context(nc.psum_tensor(uname("pR0"), [1, 512], F32))
                    pR1 = lctx.enter_context(nc.psum_tensor(uname("pR1"), [1, 512], F32))
                    pBC = lctx.enter_context(nc.psum_tensor(uname("pBC"), [128, 512], F32))
                    XSQ = lctx.enter_context(nc.sbuf_tensor(uname("XSQ"), [128, 2, 512], F32))
                    MU = lctx.enter_context(nc.sbuf_tensor(uname("MU"), [1, TOK], F32))
                    RR = lctx.enter_context(nc.sbuf_tensor(uname("RR"), [1, TOK], F32))
                    T0 = lctx.enter_context(nc.sbuf_tensor(uname("T0"), [1, TOK], F32))
                    T1 = lctx.enter_context(nc.sbuf_tensor(uname("T1"), [1, TOK], F32))
                    T2 = lctx.enter_context(nc.sbuf_tensor(uname("T2"), [1, TOK], F32))
                    for vt in range(2):
                        sl = slice(512 * vt, 512 * (vt + 1))
                        for j in range(NCH):
                            if j >= 2:
                                e.wait(s, "sm", e.sq_guard[j % 2])
                            sa_c = e.inc(s.activation(XSQ[:, j % 2, :], XACC[:, j, sl], AF.Square), "sa")
                            t.matmul(pR0[:, :], ONESC[:, 0:1], XACC[:, j, sl],
                                     start=(j == 0), stop=(j == NCH - 1))
                            e.wait(t, "sa", sa_c)
                            mm2 = t.matmul(pR1[:, :], ONESC[:, 0:1], XSQ[:, j % 2, :],
                                           start=(j == 0), stop=(j == NCH - 1))
                            e.sq_guard[j % 2] = e.inc(mm2, "sm")
                        e.wait(s, "sm")
                        s.activation(MU[:, sl], pR0[:, :], AF.Copy, scale=1.0 / D)
                        sa_c = e.inc(s.activation(T0[:, sl], pR1[:, :], AF.Copy, scale=1.0 / D), "sa")
                        e.wait(t, "sa", sa_c)
                    e.wait(v, "sa")
                    e.inc(v.tensor_tensor(T1[:, :], MU[:, :], MU[:, :], ALU.mult), "sv")
                    sv_c = e.inc(v.tensor_tensor(T0[:, :], T0[:, :], T1[:, :], ALU.subtract), "sv")
                    e.wait(s, "sv", sv_c)
                    e.inc(s.activation(T2[:, :], T0[:, :], AF.Copy, bias=EPS), "sa")
                    sa_c = e.inc(s.activation(T1[:, :], T2[:, :], AF.Sqrt), "sa")
                    e.wait(v, "sa", sa_c)
                    e.inc(v.reciprocal(T0[:, :], T1[:, :]), "sv")
                    e.inc(v.tensor_tensor(T1[:, :], T0[:, :], T0[:, :], ALU.mult), "sv")
                    sv_c = e.inc(v.tensor_tensor(T1[:, :], T1[:, :], T2[:, :], ALU.mult), "sv")
                    e.wait(s, "sv", sv_c)
                    sa_c = e.inc(s.activation(T1[:, :], T1[:, :], AF.Copy, scale=-0.5, bias=1.5), "sa")
                    e.wait(v, "sa", sa_c)
                    e.inc(v.tensor_tensor(RR[:, :], T0[:, :], T1[:, :], ALU.mult), "sv")
                    e.wait(t, "sv")
                    for (row, dstbc) in ((MU, MU_BC), (RR, R_BC)):
                        for vt in range(2):
                            sl = slice(512 * vt, 512 * (vt + 1))
                            c1 = e.inc(t.matmul(pBC[:, :], ONESR[0:1, 0:128], row[:, sl],
                                                start=True, stop=True), "sm")
                            e.wait(v, "sm", c1)
                            cv = e.inc(v.tensor_copy(dstbc[:, sl], pBC[:, :]), "sv")
                            e.wait(t, "sv", cv)

            def norm_chunk(dst_ap, j):
                e.inc(v.tensor_tensor(dst_ap, XACC[:, j, :], MU_BC[:, :], ALU.subtract), "sv")
                return e.inc(v.tensor_tensor(dst_ap, dst_ap, R_BC[:, :], ALU.mult), "sv")

            def emit_ffn():
                with ExitStack() as fctx:
                    XL = fctx.enter_context(nc.sbuf_tensor(uname("XL"), [128, NCH, TOK], F32))
                    W0BC = fctx.enter_context(nc.sbuf_tensor(uname("W0BC"), [128, TOK], F32))
                    W1BC = fctx.enter_context(nc.sbuf_tensor(uname("W1BC"), [128, TOK], F32))
                    for j in range(NCH):
                        norm_chunk(XL[:, j, :], j)
                    with ExitStack() as rctx, nc.named_scope(uname("ROUTER")):
                        pDLr = rctx.enter_context(nc.psum_tensor(uname("pDLr"), [1, 512], F32))
                        pBC2 = rctx.enter_context(nc.psum_tensor(uname("pBC2"), [128, 512], F32))
                        WRT = rctx.enter_context(nc.sbuf_tensor(uname("WRT"), [128, NCH, 2], F32))
                        WRD = rctx.enter_context(nc.sbuf_tensor(uname("WRD"), [128, NCH], F32))
                        SR0 = rctx.enter_context(nc.sbuf_tensor(uname("SR0"), [1, TOK], F32))
                        SR1 = rctx.enter_context(nc.sbuf_tensor(uname("SR1"), [1, TOK], F32))
                        MR = rctx.enter_context(nc.sbuf_tensor(uname("MR"), [1, TOK], F32))
                        DL = rctx.enter_context(nc.sbuf_tensor(uname("DL"), [1, TOK], F32))
                        W0R = rctx.enter_context(nc.sbuf_tensor(uname("W0R"), [1, TOK], F32))
                        W1R = rctx.enter_context(nc.sbuf_tensor(uname("W1R"), [1, TOK], F32))
                        cd = e.inc(q.dma_start(out=WRT[:, :, :],
                                               in_=wrT.rearrange("(j p) c -> p j c", p=128)), "sd")
                        e.wait(v, "sd", cd)
                        sv_c = e.inc(v.tensor_tensor(WRD[:, :], WRT[:, :, 1], WRT[:, :, 0], ALU.subtract), "sv")
                        e.wait(t, "sv")
                        for vt in range(2):
                            sl = slice(512 * vt, 512 * (vt + 1))
                            for j in range(NCH):
                                mm = t.matmul(pDLr[:, :], WRD[:, j:j + 1], XL[:, j, sl],
                                              start=(j == 0), stop=(j == NCH - 1))
                            cm = e.inc(mm, "sm")
                            e.wait(s, "sm", cm)
                            sa_c = e.inc(s.copy(DL[:, sl], pDLr[:, :]), "sa")
                            e.wait(t, "sa", sa_c)
                        e.wait(s, "sa")
                        e.inc(s.activation(SR1[:, :], DL[:, :], AF.Sigmoid), "sa")
                        sa_c = e.inc(s.activation(SR0[:, :], DL[:, :], AF.Sigmoid, scale=-1.0), "sa")
                        e.wait(v, "sa", sa_c)
                        e.inc(v.tensor_scalar(MR[:, :], DL[:, :], 0.0, None, ALU.is_gt), "sv")
                        e.inc(v.tensor_tensor(W1R[:, :], MR[:, :], SR1[:, :], ALU.mult), "sv")
                        e.inc(v.tensor_scalar(W1R[:, :], W1R[:, :], 0.5, None, ALU.mult), "sv")
                        e.inc(v.tensor_scalar(MR[:, :], MR[:, :], -0.5, 0.5, ALU.mult, ALU.add), "sv")
                        e.inc(v.tensor_tensor(W0R[:, :], MR[:, :], SR0[:, :], ALU.mult), "sv")
                        e.wait(t, "sv")
                        for (row, dstbc) in ((W0R, W0BC), (W1R, W1BC)):
                            for vt in range(2):
                                sl = slice(512 * vt, 512 * (vt + 1))
                                c1 = e.inc(t.matmul(pBC2[:, :], ONESR[0:1, 0:128], row[:, sl],
                                                    start=True, stop=True), "sm")
                                e.wait(v, "sm", c1)
                                cv = e.inc(v.tensor_copy(dstbc[:, sl], pBC2[:, :]), "sv")
                                e.wait(t, "sv", cv)
                    e.barrier()
                    with ExitStack() as actx, nc.named_scope(uname("FFNA")):
                        WF = actx.enter_context(nc.sbuf_tensor(uname("WF"), [128, 2, NCH, 128], F32))
                        ASL = actx.enter_context(nc.sbuf_tensor(uname("ASL"), [128, 2, TOK], F32))
                        AB1 = actx.enter_context(nc.sbuf_tensor(uname("AB1"), [128, 2, 512], F32))
                        TT = actx.enter_context(nc.sbuf_tensor(uname("TT"), [128, 2, 512], F32))
                        pH = [actx.enter_context(nc.psum_tensor(uname(f"pH{i}"), [128, 512], F32)) for i in range(4)]
                        wf_ready = [0, 0]
                        wf_guard = [0, 0]
                        ph_free = [0, 0, 0, 0]
                        ab_guard = [0, 0]
                        asl_guard = [0, 0]
                        for k in range(128):
                            kb = k % 2
                            wbc = W0BC if k < 64 else W1BC
                            e.wait(q, "sm", wf_guard[kb])
                            wf_ready[kb] = e.inc(
                                q.dma_start(out=WF[:, kb, :, :],
                                            in_=wfcT[:, 128 * k:128 * (k + 1)]
                                            .rearrange("(j p) h -> p j h", p=128)), f"wf{kb}", 16)
                            for vt in range(2):
                                pb = 2 * kb + vt
                                sl = slice(512 * vt, 512 * (vt + 1))
                                e.wait(t, f"wf{kb}", wf_ready[kb])
                                e.wait(t, "sv", ph_free[pb])
                                for j in range(NCH):
                                    mm = t.matmul(pH[pb][:, :], WF[:, kb, j, :], XL[:, j, sl],
                                                  start=(j == 0), stop=(j == NCH - 1))
                                ph_mm = e.inc(mm, "sm")
                                if vt == 1:
                                    wf_guard[kb] = ph_mm
                                e.wait(s, "sm", ph_mm)
                                e.wait(s, "sv", ab_guard[vt])
                                sa1 = e.inc(s.activation(AB1[:, vt, :], pH[pb][:, :], AF.Abs, scale=0.25), "sa")
                                e.wait(v, "sa", sa1)
                                e.wait(v, f"as{kb}", asl_guard[kb])
                                sv1 = e.inc(v.scalar_tensor_tensor(TT[:, vt, :], pH[pb][:, :], 0.75,
                                                                   AB1[:, vt, :], ALU.mult, ALU.add), "sv")
                                ph_free[pb] = sv1
                                e.wait(s, "sv", sv1)
                                sa2 = e.inc(s.activation(TT[:, vt, :], TT[:, vt, :], AF.Square), "sa")
                                e.wait(v, "sa", sa2)
                                sv2 = e.inc(v.tensor_tensor(ASL[:, kb, sl], TT[:, vt, :], wbc[:, sl],
                                                            ALU.mult), "sv")
                                ab_guard[vt] = sv2
                            e.wait(g, "sv", sv2)
                            asl_guard[kb] = e.inc(
                                g.dma_start(out=a_buf[128 * k:128 * (k + 1), :], in_=ASL[:, kb, :]),
                                f"as{kb}", 16)
                    e.barrier()
                    with ExitStack() as bctx, nc.named_scope(uname("FFNB")):
                        WP = bctx.enter_context(nc.sbuf_tensor(uname("WP"), [128, 3, 1024], F32))
                        AT = bctx.enter_context(nc.sbuf_tensor(uname("AT"), [128, 3, 512], F32))
                        pO = [bctx.enter_context(nc.psum_tensor(uname(f"pO{i}"), [128, 512], F32)) for i in range(8)]
                        wp_ready = [0, 0, 0]
                        at_ready = [0, 0, 0]
                        buf_guard = [0, 0, 0]
                        po_free = 0
                        for d2h in range(2):
                            for th in range(2):
                                sl = slice(512 * th, 512 * (th + 1))
                                for k in range(128):
                                    kb = k % 3
                                    e.wait(q, "sm", buf_guard[kb])
                                    wp_ready[kb] = e.inc(
                                        q.dma_start(out=WP[:, kb, :],
                                                    in_=wprojT[128 * k:128 * (k + 1),
                                                               1024 * d2h:1024 * (d2h + 1)]), f"wp{kb}", 16)
                                    e.wait(g, "sm", buf_guard[kb])
                                    at_ready[kb] = e.inc(
                                        g.dma_start(out=AT[:, kb, :],
                                                    in_=a_buf[128 * k:128 * (k + 1), sl]), f"at{kb}", 16)
                                    e.wait(t, f"wp{kb}", wp_ready[kb])
                                    e.wait(t, f"at{kb}", at_ready[kb])
                                    if k == 0:
                                        e.wait(t, "sv", po_free)
                                    for c8 in range(8):
                                        mm = t.matmul(pO[c8][:, :], WP[:, kb, 128 * c8:128 * (c8 + 1)],
                                                      AT[:, kb, :], start=(k == 0), stop=(k == 127))
                                    buf_guard[kb] = e.inc(mm, "sm")
                                e.wait(v, "sm")
                                for c8 in range(8):
                                    j = 8 * d2h + c8
                                    iv = v.tensor_tensor(XACC[:, j, sl], XACC[:, j, sl], pO[c8][:, :], ALU.add)
                                    po_free = e.inc(iv, "sv")
                e.barrier()

            def emit_attn():
                with ExitStack() as nctx, nc.named_scope(uname("LN2AG")):
                    XLJ = nctx.enter_context(nc.sbuf_tensor(uname("XLJ"), [128, 2, TOK], F32))
                    dma_guard = [0, 0]
                    for j in range(NCH):
                        jb = j % 2
                        e.wait(v, f"xj{jb}", dma_guard[jb])
                        cv = norm_chunk(XLJ[:, jb, :], j)
                        e.wait(q, "sv", cv)
                        dma_guard[jb] = e.inc(
                            q.dma_start(out=ag_in.ap()[128 * j:128 * (j + 1), :], in_=XLJ[:, jb, :]), f"xj{jb}", 16)
                    e.wait(g, "xj0")
                    e.wait(g, "xj1")
                cc1 = e.inc(g.collective_compute(
                    "AllGather", ALU.bypass, replica_groups=[list(range(8))],
                    ins=[ag_in.ap().opt()], outs=[ag_out.ap().opt()]), "sc")
                e.wait(g, "sc", cc1)
                e.barrier()
                if ATTN_SUB < 2:
                    return

                with ExitStack() as sctx, nc.named_scope(uname("SCAN")):
                    def ssb(name, shape):
                        return sctx.enter_context(nc.sbuf_tensor(name, shape, F32))
                    WQb, WKb, WSb = ssb("WQb", [128, NCH, 128]), ssb("WKb", [128, NCH, 128]), ssb("WSb", [128, NCH, 128])
                    WVb, WGb = ssb("WVb", [128, NCH, 256]), ssb("WGb", [128, NCH, 256])
                    XLT = ssb("XLT", [128, 2, NCH, 128])
                    QF = ssb(uname("QF"), [64, 2, 2, 128])
                    KT = ssb(uname("KT"), [128, 2, 128])
                    VT = ssb(uname("VT"), [128, 2, 256])
                    GT = ssb(uname("GT"), [128, 2, 256])
                    PT = ssb(uname("PT"), [128, 2, 128])
                    KF = ssb(uname("KF"), [64, 2, 128])
                    PF = ssb(uname("PF"), [64, 2, 128])
                    SM, W2M = ssb(uname("SM"), [128, 256]), ssb(uname("W2M"), [128, 256])
                    OKN, EXPT = ssb(uname("OKN"), [128, 128]), ssb(uname("EXPT"), [128, 128])
                    PV2 = ssb(uname("PV2"), [128, 128])
                    PV2F = ssb(uname("PV2F"), [64, 2, 128])
                    CD, RD = ssb(uname("CD"), [128, 128]), ssb(uname("RD"), [128, 128])
                    ON = ssb(uname("ON"), [128, 2, 256])
                    OF = ssb(uname("OF"), [128, 256])
                    SQG = ssb(uname("SQG"), [128, 256])
                    SG = ssb(uname("SG"), [128, 2, 256])
                    NM, RSm = ssb(uname("NM"), [128, 2]), ssb(uname("RSm"), [128, 2])
                    GSm, MSm = ssb(uname("GSm"), [128, 2]), ssb(uname("MSm"), [128, 2])
                    RN = ssb(uname("RN"), [128, 2])
                    RSW, RSR = ssb(uname("RSW"), [128, 64]), ssb(uname("RSR"), [128, 64])
                    HST = ssb(uname("HST"), [64, 2, 64])
                    HVST = ssb(uname("HVST"), [64, 2, 128])
                    D0R = ssb(uname("D0R"), [1, 128])
                    pAp = [sctx.enter_context(nc.psum_tensor(uname(f"pA{i}"), [128, 512], F32)) for i in range(2)]
                    pBp = [sctx.enter_context(nc.psum_tensor(uname(f"pB{i}"), [128, 512], F32)) for i in range(2)]
                    pC = sctx.enter_context(nc.psum_tensor(uname("pC"), [128, 512], F32))
                    pD = sctx.enter_context(nc.psum_tensor(uname("pD"), [128, 512], F32))
                    pE = sctx.enter_context(nc.psum_tensor(uname("pE"), [128, 512], F32))
                    pF = sctx.enter_context(nc.psum_tensor(uname("pF"), [128, 512], F32))
                    for (buf, wsrc) in ((WQb, wq), (WKb, wk), (WSb, ws), (WVb, wv), (WGb, wg)):
                        e.inc(q.dma_start(out=buf[:, :, :], in_=wsrc.rearrange("(j p) c -> p j c", p=128)), "sd")
                    e.wait(v, "sd")
                    e.inc(v.memset(RSW[:, :], 1.0), "sv")
                    xlt_ready = [0, 0]
                    proj_mm = [0, 0]
                    ev_b = [0, 0]
                    pt_c = [0, 0]
                    bank_free = [0, 0]
                    tile_free = [0, 0]
                    of_guard = 0
                    st_upd_prev = 0
                    d0_ev_prev = 0
                    sm_ev_prev = 0
                    w2_ev_prev = 0
                    of_c_prev = 0

                    def emit_xlt(tcx):
                        cbx = tcx % 2
                        blkx, subx = tcx // 8, tcx % 8
                        e.wait(q, "sm", proj_mm[cbx])
                        xlt_ready[cbx] = e.inc(
                            q.dma_start(out=XLT[:, cbx, :, :],
                                        in_=ag_out.ap()[2048 * blkx:2048 * (blkx + 1),
                                                        128 * subx:128 * (subx + 1)]
                                        .rearrange("(j p) t -> p j t", p=128)), f"xt{cbx}", 16)

                    def proj_gen(tcx):
                        par = tcx % 2
                        pA, pB = pAp[par], pBp[par]
                        e.wait(t, f"xt{par}", xlt_ready[par])
                        e.wait(t, "sv", bank_free[par])
                        for j in range(NCH):
                            st0 = (j == 0)
                            lst = (j == NCH - 1)
                            t.matmul(pA[:, 256:384], XLT[:, par, j, :], WKb[:, j, :], start=st0, stop=lst)
                            t.matmul(pA[:, 384:512], XLT[:, par, j, :], WSb[:, j, :], start=False, stop=lst)
                            t.matmul(pA[0:64, 0:128], WQb[:, j, 0:64], XLT[:, par, j, :], start=False, stop=lst)
                            t.matmul(pA[0:64, 128:256], WQb[:, j, 64:128], XLT[:, par, j, :], start=False, stop=lst)
                            t.matmul(pB[:, 0:256], XLT[:, par, j, :], WVb[:, j, :], start=st0, stop=lst)
                            mm = t.matmul(pB[:, 256:512], XLT[:, par, j, :], WGb[:, j, :], start=False, stop=lst)
                            if lst:
                                proj_mm[par] = e.inc(mm, "sm")
                            if j % 2 == 1:
                                yield

                    def emit_evacs(tcx):
                        par = tcx % 2
                        pA, pB = pAp[par], pBp[par]
                        e.wait(v, "sm", proj_mm[par])
                        e.wait(v, "sm", tile_free[par])
                        e.inc(v.tensor_scalar(QF[:, par, 0, :], pA[0:64, 0:128], 0.125, None, ALU.mult), "sv")
                        e.inc(v.tensor_scalar(QF[:, par, 1, :], pA[0:64, 128:256], 0.125, None, ALU.mult), "sv")
                        e.inc(v.tensor_copy(KT[:, par, :], pA[:, 256:384]), "sv")
                        sv_cl = e.inc(v.tensor_scalar(PT[:, par, :], pA[:, 384:512], -32.0, 32.0,
                                                      ALU.max, ALU.min), "sv")
                        e.inc(v.tensor_copy(VT[:, par, :], pB[:, 0:256]), "sv")
                        ev_b[par] = e.inc(v.tensor_copy(GT[:, par, :], pB[:, 256:512]), "sv")
                        e.wait(s, "sv", ev_b[par])
                        e.inc(s.activation(SG[:, par, :], GT[:, par, :], AF.Silu), "sa")
                        pt_c[par] = e.inc(s.activation(PT[:, par, :], PT[:, par, :], AF.Exp,
                                                       bias=CONSTB[:, 0:1]), "sa")

                    def emit_ofinal(tc):
                        nonlocal of_guard, of_c_prev
                        if tc < 0:
                            return
                        par = tc % 2
                        pA = pAp[par]
                        blk, sub = tc // 8, tc % 8
                        e.wait(t, "sv", ov_free_arr[par])
                        t.matmul(pF[:, 256:384], ON[:, par, 0:128], IDENT[:, :], start=False, stop=True,
                                 is_transpose=True)
                        otr = e.inc(t.matmul(pF[:, 384:512], ON[:, par, 128:256], IDENT[:, :], start=False,
                                             stop=True, is_transpose=True), "sm")
                        e.wait(v, "sm", otr)
                        e.wait(v, "sg", of_guard)
                        of_c = e.inc(v.tensor_copy(OF[:, :], pF[:, 256:512]), "sv")
                        of_c_prev = of_c
                        e.wait(g, "sv", of_c)
                        for h in range(2):
                            of_guard = e.inc(g.dma_start(
                                out=a2a_in.ap()[256 * blk + 128 * h:256 * blk + 128 * (h + 1),
                                                128 * sub:128 * (sub + 1)],
                                in_=OF[:, 128 * h:128 * (h + 1)]), "sg", 16)

                    def emit_tail(tc, projit):
                        nonlocal of_guard, st_upd_prev, d0_ev_prev, sm_ev_prev, w2_ev_prev, of_c_prev
                        par = tc % 2
                        pA = pAp[par]
                        blk, sub = tc // 8, tc % 8

                        def pj(n=1):
                            for _ in range(n):
                                try:
                                    next(projit)
                                except StopIteration:
                                    return
                        if tc % 32 == 0:
                            e.wait(v, "sm", st_guard_mm[0])
                            e.inc(v.memset(HST[:, :, :], 0.0), "sv")
                            e.inc(v.memset(HVST[:, :, :], 0.0), "sv")
                            sv_c = e.inc(v.memset(D0R[:, :], 0.0), "sv")
                            e.wait(t, "sv", sv_c)
                        pj(1)
                        e.wait(t, "sv", st_upd_prev)
                        e.wait(t, "sa", d0_ev_prev)
                        e.wait(t, "sv", ev_b[par])
                        e.wait(t, "sa", pt_c[par])
                        t.matmul(pA[0:64, 0:128], KT[:, par, 0:64], IDENT[:, :], start=True, stop=True,
                                 is_transpose=True)
                        t.matmul(pA[0:64, 128:256], KT[:, par, 64:128], IDENT[:, :], start=False, stop=True,
                                 is_transpose=True)
                        t.matmul(pA[0:64, 256:384], PT[:, par, 0:64], IDENT[:, :], start=False, stop=True,
                                 is_transpose=True)
                        tr_mm = e.inc(t.matmul(pA[0:64, 384:512], PT[:, par, 64:128], IDENT[:, :],
                                               start=False, stop=True, is_transpose=True), "sm")
                        e.wait(v, "sm", tr_mm)
                        e.inc(v.tensor_copy(KF[:, 0, :], pA[0:64, 0:128]), "sv")
                        e.inc(v.tensor_copy(KF[:, 1, :], pA[0:64, 128:256]), "sv")
                        e.inc(v.tensor_copy(PF[:, 0, :], pA[0:64, 256:384]), "sv")
                        tr_ev = e.inc(v.tensor_copy(PF[:, 1, :], pA[0:64, 384:512]), "sv")
                        emit_ofinal(tc - 1)
                        pj(1)
                        e.wait(t, "sv", sm_ev_prev)
                        e.wait(t, "sv", tr_ev)
                        for h in range(2):
                            hs = slice(64 * h, 64 * (h + 1))
                            t.matmul(pC[:, hs], TRIU[:, :], PT[:, par, hs], start=(h == 0), stop=False)
                            t.matmul(pC[:, hs], ONESR[0:1, 0:128], D0R[0:1, hs], start=False, stop=True)
                        for h in range(2):
                            hs = slice(64 * h, 64 * (h + 1))
                            d0s = slice(128 + 64 * h, 128 + 64 * (h + 1))
                            t.matmul(pC[0:1, d0s], ONESC[:, 0:1], PT[:, par, hs], start=False, stop=False)
                            mm = t.matmul(pC[0:1, d0s], ONESR[0:1, 0:1], D0R[0:1, hs], start=False, stop=True)
                        cumd_mm = e.inc(mm, "sm")
                        e.wait(v, "sm", cumd_mm)
                        e.inc(v.tensor_copy(CD[:, :], pC[:, 0:128]), "sv")
                        rd_ev = e.inc(v.reciprocal(RD[:, :], CD[:, :]), "sv")
                        e.wait(s, "sm", cumd_mm)
                        e.wait(s, "sv", rd_ev)
                        d0_ev = e.inc(s.copy(D0R[0:1, :], pC[0:1, 128:256]), "sa")
                        pj(2)
                        e.wait(t, "sv", rd_ev)
                        e.wait(t, "sa", d0_ev)
                        for h in range(2):
                            mm = t.matmul(pC[:, 256 + 128 * h:256 + 128 * (h + 1)],
                                          KF[:, h, :], QF[:, par, h, :], start=False, stop=True)
                        st_mm = e.inc(mm, "sm")
                        e.wait(v, "sm", st_mm)
                        sm_ev = e.inc(v.tensor_tensor(SM[:, :], pC[:, 256:512], TRIU2[:, :], ALU.mult), "sv")
                        sm_ev_prev = sm_ev
                        pj(1)
                        e.wait(t, "sv", w2_ev_prev)
                        e.wait(t, "sv", sm_ev)
                        for h in range(2):
                            hs = slice(64 * h, 64 * (h + 1))
                            t.matmul(pD[:, hs], QF[:, par, h, :], HST[:, h, :], start=(h == 0), stop=False)
                            mm = t.matmul(pD[:, hs], SM[:, 128 * h:128 * (h + 1)], PT[:, par, hs],
                                          start=False, stop=True)
                        ok_mm = e.inc(mm, "sm")
                        for h in range(2):
                            hs = slice(64 * h, 64 * (h + 1))
                            mm = t.matmul(pE[0:64, hs], KT[:, par, hs], PT[:, par, hs],
                                          start=(h == 0), stop=True)
                        h_mm = e.inc(mm, "sm")
                        e.wait(v, "sm", ok_mm)
                        e.inc(v.tensor_tensor(OKN[:, :], pD[:, 0:128], RD[:, :], ALU.mult), "sv")
                        nm_c = e.inc(v.reduce_max(NM[:, :], OKN.rearrange("p (g x) -> p g x", g=2),
                                                  axis=AX, negate=True), "sv")
                        e.wait(s, "sv", nm_c)
                        sa_c = 0
                        for h in range(2):
                            hs = slice(64 * h, 64 * (h + 1))
                            sa_c = e.inc(s.activation(EXPT[:, hs], OKN[:, hs], AF.Exp,
                                                      bias=NM[:, h:h + 1]), "sa")
                        e.wait(v, "sa", sa_c)
                        rs_c = e.inc(v.reduce_sum(RSm[:, :], EXPT.rearrange("p (g x) -> p g x", g=2),
                                                  axis=AX), "sv")
                        e.wait(s, "sv", rs_c)
                        sa_c = e.inc(s.copy(RSW[:, 0:2], RSm[:, :]), "sa")
                        e.wait(v, "sa", sa_c)
                        e.inc(v.reciprocal(RSR[:, :], RSW[:, :]), "sv")
                        e.inc(v.tensor_tensor(PV2.rearrange("p (g x) -> p g x", g=2),
                                              EXPT.rearrange("p (g x) -> p g x", g=2),
                                              RSR[:, 0:2].rearrange("p (g x) -> p g x", x=1)
                                              .broadcast_to((128, 2, 64)), ALU.mult), "sv")
                        pv2_c = e.inc(v.tensor_tensor(PV2[:, :], PV2[:, :], RD[:, :], ALU.mult), "sv")
                        pj(2)
                        e.wait(t, "sv", pv2_c)
                        t.matmul(pA[0:64, 0:128], PV2[:, 0:64], IDENT[:, :], start=True, stop=True,
                                 is_transpose=True)
                        tr2 = e.inc(t.matmul(pA[0:64, 128:256], PV2[:, 64:128], IDENT[:, :],
                                             start=False, stop=True, is_transpose=True), "sm")
                        e.wait(v, "sm", tr2)
                        e.inc(v.tensor_copy(PV2F[:, 0, :], pA[0:64, 0:128]), "sv")
                        pv2f_c = e.inc(v.tensor_copy(PV2F[:, 1, :], pA[0:64, 128:256]), "sv")
                        bank_free[par] = pv2f_c
                        pj(1)
                        e.wait(t, "sv", pv2f_c)
                        for h in range(2):
                            mm = t.matmul(pD[:, 128 + 128 * h:128 + 128 * (h + 1)],
                                          PF[:, h, :], PV2F[:, h, :], start=False, stop=True)
                        w2_mm = e.inc(mm, "sm")
                        e.wait(v, "sm", w2_mm)
                        w2_ev = e.inc(v.tensor_tensor(W2M[:, :], pD[:, 128:384], TRIU2[:, :], ALU.mult), "sv")
                        w2_ev_prev = w2_ev
                        pj(1)
                        e.wait(t, "sv", of_c_prev)
                        e.wait(t, "sv", w2_ev)
                        for h in range(2):
                            vs = slice(128 * h, 128 * (h + 1))
                            t.matmul(pF[:, vs], PV2F[:, h, :], HVST[:, h, :], start=(h == 0), stop=False)
                            mm = t.matmul(pF[:, vs], W2M[:, vs], VT[:, par, vs], start=False, stop=True)
                        ov_mm = e.inc(mm, "sm")
                        for h in range(2):
                            vs = slice(128 + 128 * h, 128 + 128 * (h + 1))
                            mm = t.matmul(pE[0:64, vs], PT[:, par, 64 * h:64 * (h + 1)],
                                          VT[:, par, 128 * h:128 * (h + 1)], start=False, stop=True)
                        hv_mm = e.inc(mm, "sm")
                        tile_free[par] = hv_mm
                        st_guard_mm[0] = hv_mm
                        e.wait(v, "sm", hv_mm)
                        e.inc(v.tensor_tensor(HST[:, :, :], HST[:, :, :], pE[0:64, 0:128], ALU.add), "sv")
                        st_upd = e.inc(v.tensor_tensor(HVST[:, :, :], HVST[:, :, :], pE[0:64, 128:384],
                                                       ALU.add), "sv")
                        st_upd_prev = st_upd
                        e.wait(s, "sm", ov_mm)
                        sq_c = e.inc(s.activation(SQG[:, :], pF[:, 0:256], AF.Square), "sa")
                        e.wait(v, "sa", sq_c)
                        gs_c = e.inc(v.reduce_sum(GSm[:, :], SQG.rearrange("p (g x) -> p g x", g=2),
                                                  axis=AX), "sv")
                        e.wait(s, "sv", gs_c)
                        e.inc(s.activation(MSm[:, :], GSm[:, :], AF.Copy, scale=1.0 / DV, bias=EPS), "sa")
                        rw_c = e.inc(s.activation(RSW[:, 0:2], MSm[:, :], AF.Sqrt), "sa")
                        e.wait(v, "sa", rw_c)
                        e.inc(v.reciprocal(RSR[:, :], RSW[:, :]), "sv")
                        e.inc(v.tensor_tensor(ON[:, par, :].rearrange("p (g x) -> p g x", g=2),
                                              pF[:, 0:256].rearrange("p (g x) -> p g x", g=2),
                                              RSR[:, 0:2].rearrange("p (g x) -> p g x", x=1)
                                              .broadcast_to((128, 2, 128)), ALU.mult), "sv")
                        e.inc(v.tensor_tensor(ON[:, par, :], ON[:, par, :], GNB[:, :], ALU.mult), "sv")
                        ov_free_arr[par] = e.inc(v.tensor_tensor(ON[:, par, :], ON[:, par, :], SG[:, par, :], ALU.mult), "sv")
                        pj(8)

                    st_guard_mm = [0]
                    ov_free_arr = [0, 0]
                    emit_xlt(0)
                    g0 = proj_gen(0)
                    for _ in g0:
                        pass
                    emit_evacs(0)
                    for tc in range(NTC):
                        if tc + 1 < NTC:
                            emit_xlt(tc + 1)
                            projit = proj_gen(tc + 1)
                        else:
                            projit = iter(())
                        emit_tail(tc, projit)
                        if tc + 1 < NTC:
                            emit_evacs(tc + 1)
                    emit_ofinal(NTC - 1)
                    e.wait(g, "sg")
                e.barrier()
                if SCAN_LVL < 8:
                    return
                with allow_ici_all_to_all():
                    cc2 = e.inc(g.collective_compute(
                        "AllToAll", ALU.bypass, replica_groups=[list(range(8))],
                        ins=[a2a_in.ap().opt()], outs=[a2a_out.ap().opt()]), "sc")
                e.wait(g, "sc", cc2)
                e.barrier()
                if ATTN_SUB < 3:
                    return
                with ExitStack() as wctx, nc.named_scope(uname("WO")):
                    OFM = wctx.enter_context(nc.sbuf_tensor(uname("OFM"), [128, NCH, TOK], F32))
                    WOB = wctx.enter_context(nc.sbuf_tensor(uname("WOB"), [128, 2, NCH, 128], F32))
                    pW = [wctx.enter_context(nc.psum_tensor(uname(f"pW{i}"), [128, 512], F32)) for i in range(4)]
                    cd = e.inc(q.dma_start(out=OFM[:, :, :],
                                           in_=a2a_out.ap().rearrange("(j p) t -> p j t", p=128)), "ofm", 16)
                    wo_ready = [0, 0]
                    wo_guard = [0, 0]
                    pw_free = [0, 0, 0, 0]
                    for d2 in range(NCH):
                        db = d2 % 2
                        e.wait(q, "sm", wo_guard[db])
                        wo_ready[db] = e.inc(
                            q.dma_start(out=WOB[:, db, :, :],
                                        in_=wo[:, 128 * d2:128 * (d2 + 1)]
                                        .rearrange("(j p) c -> p j c", p=128)), f"wo{db}", 16)
                        e.wait(t, f"wo{db}", wo_ready[db])
                        e.wait(t, "ofm", cd)
                        for th in range(2):
                            pb = 2 * db + th
                            sl = slice(512 * th, 512 * (th + 1))
                            e.wait(t, "sv", pw_free[pb])
                            for jv in range(NCH):
                                mm = t.matmul(pW[pb][:, :], WOB[:, db, jv, :], OFM[:, jv, sl],
                                              start=(jv == 0), stop=(jv == NCH - 1))
                            cm = e.inc(mm, "sm")
                            if th == 1:
                                wo_guard[db] = cm
                            e.wait(v, "sm", cm)
                            iv = v.tensor_tensor(XACC[:, d2, sl], XACC[:, d2, sl], pW[pb][:, :], ALU.add)
                            pw_free[pb] = e.inc(iv, "sv")
                e.barrier()

            if STAGE_FFN1:
                emit_ln_stage()
                emit_ffn()
            if STAGE_ATTN:
                emit_ln_stage()
                emit_attn()
            if STAGE_FFN2:
                emit_ln_stage()
                emit_ffn()

            e.barrier()
            cst = e.inc(q.dma_start(out=y_out.rearrange("(j p) t -> p j t", p=128),
                                    in_=XACC[:, :, :]), "sd")
            if DEBUG_DUMP:
                e.inc(q.dma_start(out=dbg_ai[:, :], in_=a2a_in.ap()[:, :]), "sd")
                e.inc(q.dma_start(out=dbg_ao[:, :], in_=a2a_out.ap()[:, :]), "sd")
            e.wait(q, "sd")

    return nc


_NC_CACHE = {}


def _get_nc():
    key = (STAGE_FFN1, STAGE_ATTN, STAGE_FFN2, ATTN_SUB, SCAN_LVL, DEBUG_DUMP, DEBUG_SCAN)
    if key not in _NC_CACHE:
        _NC_CACHE[key] = build_program()
    return _NC_CACHE[key]


def kernel(L0_attn_hidden_states, L0_attn_Wq, L0_attn_Wk, L0_attn_Wv, L0_attn_Wg,
           L0_attn_Ws, L0_attn_Wo, L0_attn_g_norm_weight, L0_ffn_w_router,
           L0_ffn_w_fc, L0_ffn_w_proj):
    x = np.asarray(L0_attn_hidden_states, np.float32).reshape(8192, 2048)
    wfcT_h = np.ascontiguousarray(
        np.transpose(np.asarray(L0_ffn_w_fc, np.float32), (2, 0, 1)).reshape(2048, 16384))
    wprojT_h = np.ascontiguousarray(
        np.transpose(np.asarray(L0_ffn_w_proj, np.float32), (0, 2, 1)).reshape(16384, 2048))
    wrT_h = np.ascontiguousarray(np.asarray(L0_ffn_w_router, np.float32).T)
    Wq = np.asarray(L0_attn_Wq, np.float32)
    Wk = np.asarray(L0_attn_Wk, np.float32)
    Ws = np.asarray(L0_attn_Ws, np.float32)
    Wv = np.asarray(L0_attn_Wv, np.float32)
    Wg = np.asarray(L0_attn_Wg, np.float32)
    Wo_h = np.ascontiguousarray(np.asarray(L0_attn_Wo, np.float32))
    gn = np.asarray(L0_attn_g_norm_weight, np.float32)
    gnb_h = np.ascontiguousarray(np.broadcast_to(np.tile(gn, 2)[None, :], (128, 256)))
    triu_h = np.triu(np.ones((128, 128), np.float32))
    ident_h = np.eye(128, dtype=np.float32)

    in_maps = []
    for c in range(8):
        in_maps.append({
            "x_in": np.ascontiguousarray(x[1024 * c:1024 * (c + 1), :].T),
            "wfcT": wfcT_h, "wprojT": wprojT_h, "wrT": wrT_h,
            "wq": np.ascontiguousarray(Wq[:, 128 * c:128 * (c + 1)]),
            "wk": np.ascontiguousarray(Wk[:, 128 * c:128 * (c + 1)]),
            "ws": np.ascontiguousarray(Ws[:, 128 * c:128 * (c + 1)]),
            "wv": np.ascontiguousarray(Wv[:, 256 * c:256 * (c + 1)]),
            "wg": np.ascontiguousarray(Wg[:, 256 * c:256 * (c + 1)]),
            "wo": Wo_h, "gnb": gnb_h, "triu_in": triu_h, "triu2_in": np.ascontiguousarray(np.tile(triu_h, (1, 2))), "ident_in": ident_h,
        })

    nc = _get_nc()
    res = run_bass_kernel_spmd(nc, in_maps, core_ids=list(range(8)))
    out = np.empty((8192, 2048), np.float32)
    for c in range(8):
        out[1024 * c:1024 * (c + 1), :] = res.results[c]["y"].T
    return out.reshape(2, 4096, 2048)


# version 09
# speedup vs baseline: 1.0404x; 1.0079x over previous
import numpy as np
from contextlib import ExitStack

import concourse.bass as bass
import concourse.mybir as mybir
from concourse.bass_utils import run_bass_kernel_spmd
from concourse.aot_env import allow_ici_all_to_all

F32 = mybir.dt.float32
AF = mybir.ActivationFunctionType
ALU = mybir.AluOpType
AX = mybir.AxisListType.X

D = 2048
NCH = 16
TOK = 1024
HID2 = 16384
DK, DV, M = 64, 128, 64
NTC = 64
EPS = 1e-5

STAGE_FFN1 = True
STAGE_ATTN = True
STAGE_FFN2 = True
ATTN_SUB = 3
SCAN_LVL = 8
DEBUG_DUMP = False
DEBUG_SCAN = False


class Emit:
    def __init__(self, nc, ctx):
        self.nc = nc
        self.t, self.v, self.s, self.q, self.g = nc.tensor, nc.vector, nc.scalar, nc.sync, nc.gpsimd
        self.sem, self.cnt = {}, {}
        for name in ["sd", "sg", "sm", "sv", "sa", "sc"]:
            self.sem[name] = ctx.enter_context(nc.semaphore(name))
            self.cnt[name] = 0
        for name in ["wf0", "wf1", "wp0", "wp1", "wp2", "at0", "at1", "at2",
                     "xt0", "xt1", "wo0", "wo1", "ofm", "xj0", "xj1", "as0", "as1"]:
            self.sem[name] = ctx.enter_context(nc.semaphore(name))
            self.cnt[name] = 0

    def inc(self, instr, name, amt=1):
        if name in ("sd", "sg") and amt == 1:
            amt = 16
        self.cnt[name] += amt
        instr.then_inc(self.sem[name], amt)
        return self.cnt[name]

    def wait(self, eng, name, value=None):
        v = self.cnt[name] if value is None else value
        if v > 0:
            eng.wait_ge(self.sem[name], v)

    def barrier(self):
        engs = {"sd": self.q, "sg": self.g, "sm": self.t, "sv": self.v, "sa": self.s}
        allsems = list(self.sem.keys())
        for ename, eng in engs.items():
            for sname in allsems:
                if sname != ename:
                    self.wait(eng, sname)


def build_program():
    nc = bass.Bass("TRN2", target_bir_lowering=False)

    x_in = nc.dram_tensor("x_in", [D, TOK], F32, kind="ExternalInput").ap()
    wfcT = nc.dram_tensor("wfcT", [D, HID2], F32, kind="ExternalInput").ap()
    wprojT = nc.dram_tensor("wprojT", [HID2, D], F32, kind="ExternalInput").ap()
    wrT = nc.dram_tensor("wrT", [D, 2], F32, kind="ExternalInput").ap()
    wq = nc.dram_tensor("wq", [D, 128], F32, kind="ExternalInput").ap()
    wk = nc.dram_tensor("wk", [D, 128], F32, kind="ExternalInput").ap()
    ws = nc.dram_tensor("ws", [D, 128], F32, kind="ExternalInput").ap()
    wv = nc.dram_tensor("wv", [D, 256], F32, kind="ExternalInput").ap()
    wg = nc.dram_tensor("wg", [D, 256], F32, kind="ExternalInput").ap()
    wo = nc.dram_tensor("wo", [D, D], F32, kind="ExternalInput").ap()
    gnb = nc.dram_tensor("gnb", [128, 256], F32, kind="ExternalInput").ap()
    triu_in = nc.dram_tensor("triu_in", [128, 128], F32, kind="ExternalInput").ap()
    triu2_in = nc.dram_tensor("triu2_in", [128, 256], F32, kind="ExternalInput").ap()
    ident_in = nc.dram_tensor("ident_in", [128, 128], F32, kind="ExternalInput").ap()

    y_out = nc.dram_tensor("y", [D, TOK], F32, kind="ExternalOutput").ap()
    if DEBUG_DUMP:
        dbg_ai = nc.dram_tensor("dbg_ai", [D, TOK], F32, kind="ExternalOutput").ap()
        dbg_ao = nc.dram_tensor("dbg_ao", [D, TOK], F32, kind="ExternalOutput").ap()
    if DEBUG_SCAN:
        dbg_sc = nc.dram_tensor("dbg_sc", [2, 22, 128, 256], F32, kind="ExternalOutput").ap()

    a_buf = nc.dram_tensor("a_buf", [HID2, TOK], F32).ap()
    ag_in = nc.dram_tensor("ag_in", [D, TOK], F32)
    ag_out = nc.dram_tensor("ag_out", [8 * D, TOK], F32)
    a2a_in = nc.dram_tensor("a2a_in", [D, TOK], F32)
    a2a_out = nc.dram_tensor("a2a_out", [D, TOK], F32)

    with ExitStack() as ctx:
        e = Emit(nc, ctx)
        t, v, s, q, g = e.t, e.v, e.s, e.q, e.g
        _uid = [0]

        def uname(base):
            _uid[0] += 1
            return f"{base}_{_uid[0]}"

        def sb(name, shape):
            return ctx.enter_context(nc.sbuf_tensor(name, shape, F32))

        XACC = sb("XACC", [128, NCH, TOK])
        TRIU = sb("TRIU", [128, 128])
        TRIU2 = sb("TRIU2", [128, 256])
        IDENT = sb("IDENT", [128, 128])
        ONESC = sb("ONESC", [128, 1])
        ONESR = sb("ONESR", [1, 512])
        GNB = sb("GNB", [128, 256])
        MU_BC = sb("MU_BC", [128, TOK])
        R_BC = sb("R_BC", [128, TOK])
        CONSTB = sb("CONSTB", [128, 4])
        e.sq_guard = [0, 0]

        block = ctx.enter_context(nc.Block())

        @block.sync
        def _(_sync):
            e.inc(q.dma_start(out=XACC[:, :, :], in_=x_in.rearrange("(j p) t -> p j t", p=128)), "sd")
            e.inc(q.dma_start(out=TRIU[:, :], in_=triu_in[:, :]), "sd")
            e.inc(q.dma_start(out=TRIU2[:, :], in_=triu2_in[:, :]), "sd")
            e.inc(q.dma_start(out=IDENT[:, :], in_=ident_in[:, :]), "sd")
            e.inc(q.dma_start(out=GNB[:, :], in_=gnb[:, :]), "sd")
            e.wait(v, "sd")
            e.inc(v.memset(ONESC[:, :], 1.0), "sv")
            e.inc(v.memset(ONESR[:, :], 1.0), "sv")
            e.inc(g.memset(CONSTB[:, 0:1], -32.0), "sg", 1)
            e.inc(g.memset(CONSTB[:, 1:2], EPS), "sg", 1)
            e.inc(g.memset(CONSTB[:, 2:3], 1.5), "sg", 1)
            e.barrier()

            def emit_ln_stage():
                with ExitStack() as lctx, nc.named_scope(uname("LN")):
                    pR0 = lctx.enter_context(nc.psum_tensor(uname("pR0"), [1, 512], F32))
                    pR1 = lctx.enter_context(nc.psum_tensor(uname("pR1"), [1, 512], F32))
                    pBC = lctx.enter_context(nc.psum_tensor(uname("pBC"), [128, 512], F32))
                    XSQ = lctx.enter_context(nc.sbuf_tensor(uname("XSQ"), [128, 2, 512], F32))
                    MU = lctx.enter_context(nc.sbuf_tensor(uname("MU"), [1, TOK], F32))
                    RR = lctx.enter_context(nc.sbuf_tensor(uname("RR"), [1, TOK], F32))
                    T0 = lctx.enter_context(nc.sbuf_tensor(uname("T0"), [1, TOK], F32))
                    T1 = lctx.enter_context(nc.sbuf_tensor(uname("T1"), [1, TOK], F32))
                    T2 = lctx.enter_context(nc.sbuf_tensor(uname("T2"), [1, TOK], F32))
                    for vt in range(2):
                        sl = slice(512 * vt, 512 * (vt + 1))
                        for j in range(NCH):
                            if j >= 2:
                                e.wait(s, "sm", e.sq_guard[j % 2])
                            sa_c = e.inc(s.activation(XSQ[:, j % 2, :], XACC[:, j, sl], AF.Square), "sa")
                            t.matmul(pR0[:, :], ONESC[:, 0:1], XACC[:, j, sl],
                                     start=(j == 0), stop=(j == NCH - 1))
                            e.wait(t, "sa", sa_c)
                            mm2 = t.matmul(pR1[:, :], ONESC[:, 0:1], XSQ[:, j % 2, :],
                                           start=(j == 0), stop=(j == NCH - 1))
                            e.sq_guard[j % 2] = e.inc(mm2, "sm")
                        e.wait(s, "sm")
                        s.activation(MU[:, sl], pR0[:, :], AF.Copy, scale=1.0 / D)
                        sa_c = e.inc(s.activation(T0[:, sl], pR1[:, :], AF.Copy, scale=1.0 / D), "sa")
                        e.wait(t, "sa", sa_c)
                    e.wait(v, "sa")
                    e.inc(v.tensor_tensor(T1[:, :], MU[:, :], MU[:, :], ALU.mult), "sv")
                    sv_c = e.inc(v.tensor_tensor(T0[:, :], T0[:, :], T1[:, :], ALU.subtract), "sv")
                    e.wait(s, "sv", sv_c)
                    e.inc(s.activation(T2[:, :], T0[:, :], AF.Copy, bias=EPS), "sa")
                    sa_c = e.inc(s.activation(T1[:, :], T2[:, :], AF.Sqrt), "sa")
                    e.wait(v, "sa", sa_c)
                    e.inc(v.reciprocal(T0[:, :], T1[:, :]), "sv")
                    e.inc(v.tensor_tensor(T1[:, :], T0[:, :], T0[:, :], ALU.mult), "sv")
                    sv_c = e.inc(v.tensor_tensor(T1[:, :], T1[:, :], T2[:, :], ALU.mult), "sv")
                    e.wait(s, "sv", sv_c)
                    sa_c = e.inc(s.activation(T1[:, :], T1[:, :], AF.Copy, scale=-0.5, bias=1.5), "sa")
                    e.wait(v, "sa", sa_c)
                    e.inc(v.tensor_tensor(RR[:, :], T0[:, :], T1[:, :], ALU.mult), "sv")
                    e.wait(t, "sv")
                    for (row, dstbc) in ((MU, MU_BC), (RR, R_BC)):
                        for vt in range(2):
                            sl = slice(512 * vt, 512 * (vt + 1))
                            c1 = e.inc(t.matmul(pBC[:, :], ONESR[0:1, 0:128], row[:, sl],
                                                start=True, stop=True), "sm")
                            e.wait(v, "sm", c1)
                            cv = e.inc(v.tensor_copy(dstbc[:, sl], pBC[:, :]), "sv")
                            e.wait(t, "sv", cv)

            def norm_chunk(dst_ap, j):
                e.inc(v.tensor_tensor(dst_ap, XACC[:, j, :], MU_BC[:, :], ALU.subtract), "sv")
                return e.inc(v.tensor_tensor(dst_ap, dst_ap, R_BC[:, :], ALU.mult), "sv")

            def emit_ffn():
                with ExitStack() as fctx:
                    XL = fctx.enter_context(nc.sbuf_tensor(uname("XL"), [128, NCH, TOK], F32))
                    W0BC = fctx.enter_context(nc.sbuf_tensor(uname("W0BC"), [128, TOK], F32))
                    W1BC = fctx.enter_context(nc.sbuf_tensor(uname("W1BC"), [128, TOK], F32))
                    for j in range(NCH):
                        norm_chunk(XL[:, j, :], j)
                    with ExitStack() as rctx, nc.named_scope(uname("ROUTER")):
                        pDLr = rctx.enter_context(nc.psum_tensor(uname("pDLr"), [1, 512], F32))
                        pBC2 = rctx.enter_context(nc.psum_tensor(uname("pBC2"), [128, 512], F32))
                        WRT = rctx.enter_context(nc.sbuf_tensor(uname("WRT"), [128, NCH, 2], F32))
                        WRD = rctx.enter_context(nc.sbuf_tensor(uname("WRD"), [128, NCH], F32))
                        SR0 = rctx.enter_context(nc.sbuf_tensor(uname("SR0"), [1, TOK], F32))
                        SR1 = rctx.enter_context(nc.sbuf_tensor(uname("SR1"), [1, TOK], F32))
                        MR = rctx.enter_context(nc.sbuf_tensor(uname("MR"), [1, TOK], F32))
                        DL = rctx.enter_context(nc.sbuf_tensor(uname("DL"), [1, TOK], F32))
                        W0R = rctx.enter_context(nc.sbuf_tensor(uname("W0R"), [1, TOK], F32))
                        W1R = rctx.enter_context(nc.sbuf_tensor(uname("W1R"), [1, TOK], F32))
                        cd = e.inc(q.dma_start(out=WRT[:, :, :],
                                               in_=wrT.rearrange("(j p) c -> p j c", p=128)), "sd")
                        e.wait(v, "sd", cd)
                        sv_c = e.inc(v.tensor_tensor(WRD[:, :], WRT[:, :, 1], WRT[:, :, 0], ALU.subtract), "sv")
                        e.wait(t, "sv")
                        for vt in range(2):
                            sl = slice(512 * vt, 512 * (vt + 1))
                            for j in range(NCH):
                                mm = t.matmul(pDLr[:, :], WRD[:, j:j + 1], XL[:, j, sl],
                                              start=(j == 0), stop=(j == NCH - 1))
                            cm = e.inc(mm, "sm")
                            e.wait(s, "sm", cm)
                            sa_c = e.inc(s.copy(DL[:, sl], pDLr[:, :]), "sa")
                            e.wait(t, "sa", sa_c)
                        e.wait(s, "sa")
                        e.inc(s.activation(SR1[:, :], DL[:, :], AF.Sigmoid), "sa")
                        sa_c = e.inc(s.activation(SR0[:, :], DL[:, :], AF.Sigmoid, scale=-1.0), "sa")
                        e.wait(v, "sa", sa_c)
                        e.inc(v.tensor_scalar(MR[:, :], DL[:, :], 0.0, None, ALU.is_gt), "sv")
                        e.inc(v.tensor_tensor(W1R[:, :], MR[:, :], SR1[:, :], ALU.mult), "sv")
                        e.inc(v.tensor_scalar(W1R[:, :], W1R[:, :], 0.5, None, ALU.mult), "sv")
                        e.inc(v.tensor_scalar(MR[:, :], MR[:, :], -0.5, 0.5, ALU.mult, ALU.add), "sv")
                        e.inc(v.tensor_tensor(W0R[:, :], MR[:, :], SR0[:, :], ALU.mult), "sv")
                        e.wait(t, "sv")
                        for (row, dstbc) in ((W0R, W0BC), (W1R, W1BC)):
                            for vt in range(2):
                                sl = slice(512 * vt, 512 * (vt + 1))
                                c1 = e.inc(t.matmul(pBC2[:, :], ONESR[0:1, 0:128], row[:, sl],
                                                    start=True, stop=True), "sm")
                                e.wait(v, "sm", c1)
                                cv = e.inc(v.tensor_copy(dstbc[:, sl], pBC2[:, :]), "sv")
                                e.wait(t, "sv", cv)
                    e.barrier()
                    with ExitStack() as actx, nc.named_scope(uname("FFNA")):
                        WF = actx.enter_context(nc.sbuf_tensor(uname("WF"), [128, 2, NCH, 128], F32))
                        ASL = actx.enter_context(nc.sbuf_tensor(uname("ASL"), [128, 2, TOK], F32))
                        AB1 = actx.enter_context(nc.sbuf_tensor(uname("AB1"), [128, 2, 512], F32))
                        TT = actx.enter_context(nc.sbuf_tensor(uname("TT"), [128, 2, 512], F32))
                        pH = [actx.enter_context(nc.psum_tensor(uname(f"pH{i}"), [128, 512], F32)) for i in range(4)]
                        wf_ready = [0, 0]
                        wf_guard = [0, 0]
                        ph_free = [0, 0, 0, 0]
                        ab_guard = [0, 0]
                        asl_guard = [0, 0]
                        for k in range(128):
                            kb = k % 2
                            wbc = W0BC if k < 64 else W1BC
                            e.wait(q, "sm", wf_guard[kb])
                            wf_ready[kb] = e.inc(
                                q.dma_start(out=WF[:, kb, :, :],
                                            in_=wfcT[:, 128 * k:128 * (k + 1)]
                                            .rearrange("(j p) h -> p j h", p=128)), f"wf{kb}", 16)
                            for vt in range(2):
                                pb = 2 * kb + vt
                                sl = slice(512 * vt, 512 * (vt + 1))
                                e.wait(t, f"wf{kb}", wf_ready[kb])
                                e.wait(t, "sv", ph_free[pb])
                                for j in range(NCH):
                                    mm = t.matmul(pH[pb][:, :], WF[:, kb, j, :], XL[:, j, sl],
                                                  start=(j == 0), stop=(j == NCH - 1))
                                ph_mm = e.inc(mm, "sm")
                                if vt == 1:
                                    wf_guard[kb] = ph_mm
                                e.wait(s, "sm", ph_mm)
                                e.wait(s, "sv", ab_guard[vt])
                                sa1 = e.inc(s.activation(AB1[:, vt, :], pH[pb][:, :], AF.Abs, scale=0.25), "sa")
                                e.wait(v, "sa", sa1)
                                e.wait(v, f"as{kb}", asl_guard[kb])
                                sv1 = e.inc(v.scalar_tensor_tensor(TT[:, vt, :], pH[pb][:, :], 0.75,
                                                                   AB1[:, vt, :], ALU.mult, ALU.add), "sv")
                                ph_free[pb] = sv1
                                e.wait(s, "sv", sv1)
                                sa2 = e.inc(s.activation(TT[:, vt, :], TT[:, vt, :], AF.Square), "sa")
                                e.wait(v, "sa", sa2)
                                sv2 = e.inc(v.tensor_tensor(ASL[:, kb, sl], TT[:, vt, :], wbc[:, sl],
                                                            ALU.mult), "sv")
                                ab_guard[vt] = sv2
                            e.wait(g, "sv", sv2)
                            asl_guard[kb] = e.inc(
                                g.dma_start(out=a_buf[128 * k:128 * (k + 1), :], in_=ASL[:, kb, :]),
                                f"as{kb}", 16)
                    e.barrier()
                    with ExitStack() as bctx, nc.named_scope(uname("FFNB")):
                        WP = bctx.enter_context(nc.sbuf_tensor(uname("WP"), [128, 3, 1024], F32))
                        AT = bctx.enter_context(nc.sbuf_tensor(uname("AT"), [128, 3, 512], F32))
                        pO = [bctx.enter_context(nc.psum_tensor(uname(f"pO{i}"), [128, 512], F32)) for i in range(8)]
                        wp_ready = [0, 0, 0]
                        at_ready = [0, 0, 0]
                        buf_guard = [0, 0, 0]
                        po_free = 0
                        for d2h in range(2):
                            for th in range(2):
                                sl = slice(512 * th, 512 * (th + 1))
                                for k in range(128):
                                    kb = k % 3
                                    e.wait(q, "sm", buf_guard[kb])
                                    wp_ready[kb] = e.inc(
                                        q.dma_start(out=WP[:, kb, :],
                                                    in_=wprojT[128 * k:128 * (k + 1),
                                                               1024 * d2h:1024 * (d2h + 1)]), f"wp{kb}", 16)
                                    e.wait(g, "sm", buf_guard[kb])
                                    at_ready[kb] = e.inc(
                                        g.dma_start(out=AT[:, kb, :],
                                                    in_=a_buf[128 * k:128 * (k + 1), sl]), f"at{kb}", 16)
                                    e.wait(t, f"wp{kb}", wp_ready[kb])
                                    e.wait(t, f"at{kb}", at_ready[kb])
                                    if k == 0:
                                        e.wait(t, "sv", po_free)
                                    for c8 in range(8):
                                        mm = t.matmul(pO[c8][:, :], WP[:, kb, 128 * c8:128 * (c8 + 1)],
                                                      AT[:, kb, :], start=(k == 0), stop=(k == 127))
                                    buf_guard[kb] = e.inc(mm, "sm")
                                e.wait(v, "sm")
                                for c8 in range(8):
                                    j = 8 * d2h + c8
                                    iv = v.tensor_tensor(XACC[:, j, sl], XACC[:, j, sl], pO[c8][:, :], ALU.add)
                                    po_free = e.inc(iv, "sv")
                e.barrier()

            def emit_attn():
                with ExitStack() as nctx, nc.named_scope(uname("LN2AG")):
                    XLJ = nctx.enter_context(nc.sbuf_tensor(uname("XLJ"), [128, 2, TOK], F32))
                    dma_guard = [0, 0]
                    for j in range(NCH):
                        jb = j % 2
                        e.wait(v, f"xj{jb}", dma_guard[jb])
                        cv = norm_chunk(XLJ[:, jb, :], j)
                        e.wait(q, "sv", cv)
                        dma_guard[jb] = e.inc(
                            q.dma_start(out=ag_in.ap()[128 * j:128 * (j + 1), :], in_=XLJ[:, jb, :]), f"xj{jb}", 16)
                    e.wait(g, "xj0")
                    e.wait(g, "xj1")
                cc1 = e.inc(g.collective_compute(
                    "AllGather", ALU.bypass, replica_groups=[list(range(8))],
                    ins=[ag_in.ap().opt()], outs=[ag_out.ap().opt()]), "sc")
                e.wait(g, "sc", cc1)
                e.barrier()
                if ATTN_SUB < 2:
                    return

                with ExitStack() as sctx, nc.named_scope(uname("SCAN")):
                    def ssb(name, shape):
                        return sctx.enter_context(nc.sbuf_tensor(name, shape, F32))
                    WQb, WKb, WSb = ssb("WQb", [128, NCH, 128]), ssb("WKb", [128, NCH, 128]), ssb("WSb", [128, NCH, 128])
                    WVb, WGb = ssb("WVb", [128, NCH, 256]), ssb("WGb", [128, NCH, 256])
                    XLT = ssb("XLT", [128, 2, NCH, 128])
                    QF = ssb(uname("QF"), [64, 2, 2, 128])
                    KT = ssb(uname("KT"), [128, 2, 128])
                    VT = ssb(uname("VT"), [128, 2, 256])
                    GT = ssb(uname("GT"), [128, 2, 256])
                    PT = ssb(uname("PT"), [128, 2, 128])
                    KF = ssb(uname("KF"), [64, 2, 128])
                    PF = ssb(uname("PF"), [64, 2, 128])
                    SM, W2M = ssb(uname("SM"), [128, 256]), ssb(uname("W2M"), [128, 256])
                    OKN, EXPT = ssb(uname("OKN"), [128, 128]), ssb(uname("EXPT"), [128, 128])
                    PV2 = ssb(uname("PV2"), [128, 128])
                    PV2F = ssb(uname("PV2F"), [64, 2, 128])
                    CD, RD = ssb(uname("CD"), [128, 128]), ssb(uname("RD"), [128, 128])
                    ON = ssb(uname("ON"), [128, 2, 256])
                    OF = ssb(uname("OF"), [128, 256])
                    SQG = ssb(uname("SQG"), [128, 256])
                    SG = ssb(uname("SG"), [128, 2, 256])
                    NM, RSm = ssb(uname("NM"), [128, 2]), ssb(uname("RSm"), [128, 2])
                    GSm, MSm = ssb(uname("GSm"), [128, 2]), ssb(uname("MSm"), [128, 2])
                    RN = ssb(uname("RN"), [128, 2])
                    RSW, RSR = ssb(uname("RSW"), [128, 64]), ssb(uname("RSR"), [128, 64])
                    HST = ssb(uname("HST"), [64, 2, 64])
                    HVST = ssb(uname("HVST"), [64, 2, 128])
                    D0R = ssb(uname("D0R"), [1, 128])
                    pAp = [sctx.enter_context(nc.psum_tensor(uname(f"pA{i}"), [128, 512], F32)) for i in range(2)]
                    pBp = [sctx.enter_context(nc.psum_tensor(uname(f"pB{i}"), [128, 512], F32)) for i in range(2)]
                    pC = sctx.enter_context(nc.psum_tensor(uname("pC"), [128, 512], F32))
                    pD = sctx.enter_context(nc.psum_tensor(uname("pD"), [128, 512], F32))
                    pE = sctx.enter_context(nc.psum_tensor(uname("pE"), [128, 512], F32))
                    pF = sctx.enter_context(nc.psum_tensor(uname("pF"), [128, 512], F32))
                    for (buf, wsrc) in ((WQb, wq), (WKb, wk), (WSb, ws), (WVb, wv), (WGb, wg)):
                        e.inc(q.dma_start(out=buf[:, :, :], in_=wsrc.rearrange("(j p) c -> p j c", p=128)), "sd")
                    e.wait(v, "sd")
                    e.inc(v.memset(RSW[:, :], 1.0), "sv")
                    xlt_ready = [0, 0]
                    proj_mm = [0, 0]
                    ev_b = [0, 0]
                    pt_c = [0, 0]
                    bank_free = [0, 0]
                    tile_free = [0, 0]
                    of_guard = 0
                    st_upd_prev = 0
                    d0_ev_prev = 0
                    sm_ev_prev = 0
                    w2_ev_prev = 0
                    of_c_prev = 0

                    def emit_xlt(tcx):
                        cbx = tcx % 2
                        blkx, subx = tcx // 8, tcx % 8
                        e.wait(q, "sm", proj_mm[cbx])
                        xlt_ready[cbx] = e.inc(
                            q.dma_start(out=XLT[:, cbx, :, :],
                                        in_=ag_out.ap()[2048 * blkx:2048 * (blkx + 1),
                                                        128 * subx:128 * (subx + 1)]
                                        .rearrange("(j p) t -> p j t", p=128)), f"xt{cbx}", 16)

                    def proj_gen(tcx):
                        par = tcx % 2
                        pA, pB = pAp[par], pBp[par]
                        e.wait(t, f"xt{par}", xlt_ready[par])
                        e.wait(t, "sv", bank_free[par])
                        for j in range(NCH):
                            st0 = (j == 0)
                            lst = (j == NCH - 1)
                            t.matmul(pA[:, 256:384], XLT[:, par, j, :], WKb[:, j, :], start=st0, stop=lst)
                            t.matmul(pA[:, 384:512], XLT[:, par, j, :], WSb[:, j, :], start=False, stop=lst)
                            t.matmul(pA[0:64, 0:128], WQb[:, j, 0:64], XLT[:, par, j, :], start=False, stop=lst)
                            t.matmul(pA[0:64, 128:256], WQb[:, j, 64:128], XLT[:, par, j, :], start=False, stop=lst)
                            t.matmul(pB[:, 0:256], XLT[:, par, j, :], WVb[:, j, :], start=st0, stop=lst)
                            mm = t.matmul(pB[:, 256:512], XLT[:, par, j, :], WGb[:, j, :], start=False, stop=lst)
                            if lst:
                                proj_mm[par] = e.inc(mm, "sm")
                            yield

                    def emit_evacs(tcx):
                        par = tcx % 2
                        pA, pB = pAp[par], pBp[par]
                        e.wait(v, "sm", proj_mm[par])
                        e.wait(v, "sm", tile_free[par])
                        e.inc(v.tensor_scalar(QF[:, par, 0, :], pA[0:64, 0:128], 0.125, None, ALU.mult), "sv")
                        e.inc(v.tensor_scalar(QF[:, par, 1, :], pA[0:64, 128:256], 0.125, None, ALU.mult), "sv")
                        e.inc(v.tensor_copy(KT[:, par, :], pA[:, 256:384]), "sv")
                        sv_cl = e.inc(v.tensor_scalar(PT[:, par, :], pA[:, 384:512], -32.0, 32.0,
                                                      ALU.max, ALU.min), "sv")
                        e.inc(v.tensor_copy(VT[:, par, :], pB[:, 0:256]), "sv")
                        ev_b[par] = e.inc(v.tensor_copy(GT[:, par, :], pB[:, 256:512]), "sv")
                        e.wait(s, "sv", ev_b[par])
                        e.inc(s.activation(SG[:, par, :], GT[:, par, :], AF.Silu), "sa")
                        pt_c[par] = e.inc(s.activation(PT[:, par, :], PT[:, par, :], AF.Exp,
                                                       bias=CONSTB[:, 0:1]), "sa")

                    def emit_ofinal(tc):
                        nonlocal of_guard, of_c_prev
                        if tc < 0:
                            return
                        par = tc % 2
                        pA = pAp[par]
                        blk, sub = tc // 8, tc % 8
                        e.wait(t, "sv", ov_free_arr[par])
                        t.matmul(pF[:, 256:384], ON[:, par, 0:128], IDENT[:, :], start=False, stop=True,
                                 is_transpose=True)
                        otr = e.inc(t.matmul(pF[:, 384:512], ON[:, par, 128:256], IDENT[:, :], start=False,
                                             stop=True, is_transpose=True), "sm")
                        e.wait(v, "sm", otr)
                        e.wait(v, "sg", of_guard)
                        of_c = e.inc(v.tensor_copy(OF[:, :], pF[:, 256:512]), "sv")
                        of_c_prev = of_c
                        e.wait(g, "sv", of_c)
                        for h in range(2):
                            of_guard = e.inc(g.dma_start(
                                out=a2a_in.ap()[256 * blk + 128 * h:256 * blk + 128 * (h + 1),
                                                128 * sub:128 * (sub + 1)],
                                in_=OF[:, 128 * h:128 * (h + 1)]), "sg", 16)

                    def emit_tail(tc, projit):
                        nonlocal of_guard, st_upd_prev, d0_ev_prev, sm_ev_prev, w2_ev_prev, of_c_prev
                        par = tc % 2
                        pA = pAp[par]
                        blk, sub = tc // 8, tc % 8

                        def pj(n=1):
                            for _ in range(n):
                                try:
                                    next(projit)
                                except StopIteration:
                                    return
                        if tc % 32 == 0:
                            e.wait(v, "sm", st_guard_mm[0])
                            e.inc(v.memset(HST[:, :, :], 0.0), "sv")
                            e.inc(v.memset(HVST[:, :, :], 0.0), "sv")
                            sv_c = e.inc(v.memset(D0R[:, :], 0.0), "sv")
                            e.wait(t, "sv", sv_c)
                        pj(2)
                        e.wait(t, "sv", st_upd_prev)
                        e.wait(t, "sa", d0_ev_prev)
                        e.wait(t, "sv", ev_b[par])
                        e.wait(t, "sa", pt_c[par])
                        t.matmul(pA[0:64, 0:128], KT[:, par, 0:64], IDENT[:, :], start=True, stop=True,
                                 is_transpose=True)
                        t.matmul(pA[0:64, 128:256], KT[:, par, 64:128], IDENT[:, :], start=False, stop=True,
                                 is_transpose=True)
                        t.matmul(pA[0:64, 256:384], PT[:, par, 0:64], IDENT[:, :], start=False, stop=True,
                                 is_transpose=True)
                        tr_mm = e.inc(t.matmul(pA[0:64, 384:512], PT[:, par, 64:128], IDENT[:, :],
                                               start=False, stop=True, is_transpose=True), "sm")
                        e.wait(v, "sm", tr_mm)
                        e.inc(v.tensor_copy(KF[:, 0, :], pA[0:64, 0:128]), "sv")
                        e.inc(v.tensor_copy(KF[:, 1, :], pA[0:64, 128:256]), "sv")
                        e.inc(v.tensor_copy(PF[:, 0, :], pA[0:64, 256:384]), "sv")
                        tr_ev = e.inc(v.tensor_copy(PF[:, 1, :], pA[0:64, 384:512]), "sv")
                        emit_ofinal(tc - 1)
                        pj(2)
                        e.wait(t, "sv", sm_ev_prev)
                        e.wait(t, "sv", tr_ev)
                        for h in range(2):
                            hs = slice(64 * h, 64 * (h + 1))
                            t.matmul(pC[:, hs], TRIU[:, :], PT[:, par, hs], start=(h == 0), stop=False)
                            t.matmul(pC[:, hs], ONESR[0:1, 0:128], D0R[0:1, hs], start=False, stop=True)
                        for h in range(2):
                            hs = slice(64 * h, 64 * (h + 1))
                            d0s = slice(128 + 64 * h, 128 + 64 * (h + 1))
                            t.matmul(pC[0:1, d0s], ONESC[:, 0:1], PT[:, par, hs], start=False, stop=False)
                            mm = t.matmul(pC[0:1, d0s], ONESR[0:1, 0:1], D0R[0:1, hs], start=False, stop=True)
                        cumd_mm = e.inc(mm, "sm")
                        e.wait(v, "sm", cumd_mm)
                        e.inc(v.tensor_copy(CD[:, :], pC[:, 0:128]), "sv")
                        rd_ev = e.inc(v.reciprocal(RD[:, :], CD[:, :]), "sv")
                        e.wait(s, "sm", cumd_mm)
                        e.wait(s, "sv", rd_ev)
                        d0_ev = e.inc(s.copy(D0R[0:1, :], pC[0:1, 128:256]), "sa")
                        pj(3)
                        e.wait(t, "sv", rd_ev)
                        e.wait(t, "sa", d0_ev)
                        for h in range(2):
                            mm = t.matmul(pC[:, 256 + 128 * h:256 + 128 * (h + 1)],
                                          KF[:, h, :], QF[:, par, h, :], start=False, stop=True)
                        st_mm = e.inc(mm, "sm")
                        e.wait(v, "sm", st_mm)
                        sm_ev = e.inc(v.tensor_tensor(SM[:, :], pC[:, 256:512], TRIU2[:, :], ALU.mult), "sv")
                        sm_ev_prev = sm_ev
                        pj(2)
                        e.wait(t, "sv", w2_ev_prev)
                        e.wait(t, "sv", sm_ev)
                        for h in range(2):
                            hs = slice(64 * h, 64 * (h + 1))
                            t.matmul(pD[:, hs], QF[:, par, h, :], HST[:, h, :], start=(h == 0), stop=False)
                            mm = t.matmul(pD[:, hs], SM[:, 128 * h:128 * (h + 1)], PT[:, par, hs],
                                          start=False, stop=True)
                        ok_mm = e.inc(mm, "sm")
                        for h in range(2):
                            hs = slice(64 * h, 64 * (h + 1))
                            mm = t.matmul(pE[0:64, hs], KT[:, par, hs], PT[:, par, hs],
                                          start=(h == 0), stop=True)
                        h_mm = e.inc(mm, "sm")
                        e.wait(v, "sm", ok_mm)
                        e.inc(v.tensor_tensor(OKN[:, :], pD[:, 0:128], RD[:, :], ALU.mult), "sv")
                        nm_c = e.inc(v.reduce_max(NM[:, :], OKN.rearrange("p (g x) -> p g x", g=2),
                                                  axis=AX, negate=True), "sv")
                        e.wait(s, "sv", nm_c)
                        sa_c = 0
                        for h in range(2):
                            hs = slice(64 * h, 64 * (h + 1))
                            sa_c = e.inc(s.activation(EXPT[:, hs], OKN[:, hs], AF.Exp,
                                                      bias=NM[:, h:h + 1]), "sa")
                        e.wait(v, "sa", sa_c)
                        rs_c = e.inc(v.reduce_sum(RSm[:, :], EXPT.rearrange("p (g x) -> p g x", g=2),
                                                  axis=AX), "sv")
                        e.wait(s, "sv", rs_c)
                        sa_c = e.inc(s.copy(RSW[:, 0:2], RSm[:, :]), "sa")
                        e.wait(v, "sa", sa_c)
                        e.inc(v.reciprocal(RSR[:, :], RSW[:, :]), "sv")
                        e.inc(v.tensor_tensor(PV2.rearrange("p (g x) -> p g x", g=2),
                                              EXPT.rearrange("p (g x) -> p g x", g=2),
                                              RSR[:, 0:2].rearrange("p (g x) -> p g x", x=1)
                                              .broadcast_to((128, 2, 64)), ALU.mult), "sv")
                        pv2_c = e.inc(v.tensor_tensor(PV2[:, :], PV2[:, :], RD[:, :], ALU.mult), "sv")
                        pj(4)
                        e.wait(t, "sv", pv2_c)
                        t.matmul(pA[0:64, 0:128], PV2[:, 0:64], IDENT[:, :], start=True, stop=True,
                                 is_transpose=True)
                        tr2 = e.inc(t.matmul(pA[0:64, 128:256], PV2[:, 64:128], IDENT[:, :],
                                             start=False, stop=True, is_transpose=True), "sm")
                        e.wait(v, "sm", tr2)
                        e.inc(v.tensor_copy(PV2F[:, 0, :], pA[0:64, 0:128]), "sv")
                        pv2f_c = e.inc(v.tensor_copy(PV2F[:, 1, :], pA[0:64, 128:256]), "sv")
                        bank_free[par] = pv2f_c
                        pj(2)
                        e.wait(t, "sv", pv2f_c)
                        for h in range(2):
                            mm = t.matmul(pD[:, 128 + 128 * h:128 + 128 * (h + 1)],
                                          PF[:, h, :], PV2F[:, h, :], start=False, stop=True)
                        w2_mm = e.inc(mm, "sm")
                        e.wait(v, "sm", w2_mm)
                        w2_ev = e.inc(v.tensor_tensor(W2M[:, :], pD[:, 128:384], TRIU2[:, :], ALU.mult), "sv")
                        w2_ev_prev = w2_ev
                        pj(1)
                        e.wait(t, "sv", of_c_prev)
                        e.wait(t, "sv", w2_ev)
                        for h in range(2):
                            vs = slice(128 * h, 128 * (h + 1))
                            t.matmul(pF[:, vs], PV2F[:, h, :], HVST[:, h, :], start=(h == 0), stop=False)
                            mm = t.matmul(pF[:, vs], W2M[:, vs], VT[:, par, vs], start=False, stop=True)
                        ov_mm = e.inc(mm, "sm")
                        for h in range(2):
                            vs = slice(128 + 128 * h, 128 + 128 * (h + 1))
                            mm = t.matmul(pE[0:64, vs], PT[:, par, 64 * h:64 * (h + 1)],
                                          VT[:, par, 128 * h:128 * (h + 1)], start=False, stop=True)
                        hv_mm = e.inc(mm, "sm")
                        tile_free[par] = hv_mm
                        st_guard_mm[0] = hv_mm
                        e.wait(v, "sm", hv_mm)
                        e.inc(v.tensor_tensor(HST[:, :, :], HST[:, :, :], pE[0:64, 0:128], ALU.add), "sv")
                        st_upd = e.inc(v.tensor_tensor(HVST[:, :, :], HVST[:, :, :], pE[0:64, 128:384],
                                                       ALU.add), "sv")
                        st_upd_prev = st_upd
                        e.wait(s, "sm", ov_mm)
                        sq_c = e.inc(s.activation(SQG[:, :], pF[:, 0:256], AF.Square), "sa")
                        e.wait(v, "sa", sq_c)
                        gs_c = e.inc(v.reduce_sum(GSm[:, :], SQG.rearrange("p (g x) -> p g x", g=2),
                                                  axis=AX), "sv")
                        e.wait(s, "sv", gs_c)
                        e.inc(s.activation(MSm[:, :], GSm[:, :], AF.Copy, scale=1.0 / DV, bias=EPS), "sa")
                        rw_c = e.inc(s.activation(RSW[:, 0:2], MSm[:, :], AF.Sqrt), "sa")
                        e.wait(v, "sa", rw_c)
                        e.inc(v.reciprocal(RSR[:, :], RSW[:, :]), "sv")
                        e.inc(v.tensor_tensor(ON[:, par, :].rearrange("p (g x) -> p g x", g=2),
                                              pF[:, 0:256].rearrange("p (g x) -> p g x", g=2),
                                              RSR[:, 0:2].rearrange("p (g x) -> p g x", x=1)
                                              .broadcast_to((128, 2, 128)), ALU.mult), "sv")
                        e.inc(v.tensor_tensor(ON[:, par, :], ON[:, par, :], GNB[:, :], ALU.mult), "sv")
                        ov_free_arr[par] = e.inc(v.tensor_tensor(ON[:, par, :], ON[:, par, :], SG[:, par, :], ALU.mult), "sv")
                        pj(8)

                    st_guard_mm = [0]
                    ov_free_arr = [0, 0]
                    emit_xlt(0)
                    g0 = proj_gen(0)
                    for _ in g0:
                        pass
                    emit_evacs(0)
                    for tc in range(NTC):
                        if tc + 1 < NTC:
                            emit_xlt(tc + 1)
                            projit = proj_gen(tc + 1)
                        else:
                            projit = iter(())
                        emit_tail(tc, projit)
                        if tc + 1 < NTC:
                            emit_evacs(tc + 1)
                    emit_ofinal(NTC - 1)
                    e.wait(g, "sg")
                e.barrier()
                if SCAN_LVL < 8:
                    return
                with allow_ici_all_to_all():
                    cc2 = e.inc(g.collective_compute(
                        "AllToAll", ALU.bypass, replica_groups=[list(range(8))],
                        ins=[a2a_in.ap().opt()], outs=[a2a_out.ap().opt()]), "sc")
                e.wait(g, "sc", cc2)
                e.barrier()
                if ATTN_SUB < 3:
                    return
                with ExitStack() as wctx, nc.named_scope(uname("WO")):
                    OFM = wctx.enter_context(nc.sbuf_tensor(uname("OFM"), [128, NCH, TOK], F32))
                    WOB = wctx.enter_context(nc.sbuf_tensor(uname("WOB"), [128, 2, NCH, 128], F32))
                    pW = [wctx.enter_context(nc.psum_tensor(uname(f"pW{i}"), [128, 512], F32)) for i in range(4)]
                    cd = e.inc(q.dma_start(out=OFM[:, :, :],
                                           in_=a2a_out.ap().rearrange("(j p) t -> p j t", p=128)), "ofm", 16)
                    wo_ready = [0, 0]
                    wo_guard = [0, 0]
                    pw_free = [0, 0, 0, 0]
                    for d2 in range(NCH):
                        db = d2 % 2
                        e.wait(q, "sm", wo_guard[db])
                        wo_ready[db] = e.inc(
                            q.dma_start(out=WOB[:, db, :, :],
                                        in_=wo[:, 128 * d2:128 * (d2 + 1)]
                                        .rearrange("(j p) c -> p j c", p=128)), f"wo{db}", 16)
                        e.wait(t, f"wo{db}", wo_ready[db])
                        e.wait(t, "ofm", cd)
                        for th in range(2):
                            pb = 2 * db + th
                            sl = slice(512 * th, 512 * (th + 1))
                            e.wait(t, "sv", pw_free[pb])
                            for jv in range(NCH):
                                mm = t.matmul(pW[pb][:, :], WOB[:, db, jv, :], OFM[:, jv, sl],
                                              start=(jv == 0), stop=(jv == NCH - 1))
                            cm = e.inc(mm, "sm")
                            if th == 1:
                                wo_guard[db] = cm
                            e.wait(v, "sm", cm)
                            iv = v.tensor_tensor(XACC[:, d2, sl], XACC[:, d2, sl], pW[pb][:, :], ALU.add)
                            pw_free[pb] = e.inc(iv, "sv")
                e.barrier()

            if STAGE_FFN1:
                emit_ln_stage()
                emit_ffn()
            if STAGE_ATTN:
                emit_ln_stage()
                emit_attn()
            if STAGE_FFN2:
                emit_ln_stage()
                emit_ffn()

            e.barrier()
            cst = e.inc(q.dma_start(out=y_out.rearrange("(j p) t -> p j t", p=128),
                                    in_=XACC[:, :, :]), "sd")
            if DEBUG_DUMP:
                e.inc(q.dma_start(out=dbg_ai[:, :], in_=a2a_in.ap()[:, :]), "sd")
                e.inc(q.dma_start(out=dbg_ao[:, :], in_=a2a_out.ap()[:, :]), "sd")
            e.wait(q, "sd")

    return nc


_NC_CACHE = {}


def _get_nc():
    key = (STAGE_FFN1, STAGE_ATTN, STAGE_FFN2, ATTN_SUB, SCAN_LVL, DEBUG_DUMP, DEBUG_SCAN)
    if key not in _NC_CACHE:
        _NC_CACHE[key] = build_program()
    return _NC_CACHE[key]


def kernel(L0_attn_hidden_states, L0_attn_Wq, L0_attn_Wk, L0_attn_Wv, L0_attn_Wg,
           L0_attn_Ws, L0_attn_Wo, L0_attn_g_norm_weight, L0_ffn_w_router,
           L0_ffn_w_fc, L0_ffn_w_proj):
    x = np.asarray(L0_attn_hidden_states, np.float32).reshape(8192, 2048)
    wfcT_h = np.ascontiguousarray(
        np.transpose(np.asarray(L0_ffn_w_fc, np.float32), (2, 0, 1)).reshape(2048, 16384))
    wprojT_h = np.ascontiguousarray(
        np.transpose(np.asarray(L0_ffn_w_proj, np.float32), (0, 2, 1)).reshape(16384, 2048))
    wrT_h = np.ascontiguousarray(np.asarray(L0_ffn_w_router, np.float32).T)
    Wq = np.asarray(L0_attn_Wq, np.float32)
    Wk = np.asarray(L0_attn_Wk, np.float32)
    Ws = np.asarray(L0_attn_Ws, np.float32)
    Wv = np.asarray(L0_attn_Wv, np.float32)
    Wg = np.asarray(L0_attn_Wg, np.float32)
    Wo_h = np.ascontiguousarray(np.asarray(L0_attn_Wo, np.float32))
    gn = np.asarray(L0_attn_g_norm_weight, np.float32)
    gnb_h = np.ascontiguousarray(np.broadcast_to(np.tile(gn, 2)[None, :], (128, 256)))
    triu_h = np.triu(np.ones((128, 128), np.float32))
    ident_h = np.eye(128, dtype=np.float32)

    in_maps = []
    for c in range(8):
        in_maps.append({
            "x_in": np.ascontiguousarray(x[1024 * c:1024 * (c + 1), :].T),
            "wfcT": wfcT_h, "wprojT": wprojT_h, "wrT": wrT_h,
            "wq": np.ascontiguousarray(Wq[:, 128 * c:128 * (c + 1)]),
            "wk": np.ascontiguousarray(Wk[:, 128 * c:128 * (c + 1)]),
            "ws": np.ascontiguousarray(Ws[:, 128 * c:128 * (c + 1)]),
            "wv": np.ascontiguousarray(Wv[:, 256 * c:256 * (c + 1)]),
            "wg": np.ascontiguousarray(Wg[:, 256 * c:256 * (c + 1)]),
            "wo": Wo_h, "gnb": gnb_h, "triu_in": triu_h, "triu2_in": np.ascontiguousarray(np.tile(triu_h, (1, 2))), "ident_in": ident_h,
        })

    nc = _get_nc()
    res = run_bass_kernel_spmd(nc, in_maps, core_ids=list(range(8)))
    out = np.empty((8192, 2048), np.float32)
    for c in range(8):
        out[1024 * c:1024 * (c + 1), :] = res.results[c]["y"].T
    return out.reshape(2, 4096, 2048)


# version 10
# speedup vs baseline: 1.0435x; 1.0030x over previous
import numpy as np
from contextlib import ExitStack

import concourse.bass as bass
import concourse.mybir as mybir
from concourse.bass_utils import run_bass_kernel_spmd
from concourse.aot_env import allow_ici_all_to_all

F32 = mybir.dt.float32
AF = mybir.ActivationFunctionType
ALU = mybir.AluOpType
AX = mybir.AxisListType.X

D = 2048
NCH = 16
TOK = 1024
HID2 = 16384
DK, DV, M = 64, 128, 64
NTC = 64
EPS = 1e-5

STAGE_FFN1 = True
STAGE_ATTN = True
STAGE_FFN2 = True
ATTN_SUB = 3
SCAN_LVL = 8
DEBUG_DUMP = False
DEBUG_SCAN = False


class Emit:
    def __init__(self, nc, ctx):
        self.nc = nc
        self.t, self.v, self.s, self.q, self.g = nc.tensor, nc.vector, nc.scalar, nc.sync, nc.gpsimd
        self.sem, self.cnt = {}, {}
        for name in ["sd", "sg", "sm", "sv", "sa", "sc"]:
            self.sem[name] = ctx.enter_context(nc.semaphore(name))
            self.cnt[name] = 0
        for name in ["wf0", "wf1", "wp0", "wp1", "wp2", "at0", "at1", "at2",
                     "xt0", "xt1", "wo0", "wo1", "ofm", "xj0", "xj1", "as0", "as1"]:
            self.sem[name] = ctx.enter_context(nc.semaphore(name))
            self.cnt[name] = 0

    def inc(self, instr, name, amt=1):
        if name in ("sd", "sg") and amt == 1:
            amt = 16
        self.cnt[name] += amt
        instr.then_inc(self.sem[name], amt)
        return self.cnt[name]

    def wait(self, eng, name, value=None):
        v = self.cnt[name] if value is None else value
        if v > 0:
            eng.wait_ge(self.sem[name], v)

    def barrier(self):
        engs = {"sd": self.q, "sg": self.g, "sm": self.t, "sv": self.v, "sa": self.s}
        allsems = list(self.sem.keys())
        for ename, eng in engs.items():
            for sname in allsems:
                if sname != ename:
                    self.wait(eng, sname)


def build_program():
    nc = bass.Bass("TRN2", target_bir_lowering=False)

    x_in = nc.dram_tensor("x_in", [D, TOK], F32, kind="ExternalInput").ap()
    wfcT = nc.dram_tensor("wfcT", [D, HID2], F32, kind="ExternalInput").ap()
    wprojT = nc.dram_tensor("wprojT", [HID2, D], F32, kind="ExternalInput").ap()
    wrT = nc.dram_tensor("wrT", [D, 2], F32, kind="ExternalInput").ap()
    wq = nc.dram_tensor("wq", [D, 128], F32, kind="ExternalInput").ap()
    wk = nc.dram_tensor("wk", [D, 128], F32, kind="ExternalInput").ap()
    ws = nc.dram_tensor("ws", [D, 128], F32, kind="ExternalInput").ap()
    wv = nc.dram_tensor("wv", [D, 256], F32, kind="ExternalInput").ap()
    wg = nc.dram_tensor("wg", [D, 256], F32, kind="ExternalInput").ap()
    wo = nc.dram_tensor("wo", [D, D], F32, kind="ExternalInput").ap()
    gnb = nc.dram_tensor("gnb", [128, 256], F32, kind="ExternalInput").ap()
    triu_in = nc.dram_tensor("triu_in", [128, 128], F32, kind="ExternalInput").ap()
    triu2_in = nc.dram_tensor("triu2_in", [128, 256], F32, kind="ExternalInput").ap()
    ident_in = nc.dram_tensor("ident_in", [128, 128], F32, kind="ExternalInput").ap()

    y_out = nc.dram_tensor("y", [D, TOK], F32, kind="ExternalOutput").ap()
    if DEBUG_DUMP:
        dbg_ai = nc.dram_tensor("dbg_ai", [D, TOK], F32, kind="ExternalOutput").ap()
        dbg_ao = nc.dram_tensor("dbg_ao", [D, TOK], F32, kind="ExternalOutput").ap()
    if DEBUG_SCAN:
        dbg_sc = nc.dram_tensor("dbg_sc", [2, 22, 128, 256], F32, kind="ExternalOutput").ap()

    a_buf = nc.dram_tensor("a_buf", [HID2, TOK], F32).ap()
    ag_in = nc.dram_tensor("ag_in", [D, TOK], F32)
    ag_out = nc.dram_tensor("ag_out", [8 * D, TOK], F32)
    a2a_in = nc.dram_tensor("a2a_in", [D, TOK], F32)
    a2a_out = nc.dram_tensor("a2a_out", [D, TOK], F32)

    with ExitStack() as ctx:
        e = Emit(nc, ctx)
        t, v, s, q, g = e.t, e.v, e.s, e.q, e.g
        _uid = [0]

        def uname(base):
            _uid[0] += 1
            return f"{base}_{_uid[0]}"

        def sb(name, shape):
            return ctx.enter_context(nc.sbuf_tensor(name, shape, F32))

        XACC = sb("XACC", [128, NCH, TOK])
        TRIU = sb("TRIU", [128, 128])
        TRIU2 = sb("TRIU2", [128, 256])
        IDENT = sb("IDENT", [128, 128])
        ONESC = sb("ONESC", [128, 1])
        ONESR = sb("ONESR", [1, 512])
        GNB = sb("GNB", [128, 256])
        MU_BC = sb("MU_BC", [128, TOK])
        R_BC = sb("R_BC", [128, TOK])
        CONSTB = sb("CONSTB", [128, 4])
        e.sq_guard = [0, 0]

        block = ctx.enter_context(nc.Block())

        @block.sync
        def _(_sync):
            e.inc(q.dma_start(out=XACC[:, :, :], in_=x_in.rearrange("(j p) t -> p j t", p=128)), "sd")
            e.inc(q.dma_start(out=TRIU[:, :], in_=triu_in[:, :]), "sd")
            e.inc(q.dma_start(out=TRIU2[:, :], in_=triu2_in[:, :]), "sd")
            e.inc(q.dma_start(out=IDENT[:, :], in_=ident_in[:, :]), "sd")
            e.inc(q.dma_start(out=GNB[:, :], in_=gnb[:, :]), "sd")
            e.wait(v, "sd")
            e.inc(v.memset(ONESC[:, :], 1.0), "sv")
            e.inc(v.memset(ONESR[:, :], 1.0), "sv")
            e.inc(g.memset(CONSTB[:, 0:1], -32.0), "sg", 1)
            e.inc(g.memset(CONSTB[:, 1:2], EPS), "sg", 1)
            e.inc(g.memset(CONSTB[:, 2:3], 1.5), "sg", 1)
            e.barrier()

            def emit_ln_stage():
                with ExitStack() as lctx, nc.named_scope(uname("LN")):
                    pR0 = lctx.enter_context(nc.psum_tensor(uname("pR0"), [1, 512], F32))
                    pR1 = lctx.enter_context(nc.psum_tensor(uname("pR1"), [1, 512], F32))
                    pBC = lctx.enter_context(nc.psum_tensor(uname("pBC"), [128, 512], F32))
                    XSQ = lctx.enter_context(nc.sbuf_tensor(uname("XSQ"), [128, 2, 512], F32))
                    MU = lctx.enter_context(nc.sbuf_tensor(uname("MU"), [1, TOK], F32))
                    RR = lctx.enter_context(nc.sbuf_tensor(uname("RR"), [1, TOK], F32))
                    T0 = lctx.enter_context(nc.sbuf_tensor(uname("T0"), [1, TOK], F32))
                    T1 = lctx.enter_context(nc.sbuf_tensor(uname("T1"), [1, TOK], F32))
                    T2 = lctx.enter_context(nc.sbuf_tensor(uname("T2"), [1, TOK], F32))
                    for vt in range(2):
                        sl = slice(512 * vt, 512 * (vt + 1))
                        for j in range(NCH):
                            if j >= 2:
                                e.wait(s, "sm", e.sq_guard[j % 2])
                            sa_c = e.inc(s.activation(XSQ[:, j % 2, :], XACC[:, j, sl], AF.Square), "sa")
                            t.matmul(pR0[:, :], ONESC[:, 0:1], XACC[:, j, sl],
                                     start=(j == 0), stop=(j == NCH - 1))
                            e.wait(t, "sa", sa_c)
                            mm2 = t.matmul(pR1[:, :], ONESC[:, 0:1], XSQ[:, j % 2, :],
                                           start=(j == 0), stop=(j == NCH - 1))
                            e.sq_guard[j % 2] = e.inc(mm2, "sm")
                        e.wait(s, "sm")
                        s.activation(MU[:, sl], pR0[:, :], AF.Copy, scale=1.0 / D)
                        sa_c = e.inc(s.activation(T0[:, sl], pR1[:, :], AF.Copy, scale=1.0 / D), "sa")
                        e.wait(t, "sa", sa_c)
                    e.wait(v, "sa")
                    e.inc(v.tensor_tensor(T1[:, :], MU[:, :], MU[:, :], ALU.mult), "sv")
                    sv_c = e.inc(v.tensor_tensor(T0[:, :], T0[:, :], T1[:, :], ALU.subtract), "sv")
                    e.wait(s, "sv", sv_c)
                    e.inc(s.activation(T2[:, :], T0[:, :], AF.Copy, bias=EPS), "sa")
                    sa_c = e.inc(s.activation(T1[:, :], T2[:, :], AF.Sqrt), "sa")
                    e.wait(v, "sa", sa_c)
                    e.inc(v.reciprocal(T0[:, :], T1[:, :]), "sv")
                    e.inc(v.tensor_tensor(T1[:, :], T0[:, :], T0[:, :], ALU.mult), "sv")
                    sv_c = e.inc(v.tensor_tensor(T1[:, :], T1[:, :], T2[:, :], ALU.mult), "sv")
                    e.wait(s, "sv", sv_c)
                    sa_c = e.inc(s.activation(T1[:, :], T1[:, :], AF.Copy, scale=-0.5, bias=1.5), "sa")
                    e.wait(v, "sa", sa_c)
                    e.inc(v.tensor_tensor(RR[:, :], T0[:, :], T1[:, :], ALU.mult), "sv")
                    e.wait(t, "sv")
                    for (row, dstbc) in ((MU, MU_BC), (RR, R_BC)):
                        for vt in range(2):
                            sl = slice(512 * vt, 512 * (vt + 1))
                            c1 = e.inc(t.matmul(pBC[:, :], ONESR[0:1, 0:128], row[:, sl],
                                                start=True, stop=True), "sm")
                            e.wait(v, "sm", c1)
                            cv = e.inc(v.tensor_copy(dstbc[:, sl], pBC[:, :]), "sv")
                            e.wait(t, "sv", cv)

            def norm_chunk(dst_ap, j):
                e.inc(v.tensor_tensor(dst_ap, XACC[:, j, :], MU_BC[:, :], ALU.subtract), "sv")
                return e.inc(v.tensor_tensor(dst_ap, dst_ap, R_BC[:, :], ALU.mult), "sv")

            def emit_ffn():
                with ExitStack() as fctx:
                    XL = fctx.enter_context(nc.sbuf_tensor(uname("XL"), [128, NCH, TOK], F32))
                    W0BC = fctx.enter_context(nc.sbuf_tensor(uname("W0BC"), [128, TOK], F32))
                    W1BC = fctx.enter_context(nc.sbuf_tensor(uname("W1BC"), [128, TOK], F32))
                    for j in range(NCH):
                        norm_chunk(XL[:, j, :], j)
                    with ExitStack() as rctx, nc.named_scope(uname("ROUTER")):
                        pDLr = rctx.enter_context(nc.psum_tensor(uname("pDLr"), [1, 512], F32))
                        pBC2 = rctx.enter_context(nc.psum_tensor(uname("pBC2"), [128, 512], F32))
                        WRT = rctx.enter_context(nc.sbuf_tensor(uname("WRT"), [128, NCH, 2], F32))
                        WRD = rctx.enter_context(nc.sbuf_tensor(uname("WRD"), [128, NCH], F32))
                        SR0 = rctx.enter_context(nc.sbuf_tensor(uname("SR0"), [1, TOK], F32))
                        SR1 = rctx.enter_context(nc.sbuf_tensor(uname("SR1"), [1, TOK], F32))
                        MR = rctx.enter_context(nc.sbuf_tensor(uname("MR"), [1, TOK], F32))
                        DL = rctx.enter_context(nc.sbuf_tensor(uname("DL"), [1, TOK], F32))
                        W0R = rctx.enter_context(nc.sbuf_tensor(uname("W0R"), [1, TOK], F32))
                        W1R = rctx.enter_context(nc.sbuf_tensor(uname("W1R"), [1, TOK], F32))
                        cd = e.inc(q.dma_start(out=WRT[:, :, :],
                                               in_=wrT.rearrange("(j p) c -> p j c", p=128)), "sd")
                        e.wait(v, "sd", cd)
                        sv_c = e.inc(v.tensor_tensor(WRD[:, :], WRT[:, :, 1], WRT[:, :, 0], ALU.subtract), "sv")
                        e.wait(t, "sv")
                        for vt in range(2):
                            sl = slice(512 * vt, 512 * (vt + 1))
                            for j in range(NCH):
                                mm = t.matmul(pDLr[:, :], WRD[:, j:j + 1], XL[:, j, sl],
                                              start=(j == 0), stop=(j == NCH - 1))
                            cm = e.inc(mm, "sm")
                            e.wait(s, "sm", cm)
                            sa_c = e.inc(s.copy(DL[:, sl], pDLr[:, :]), "sa")
                            e.wait(t, "sa", sa_c)
                        e.wait(s, "sa")
                        e.inc(s.activation(SR1[:, :], DL[:, :], AF.Sigmoid), "sa")
                        sa_c = e.inc(s.activation(SR0[:, :], DL[:, :], AF.Sigmoid, scale=-1.0), "sa")
                        e.wait(v, "sa", sa_c)
                        e.inc(v.tensor_scalar(MR[:, :], DL[:, :], 0.0, None, ALU.is_gt), "sv")
                        e.inc(v.tensor_tensor(W1R[:, :], MR[:, :], SR1[:, :], ALU.mult), "sv")
                        e.inc(v.tensor_scalar(W1R[:, :], W1R[:, :], 0.5, None, ALU.mult), "sv")
                        e.inc(v.tensor_scalar(MR[:, :], MR[:, :], -0.5, 0.5, ALU.mult, ALU.add), "sv")
                        e.inc(v.tensor_tensor(W0R[:, :], MR[:, :], SR0[:, :], ALU.mult), "sv")
                        e.wait(t, "sv")
                        for (row, dstbc) in ((W0R, W0BC), (W1R, W1BC)):
                            for vt in range(2):
                                sl = slice(512 * vt, 512 * (vt + 1))
                                c1 = e.inc(t.matmul(pBC2[:, :], ONESR[0:1, 0:128], row[:, sl],
                                                    start=True, stop=True), "sm")
                                e.wait(v, "sm", c1)
                                cv = e.inc(v.tensor_copy(dstbc[:, sl], pBC2[:, :]), "sv")
                                e.wait(t, "sv", cv)
                    e.barrier()
                    with ExitStack() as actx, nc.named_scope(uname("FFNA")):
                        WF = actx.enter_context(nc.sbuf_tensor(uname("WF"), [128, 2, NCH, 128], F32))
                        ASL = actx.enter_context(nc.sbuf_tensor(uname("ASL"), [128, 2, TOK], F32))
                        AB1 = actx.enter_context(nc.sbuf_tensor(uname("AB1"), [128, 2, 512], F32))
                        TT = actx.enter_context(nc.sbuf_tensor(uname("TT"), [128, 2, 512], F32))
                        pH = [actx.enter_context(nc.psum_tensor(uname(f"pH{i}"), [128, 512], F32)) for i in range(4)]
                        wf_ready = [0, 0]
                        wf_guard = [0, 0]
                        ph_free = [0, 0, 0, 0]
                        ab_guard = [0, 0]
                        asl_guard = [0, 0]
                        for k in range(128):
                            kb = k % 2
                            wbc = W0BC if k < 64 else W1BC
                            e.wait(q, "sm", wf_guard[kb])
                            wf_ready[kb] = e.inc(
                                q.dma_start(out=WF[:, kb, :, :],
                                            in_=wfcT[:, 128 * k:128 * (k + 1)]
                                            .rearrange("(j p) h -> p j h", p=128)), f"wf{kb}", 16)
                            for vt in range(2):
                                pb = 2 * kb + vt
                                sl = slice(512 * vt, 512 * (vt + 1))
                                e.wait(t, f"wf{kb}", wf_ready[kb])
                                e.wait(t, "sv", ph_free[pb])
                                for j in range(NCH):
                                    mm = t.matmul(pH[pb][:, :], WF[:, kb, j, :], XL[:, j, sl],
                                                  start=(j == 0), stop=(j == NCH - 1))
                                ph_mm = e.inc(mm, "sm")
                                if vt == 1:
                                    wf_guard[kb] = ph_mm
                                e.wait(s, "sm", ph_mm)
                                e.wait(s, "sv", ab_guard[vt])
                                sa1 = e.inc(s.activation(AB1[:, vt, :], pH[pb][:, :], AF.Abs, scale=0.25), "sa")
                                e.wait(v, "sa", sa1)
                                e.wait(v, f"as{kb}", asl_guard[kb])
                                sv1 = e.inc(v.scalar_tensor_tensor(TT[:, vt, :], pH[pb][:, :], 0.75,
                                                                   AB1[:, vt, :], ALU.mult, ALU.add), "sv")
                                ph_free[pb] = sv1
                                e.wait(s, "sv", sv1)
                                sa2 = e.inc(s.activation(TT[:, vt, :], TT[:, vt, :], AF.Square), "sa")
                                e.wait(v, "sa", sa2)
                                sv2 = e.inc(v.tensor_tensor(ASL[:, kb, sl], TT[:, vt, :], wbc[:, sl],
                                                            ALU.mult), "sv")
                                ab_guard[vt] = sv2
                            e.wait(g, "sv", sv2)
                            asl_guard[kb] = e.inc(
                                g.dma_start(out=a_buf[128 * k:128 * (k + 1), :], in_=ASL[:, kb, :]),
                                f"as{kb}", 16)
                    e.barrier()
                    with ExitStack() as bctx, nc.named_scope(uname("FFNB")):
                        WP = bctx.enter_context(nc.sbuf_tensor(uname("WP"), [128, 3, 1024], F32))
                        AT = bctx.enter_context(nc.sbuf_tensor(uname("AT"), [128, 3, 512], F32))
                        pO = [bctx.enter_context(nc.psum_tensor(uname(f"pO{i}"), [128, 512], F32)) for i in range(8)]
                        wp_ready = [0, 0, 0]
                        at_ready = [0, 0, 0]
                        buf_guard = [0, 0, 0]
                        po_free = 0
                        for d2h in range(2):
                            for th in range(2):
                                sl = slice(512 * th, 512 * (th + 1))
                                for k in range(128):
                                    kb = k % 3
                                    e.wait(q, "sm", buf_guard[kb])
                                    wp_ready[kb] = e.inc(
                                        q.dma_start(out=WP[:, kb, :],
                                                    in_=wprojT[128 * k:128 * (k + 1),
                                                               1024 * d2h:1024 * (d2h + 1)]), f"wp{kb}", 16)
                                    e.wait(g, "sm", buf_guard[kb])
                                    at_ready[kb] = e.inc(
                                        g.dma_start(out=AT[:, kb, :],
                                                    in_=a_buf[128 * k:128 * (k + 1), sl]), f"at{kb}", 16)
                                    e.wait(t, f"wp{kb}", wp_ready[kb])
                                    e.wait(t, f"at{kb}", at_ready[kb])
                                    if k == 0:
                                        e.wait(t, "sv", po_free)
                                    for c8 in range(8):
                                        mm = t.matmul(pO[c8][:, :], WP[:, kb, 128 * c8:128 * (c8 + 1)],
                                                      AT[:, kb, :], start=(k == 0), stop=(k == 127))
                                    buf_guard[kb] = e.inc(mm, "sm")
                                e.wait(v, "sm")
                                for c8 in range(8):
                                    j = 8 * d2h + c8
                                    iv = v.tensor_tensor(XACC[:, j, sl], XACC[:, j, sl], pO[c8][:, :], ALU.add)
                                    po_free = e.inc(iv, "sv")
                e.barrier()

            def emit_attn():
                with ExitStack() as nctx, nc.named_scope(uname("LN2AG")):
                    XLJ = nctx.enter_context(nc.sbuf_tensor(uname("XLJ"), [128, 2, TOK], F32))
                    dma_guard = [0, 0]
                    for j in range(NCH):
                        jb = j % 2
                        e.wait(v, f"xj{jb}", dma_guard[jb])
                        cv = norm_chunk(XLJ[:, jb, :], j)
                        e.wait(q, "sv", cv)
                        dma_guard[jb] = e.inc(
                            q.dma_start(out=ag_in.ap()[128 * j:128 * (j + 1), :], in_=XLJ[:, jb, :]), f"xj{jb}", 16)
                    e.wait(g, "xj0")
                    e.wait(g, "xj1")
                cc1 = e.inc(g.collective_compute(
                    "AllGather", ALU.bypass, replica_groups=[list(range(8))],
                    ins=[ag_in.ap().opt()], outs=[ag_out.ap().opt()]), "sc")
                e.wait(g, "sc", cc1)
                e.barrier()
                if ATTN_SUB < 2:
                    return

                with ExitStack() as sctx, nc.named_scope(uname("SCAN")):
                    def ssb(name, shape):
                        return sctx.enter_context(nc.sbuf_tensor(name, shape, F32))
                    WQb, WKb, WSb = ssb("WQb", [128, NCH, 128]), ssb("WKb", [128, NCH, 128]), ssb("WSb", [128, NCH, 128])
                    WVb, WGb = ssb("WVb", [128, NCH, 256]), ssb("WGb", [128, NCH, 256])
                    XLT = ssb("XLT", [128, 2, NCH, 128])
                    QF = ssb(uname("QF"), [64, 2, 2, 128])
                    KT = ssb(uname("KT"), [128, 2, 128])
                    VT = ssb(uname("VT"), [128, 2, 256])
                    GT = ssb(uname("GT"), [128, 2, 256])
                    PT = ssb(uname("PT"), [128, 2, 128])
                    KF = ssb(uname("KF"), [64, 2, 128])
                    PF = ssb(uname("PF"), [64, 2, 128])
                    SM, W2M = ssb(uname("SM"), [128, 256]), ssb(uname("W2M"), [128, 256])
                    OKN, EXPT = ssb(uname("OKN"), [128, 128]), ssb(uname("EXPT"), [128, 128])
                    PV2 = ssb(uname("PV2"), [128, 128])
                    PV2F = ssb(uname("PV2F"), [64, 2, 128])
                    CD, RD = ssb(uname("CD"), [128, 128]), ssb(uname("RD"), [128, 128])
                    ON = ssb(uname("ON"), [128, 2, 256])
                    OF = ssb(uname("OF"), [128, 256])
                    SQG = ssb(uname("SQG"), [128, 256])
                    SGT = ssb(uname("SGT"), [128, 256])
                    SG = ssb(uname("SG"), [128, 2, 256])
                    NM, RSm = ssb(uname("NM"), [128, 2]), ssb(uname("RSm"), [128, 2])
                    GSm, MSm = ssb(uname("GSm"), [128, 2]), ssb(uname("MSm"), [128, 2])
                    RN = ssb(uname("RN"), [128, 2])
                    RSW, RSR = ssb(uname("RSW"), [128, 64]), ssb(uname("RSR"), [128, 64])
                    HST = ssb(uname("HST"), [64, 2, 64])
                    HVST = ssb(uname("HVST"), [64, 2, 128])
                    D0R = ssb(uname("D0R"), [1, 128])
                    pAp = [sctx.enter_context(nc.psum_tensor(uname(f"pA{i}"), [128, 512], F32)) for i in range(2)]
                    pBp = [sctx.enter_context(nc.psum_tensor(uname(f"pB{i}"), [128, 512], F32)) for i in range(2)]
                    pC = sctx.enter_context(nc.psum_tensor(uname("pC"), [128, 512], F32))
                    pD = sctx.enter_context(nc.psum_tensor(uname("pD"), [128, 512], F32))
                    pE = sctx.enter_context(nc.psum_tensor(uname("pE"), [128, 512], F32))
                    pF = sctx.enter_context(nc.psum_tensor(uname("pF"), [128, 512], F32))
                    for (buf, wsrc) in ((WQb, wq), (WKb, wk), (WSb, ws), (WVb, wv), (WGb, wg)):
                        e.inc(q.dma_start(out=buf[:, :, :], in_=wsrc.rearrange("(j p) c -> p j c", p=128)), "sd")
                    e.wait(v, "sd")
                    e.inc(v.memset(RSW[:, :], 1.0), "sv")
                    xlt_ready = [0, 0]
                    proj_mm = [0, 0]
                    ev_b = [0, 0]
                    pt_c = [0, 0]
                    bank_free = [0, 0]
                    tile_free = [0, 0]
                    of_guard = 0
                    st_upd_prev = 0
                    d0_ev_prev = 0
                    sm_ev_prev = 0
                    w2_ev_prev = 0
                    of_c_prev = 0

                    def emit_xlt(tcx):
                        cbx = tcx % 2
                        blkx, subx = tcx // 8, tcx % 8
                        e.wait(q, "sm", proj_mm[cbx])
                        xlt_ready[cbx] = e.inc(
                            q.dma_start(out=XLT[:, cbx, :, :],
                                        in_=ag_out.ap()[2048 * blkx:2048 * (blkx + 1),
                                                        128 * subx:128 * (subx + 1)]
                                        .rearrange("(j p) t -> p j t", p=128)), f"xt{cbx}", 16)

                    def proj_gen(tcx):
                        par = tcx % 2
                        pA, pB = pAp[par], pBp[par]
                        e.wait(t, f"xt{par}", xlt_ready[par])
                        e.wait(t, "sv", bank_free[par])
                        for j in range(NCH):
                            st0 = (j == 0)
                            lst = (j == NCH - 1)
                            t.matmul(pA[:, 256:384], XLT[:, par, j, :], WKb[:, j, :], start=st0, stop=lst)
                            t.matmul(pA[:, 384:512], XLT[:, par, j, :], WSb[:, j, :], start=False, stop=lst)
                            t.matmul(pA[0:64, 0:128], WQb[:, j, 0:64], XLT[:, par, j, :], start=False, stop=lst)
                            t.matmul(pA[0:64, 128:256], WQb[:, j, 64:128], XLT[:, par, j, :], start=False, stop=lst)
                            t.matmul(pB[:, 0:256], XLT[:, par, j, :], WVb[:, j, :], start=st0, stop=lst)
                            mm = t.matmul(pB[:, 256:512], XLT[:, par, j, :], WGb[:, j, :], start=False, stop=lst)
                            if lst:
                                proj_mm[par] = e.inc(mm, "sm")
                            yield

                    def emit_evacs(tcx):
                        par = tcx % 2
                        pA, pB = pAp[par], pBp[par]
                        e.wait(v, "sm", proj_mm[par])
                        e.wait(v, "sm", tile_free[par])
                        e.inc(v.tensor_scalar(QF[:, par, 0, :], pA[0:64, 0:128], 0.125, None, ALU.mult), "sv")
                        e.inc(v.tensor_scalar(QF[:, par, 1, :], pA[0:64, 128:256], 0.125, None, ALU.mult), "sv")
                        e.inc(v.tensor_copy(KT[:, par, :], pA[:, 256:384]), "sv")
                        sv_cl = e.inc(v.tensor_scalar(PT[:, par, :], pA[:, 384:512], -32.0, 32.0,
                                                      ALU.max, ALU.min), "sv")
                        e.inc(v.tensor_copy(VT[:, par, :], pB[:, 0:256]), "sv")
                        ev_b[par] = e.inc(v.tensor_copy(GT[:, par, :], pB[:, 256:512]), "sv")
                        e.wait(s, "sv", ev_b[par])
                        sgt_c = e.inc(s.activation(SGT[:, :], GT[:, par, :], AF.Tanh, scale=0.5), "sa")
                        pt_c[par] = e.inc(s.activation(PT[:, par, :], PT[:, par, :], AF.Exp,
                                                       bias=CONSTB[:, 0:1]), "sa")
                        e.wait(v, "sa", sgt_c)
                        e.inc(v.tensor_scalar(SG[:, par, :], SGT[:, :], 0.5, 0.5, ALU.mult, ALU.add), "sv")
                        e.inc(v.tensor_tensor(SG[:, par, :], SG[:, par, :], GT[:, par, :], ALU.mult), "sv")

                    def emit_ofinal(tc):
                        nonlocal of_guard, of_c_prev
                        if tc < 0:
                            return
                        par = tc % 2
                        pA = pAp[par]
                        blk, sub = tc // 8, tc % 8
                        e.wait(t, "sv", ov_free_arr[par])
                        t.matmul(pF[:, 256:384], ON[:, par, 0:128], IDENT[:, :], start=False, stop=True,
                                 is_transpose=True)
                        otr = e.inc(t.matmul(pF[:, 384:512], ON[:, par, 128:256], IDENT[:, :], start=False,
                                             stop=True, is_transpose=True), "sm")
                        e.wait(v, "sm", otr)
                        e.wait(v, "sg", of_guard)
                        of_c = e.inc(v.tensor_copy(OF[:, :], pF[:, 256:512]), "sv")
                        of_c_prev = of_c
                        e.wait(g, "sv", of_c)
                        for h in range(2):
                            of_guard = e.inc(g.dma_start(
                                out=a2a_in.ap()[256 * blk + 128 * h:256 * blk + 128 * (h + 1),
                                                128 * sub:128 * (sub + 1)],
                                in_=OF[:, 128 * h:128 * (h + 1)]), "sg", 16)

                    def emit_tail(tc, projit):
                        nonlocal of_guard, st_upd_prev, d0_ev_prev, sm_ev_prev, w2_ev_prev, of_c_prev
                        par = tc % 2
                        pA = pAp[par]
                        blk, sub = tc // 8, tc % 8

                        def pj(n=1):
                            for _ in range(n):
                                try:
                                    next(projit)
                                except StopIteration:
                                    return
                        if tc % 32 == 0:
                            e.wait(v, "sm", st_guard_mm[0])
                            e.inc(v.memset(HST[:, :, :], 0.0), "sv")
                            e.inc(v.memset(HVST[:, :, :], 0.0), "sv")
                            sv_c = e.inc(v.memset(D0R[:, :], 0.0), "sv")
                            e.wait(t, "sv", sv_c)
                        pj(2)
                        e.wait(t, "sv", st_upd_prev)
                        e.wait(t, "sa", d0_ev_prev)
                        e.wait(t, "sv", ev_b[par])
                        e.wait(t, "sa", pt_c[par])
                        t.matmul(pA[0:64, 0:128], KT[:, par, 0:64], IDENT[:, :], start=True, stop=True,
                                 is_transpose=True)
                        t.matmul(pA[0:64, 128:256], KT[:, par, 64:128], IDENT[:, :], start=False, stop=True,
                                 is_transpose=True)
                        t.matmul(pA[0:64, 256:384], PT[:, par, 0:64], IDENT[:, :], start=False, stop=True,
                                 is_transpose=True)
                        tr_mm = e.inc(t.matmul(pA[0:64, 384:512], PT[:, par, 64:128], IDENT[:, :],
                                               start=False, stop=True, is_transpose=True), "sm")
                        e.wait(v, "sm", tr_mm)
                        e.inc(v.tensor_copy(KF[:, 0, :], pA[0:64, 0:128]), "sv")
                        e.inc(v.tensor_copy(KF[:, 1, :], pA[0:64, 128:256]), "sv")
                        e.inc(v.tensor_copy(PF[:, 0, :], pA[0:64, 256:384]), "sv")
                        tr_ev = e.inc(v.tensor_copy(PF[:, 1, :], pA[0:64, 384:512]), "sv")
                        emit_ofinal(tc - 1)
                        pj(2)
                        e.wait(t, "sv", sm_ev_prev)
                        e.wait(t, "sv", tr_ev)
                        for h in range(2):
                            hs = slice(64 * h, 64 * (h + 1))
                            t.matmul(pC[:, hs], TRIU[:, :], PT[:, par, hs], start=(h == 0), stop=False)
                            t.matmul(pC[:, hs], ONESR[0:1, 0:128], D0R[0:1, hs], start=False, stop=True)
                        for h in range(2):
                            hs = slice(64 * h, 64 * (h + 1))
                            d0s = slice(128 + 64 * h, 128 + 64 * (h + 1))
                            t.matmul(pC[0:1, d0s], ONESC[:, 0:1], PT[:, par, hs], start=False, stop=False)
                            mm = t.matmul(pC[0:1, d0s], ONESR[0:1, 0:1], D0R[0:1, hs], start=False, stop=True)
                        cumd_mm = e.inc(mm, "sm")
                        e.wait(v, "sm", cumd_mm)
                        e.inc(v.tensor_copy(CD[:, :], pC[:, 0:128]), "sv")
                        rd_ev = e.inc(v.reciprocal(RD[:, :], CD[:, :]), "sv")
                        e.wait(s, "sm", cumd_mm)
                        e.wait(s, "sv", rd_ev)
                        d0_ev = e.inc(s.copy(D0R[0:1, :], pC[0:1, 128:256]), "sa")
                        pj(3)
                        e.wait(t, "sv", rd_ev)
                        e.wait(t, "sa", d0_ev)
                        for h in range(2):
                            mm = t.matmul(pC[:, 256 + 128 * h:256 + 128 * (h + 1)],
                                          KF[:, h, :], QF[:, par, h, :], start=False, stop=True)
                        st_mm = e.inc(mm, "sm")
                        e.wait(v, "sm", st_mm)
                        sm_ev = e.inc(v.tensor_tensor(SM[:, :], pC[:, 256:512], TRIU2[:, :], ALU.mult), "sv")
                        sm_ev_prev = sm_ev
                        pj(2)
                        e.wait(t, "sv", w2_ev_prev)
                        e.wait(t, "sv", sm_ev)
                        for h in range(2):
                            hs = slice(64 * h, 64 * (h + 1))
                            t.matmul(pD[:, hs], QF[:, par, h, :], HST[:, h, :], start=(h == 0), stop=False)
                            mm = t.matmul(pD[:, hs], SM[:, 128 * h:128 * (h + 1)], PT[:, par, hs],
                                          start=False, stop=True)
                        ok_mm = e.inc(mm, "sm")
                        for h in range(2):
                            hs = slice(64 * h, 64 * (h + 1))
                            mm = t.matmul(pE[0:64, hs], KT[:, par, hs], PT[:, par, hs],
                                          start=(h == 0), stop=True)
                        h_mm = e.inc(mm, "sm")
                        e.wait(v, "sm", ok_mm)
                        e.inc(v.tensor_tensor(OKN[:, :], pD[:, 0:128], RD[:, :], ALU.mult), "sv")
                        nm_c = e.inc(v.reduce_max(NM[:, :], OKN.rearrange("p (g x) -> p g x", g=2),
                                                  axis=AX, negate=True), "sv")
                        e.wait(s, "sv", nm_c)
                        sa_c = 0
                        for h in range(2):
                            hs = slice(64 * h, 64 * (h + 1))
                            sa_c = e.inc(s.activation(EXPT[:, hs], OKN[:, hs], AF.Exp,
                                                      bias=NM[:, h:h + 1]), "sa")
                        e.wait(v, "sa", sa_c)
                        rs_c = e.inc(v.reduce_sum(RSm[:, :], EXPT.rearrange("p (g x) -> p g x", g=2),
                                                  axis=AX), "sv")
                        e.wait(s, "sv", rs_c)
                        sa_c = e.inc(s.copy(RSW[:, 0:2], RSm[:, :]), "sa")
                        e.wait(v, "sa", sa_c)
                        e.inc(v.reciprocal(RSR[:, :], RSW[:, :]), "sv")
                        e.inc(v.tensor_tensor(PV2.rearrange("p (g x) -> p g x", g=2),
                                              EXPT.rearrange("p (g x) -> p g x", g=2),
                                              RSR[:, 0:2].rearrange("p (g x) -> p g x", x=1)
                                              .broadcast_to((128, 2, 64)), ALU.mult), "sv")
                        pv2_c = e.inc(v.tensor_tensor(PV2[:, :], PV2[:, :], RD[:, :], ALU.mult), "sv")
                        pj(4)
                        e.wait(t, "sv", pv2_c)
                        t.matmul(pA[0:64, 0:128], PV2[:, 0:64], IDENT[:, :], start=True, stop=True,
                                 is_transpose=True)
                        tr2 = e.inc(t.matmul(pA[0:64, 128:256], PV2[:, 64:128], IDENT[:, :],
                                             start=False, stop=True, is_transpose=True), "sm")
                        e.wait(v, "sm", tr2)
                        e.inc(v.tensor_copy(PV2F[:, 0, :], pA[0:64, 0:128]), "sv")
                        pv2f_c = e.inc(v.tensor_copy(PV2F[:, 1, :], pA[0:64, 128:256]), "sv")
                        bank_free[par] = pv2f_c
                        pj(2)
                        e.wait(t, "sv", pv2f_c)
                        for h in range(2):
                            mm = t.matmul(pD[:, 128 + 128 * h:128 + 128 * (h + 1)],
                                          PF[:, h, :], PV2F[:, h, :], start=False, stop=True)
                        w2_mm = e.inc(mm, "sm")
                        e.wait(v, "sm", w2_mm)
                        w2_ev = e.inc(v.tensor_tensor(W2M[:, :], pD[:, 128:384], TRIU2[:, :], ALU.mult), "sv")
                        w2_ev_prev = w2_ev
                        pj(1)
                        e.wait(t, "sv", of_c_prev)
                        e.wait(t, "sv", w2_ev)
                        for h in range(2):
                            vs = slice(128 * h, 128 * (h + 1))
                            t.matmul(pF[:, vs], PV2F[:, h, :], HVST[:, h, :], start=(h == 0), stop=False)
                            mm = t.matmul(pF[:, vs], W2M[:, vs], VT[:, par, vs], start=False, stop=True)
                        ov_mm = e.inc(mm, "sm")
                        for h in range(2):
                            vs = slice(128 + 128 * h, 128 + 128 * (h + 1))
                            mm = t.matmul(pE[0:64, vs], PT[:, par, 64 * h:64 * (h + 1)],
                                          VT[:, par, 128 * h:128 * (h + 1)], start=False, stop=True)
                        hv_mm = e.inc(mm, "sm")
                        tile_free[par] = hv_mm
                        st_guard_mm[0] = hv_mm
                        e.wait(v, "sm", hv_mm)
                        e.inc(v.tensor_tensor(HST[:, :, :], HST[:, :, :], pE[0:64, 0:128], ALU.add), "sv")
                        st_upd = e.inc(v.tensor_tensor(HVST[:, :, :], HVST[:, :, :], pE[0:64, 128:384],
                                                       ALU.add), "sv")
                        st_upd_prev = st_upd
                        e.wait(s, "sm", ov_mm)
                        sq_c = e.inc(s.activation(SQG[:, :], pF[:, 0:256], AF.Square), "sa")
                        e.wait(v, "sa", sq_c)
                        gs_c = e.inc(v.reduce_sum(GSm[:, :], SQG.rearrange("p (g x) -> p g x", g=2),
                                                  axis=AX), "sv")
                        e.wait(s, "sv", gs_c)
                        e.inc(s.activation(MSm[:, :], GSm[:, :], AF.Copy, scale=1.0 / DV, bias=EPS), "sa")
                        e.inc(s.activation(RN[:, :], MSm[:, :], AF.Ln), "sa")
                        rw_c = e.inc(s.activation(RN[:, :], RN[:, :], AF.Exp, scale=-0.5), "sa")
                        e.wait(v, "sa", rw_c)
                        e.inc(v.tensor_tensor(ON[:, par, :].rearrange("p (g x) -> p g x", g=2),
                                              pF[:, 0:256].rearrange("p (g x) -> p g x", g=2),
                                              RN[:, 0:2].rearrange("p (g x) -> p g x", x=1)
                                              .broadcast_to((128, 2, 128)), ALU.mult), "sv")
                        e.inc(v.tensor_tensor(ON[:, par, :], ON[:, par, :], GNB[:, :], ALU.mult), "sv")
                        ov_free_arr[par] = e.inc(v.tensor_tensor(ON[:, par, :], ON[:, par, :], SG[:, par, :], ALU.mult), "sv")
                        pj(8)

                    st_guard_mm = [0]
                    ov_free_arr = [0, 0]
                    emit_xlt(0)
                    g0 = proj_gen(0)
                    for _ in g0:
                        pass
                    emit_evacs(0)
                    for tc in range(NTC):
                        if tc + 1 < NTC:
                            emit_xlt(tc + 1)
                            projit = proj_gen(tc + 1)
                        else:
                            projit = iter(())
                        emit_tail(tc, projit)
                        if tc + 1 < NTC:
                            emit_evacs(tc + 1)
                    emit_ofinal(NTC - 1)
                    e.wait(g, "sg")
                e.barrier()
                if SCAN_LVL < 8:
                    return
                with allow_ici_all_to_all():
                    cc2 = e.inc(g.collective_compute(
                        "AllToAll", ALU.bypass, replica_groups=[list(range(8))],
                        ins=[a2a_in.ap().opt()], outs=[a2a_out.ap().opt()]), "sc")
                e.wait(g, "sc", cc2)
                e.barrier()
                if ATTN_SUB < 3:
                    return
                with ExitStack() as wctx, nc.named_scope(uname("WO")):
                    OFM = wctx.enter_context(nc.sbuf_tensor(uname("OFM"), [128, NCH, TOK], F32))
                    WOB = wctx.enter_context(nc.sbuf_tensor(uname("WOB"), [128, 2, NCH, 128], F32))
                    pW = [wctx.enter_context(nc.psum_tensor(uname(f"pW{i}"), [128, 512], F32)) for i in range(4)]
                    cd = e.inc(q.dma_start(out=OFM[:, :, :],
                                           in_=a2a_out.ap().rearrange("(j p) t -> p j t", p=128)), "ofm", 16)
                    wo_ready = [0, 0]
                    wo_guard = [0, 0]
                    pw_free = [0, 0, 0, 0]
                    for d2 in range(NCH):
                        db = d2 % 2
                        e.wait(q, "sm", wo_guard[db])
                        wo_ready[db] = e.inc(
                            q.dma_start(out=WOB[:, db, :, :],
                                        in_=wo[:, 128 * d2:128 * (d2 + 1)]
                                        .rearrange("(j p) c -> p j c", p=128)), f"wo{db}", 16)
                        e.wait(t, f"wo{db}", wo_ready[db])
                        e.wait(t, "ofm", cd)
                        for th in range(2):
                            pb = 2 * db + th
                            sl = slice(512 * th, 512 * (th + 1))
                            e.wait(t, "sv", pw_free[pb])
                            for jv in range(NCH):
                                mm = t.matmul(pW[pb][:, :], WOB[:, db, jv, :], OFM[:, jv, sl],
                                              start=(jv == 0), stop=(jv == NCH - 1))
                            cm = e.inc(mm, "sm")
                            if th == 1:
                                wo_guard[db] = cm
                            e.wait(v, "sm", cm)
                            iv = v.tensor_tensor(XACC[:, d2, sl], XACC[:, d2, sl], pW[pb][:, :], ALU.add)
                            pw_free[pb] = e.inc(iv, "sv")
                e.barrier()

            if STAGE_FFN1:
                emit_ln_stage()
                emit_ffn()
            if STAGE_ATTN:
                emit_ln_stage()
                emit_attn()
            if STAGE_FFN2:
                emit_ln_stage()
                emit_ffn()

            e.barrier()
            cst = e.inc(q.dma_start(out=y_out.rearrange("(j p) t -> p j t", p=128),
                                    in_=XACC[:, :, :]), "sd")
            if DEBUG_DUMP:
                e.inc(q.dma_start(out=dbg_ai[:, :], in_=a2a_in.ap()[:, :]), "sd")
                e.inc(q.dma_start(out=dbg_ao[:, :], in_=a2a_out.ap()[:, :]), "sd")
            e.wait(q, "sd")

    return nc


_NC_CACHE = {}


def _get_nc():
    key = (STAGE_FFN1, STAGE_ATTN, STAGE_FFN2, ATTN_SUB, SCAN_LVL, DEBUG_DUMP, DEBUG_SCAN)
    if key not in _NC_CACHE:
        _NC_CACHE[key] = build_program()
    return _NC_CACHE[key]


def kernel(L0_attn_hidden_states, L0_attn_Wq, L0_attn_Wk, L0_attn_Wv, L0_attn_Wg,
           L0_attn_Ws, L0_attn_Wo, L0_attn_g_norm_weight, L0_ffn_w_router,
           L0_ffn_w_fc, L0_ffn_w_proj):
    x = np.asarray(L0_attn_hidden_states, np.float32).reshape(8192, 2048)
    wfcT_h = np.ascontiguousarray(
        np.transpose(np.asarray(L0_ffn_w_fc, np.float32), (2, 0, 1)).reshape(2048, 16384))
    wprojT_h = np.ascontiguousarray(
        np.transpose(np.asarray(L0_ffn_w_proj, np.float32), (0, 2, 1)).reshape(16384, 2048))
    wrT_h = np.ascontiguousarray(np.asarray(L0_ffn_w_router, np.float32).T)
    Wq = np.asarray(L0_attn_Wq, np.float32)
    Wk = np.asarray(L0_attn_Wk, np.float32)
    Ws = np.asarray(L0_attn_Ws, np.float32)
    Wv = np.asarray(L0_attn_Wv, np.float32)
    Wg = np.asarray(L0_attn_Wg, np.float32)
    Wo_h = np.ascontiguousarray(np.asarray(L0_attn_Wo, np.float32))
    gn = np.asarray(L0_attn_g_norm_weight, np.float32)
    gnb_h = np.ascontiguousarray(np.broadcast_to(np.tile(gn, 2)[None, :], (128, 256)))
    triu_h = np.triu(np.ones((128, 128), np.float32))
    ident_h = np.eye(128, dtype=np.float32)

    in_maps = []
    for c in range(8):
        in_maps.append({
            "x_in": np.ascontiguousarray(x[1024 * c:1024 * (c + 1), :].T),
            "wfcT": wfcT_h, "wprojT": wprojT_h, "wrT": wrT_h,
            "wq": np.ascontiguousarray(Wq[:, 128 * c:128 * (c + 1)]),
            "wk": np.ascontiguousarray(Wk[:, 128 * c:128 * (c + 1)]),
            "ws": np.ascontiguousarray(Ws[:, 128 * c:128 * (c + 1)]),
            "wv": np.ascontiguousarray(Wv[:, 256 * c:256 * (c + 1)]),
            "wg": np.ascontiguousarray(Wg[:, 256 * c:256 * (c + 1)]),
            "wo": Wo_h, "gnb": gnb_h, "triu_in": triu_h, "triu2_in": np.ascontiguousarray(np.tile(triu_h, (1, 2))), "ident_in": ident_h,
        })

    nc = _get_nc()
    res = run_bass_kernel_spmd(nc, in_maps, core_ids=list(range(8)))
    out = np.empty((8192, 2048), np.float32)
    for c in range(8):
        out[1024 * c:1024 * (c + 1), :] = res.results[c]["y"].T
    return out.reshape(2, 4096, 2048)


# version 11
# speedup vs baseline: 1.0487x; 1.0050x over previous
import numpy as np
from contextlib import ExitStack

import concourse.bass as bass
import concourse.mybir as mybir
from concourse.bass_utils import run_bass_kernel_spmd
from concourse.aot_env import allow_ici_all_to_all

F32 = mybir.dt.float32
AF = mybir.ActivationFunctionType
ALU = mybir.AluOpType
AX = mybir.AxisListType.X

D = 2048
NCH = 16
TOK = 1024
HID2 = 16384
DK, DV, M = 64, 128, 64
NTC = 64
EPS = 1e-5

STAGE_FFN1 = True
STAGE_ATTN = True
STAGE_FFN2 = True
ATTN_SUB = 3
SCAN_LVL = 8
DEBUG_DUMP = False
DEBUG_SCAN = False


class Emit:
    def __init__(self, nc, ctx):
        self.nc = nc
        self.t, self.v, self.s, self.q, self.g = nc.tensor, nc.vector, nc.scalar, nc.sync, nc.gpsimd
        self.sem, self.cnt = {}, {}
        for name in ["sd", "sg", "sm", "sv", "sa", "sc"]:
            self.sem[name] = ctx.enter_context(nc.semaphore(name))
            self.cnt[name] = 0
        for name in ["wf0", "wf1", "wp0", "wp1", "wp2", "at0", "at1", "at2",
                     "xt0", "xt1", "wo0", "wo1", "ofm", "xj0", "xj1", "as0", "as1"]:
            self.sem[name] = ctx.enter_context(nc.semaphore(name))
            self.cnt[name] = 0

    def inc(self, instr, name, amt=1):
        if name in ("sd", "sg") and amt == 1:
            amt = 16
        self.cnt[name] += amt
        instr.then_inc(self.sem[name], amt)
        return self.cnt[name]

    def wait(self, eng, name, value=None):
        v = self.cnt[name] if value is None else value
        if v > 0:
            eng.wait_ge(self.sem[name], v)

    def barrier(self):
        engs = {"sd": self.q, "sg": self.g, "sm": self.t, "sv": self.v, "sa": self.s}
        allsems = list(self.sem.keys())
        for ename, eng in engs.items():
            for sname in allsems:
                if sname != ename:
                    self.wait(eng, sname)


def build_program():
    nc = bass.Bass("TRN2", target_bir_lowering=False)

    x_in = nc.dram_tensor("x_in", [D, TOK], F32, kind="ExternalInput").ap()
    wfcT = nc.dram_tensor("wfcT", [D, HID2], F32, kind="ExternalInput").ap()
    wprojT = nc.dram_tensor("wprojT", [HID2, D], F32, kind="ExternalInput").ap()
    wrT = nc.dram_tensor("wrT", [D, 2], F32, kind="ExternalInput").ap()
    wq = nc.dram_tensor("wq", [D, 128], F32, kind="ExternalInput").ap()
    wk = nc.dram_tensor("wk", [D, 128], F32, kind="ExternalInput").ap()
    ws = nc.dram_tensor("ws", [D, 128], F32, kind="ExternalInput").ap()
    wv = nc.dram_tensor("wv", [D, 256], F32, kind="ExternalInput").ap()
    wg = nc.dram_tensor("wg", [D, 256], F32, kind="ExternalInput").ap()
    wo = nc.dram_tensor("wo", [D, D], F32, kind="ExternalInput").ap()
    gnb = nc.dram_tensor("gnb", [128, 256], F32, kind="ExternalInput").ap()
    triu_in = nc.dram_tensor("triu_in", [128, 128], F32, kind="ExternalInput").ap()
    triu2_in = nc.dram_tensor("triu2_in", [128, 256], F32, kind="ExternalInput").ap()
    ident_in = nc.dram_tensor("ident_in", [128, 128], F32, kind="ExternalInput").ap()

    y_out = nc.dram_tensor("y", [D, TOK], F32, kind="ExternalOutput").ap()
    if DEBUG_DUMP:
        dbg_ai = nc.dram_tensor("dbg_ai", [D, TOK], F32, kind="ExternalOutput").ap()
        dbg_ao = nc.dram_tensor("dbg_ao", [D, TOK], F32, kind="ExternalOutput").ap()
    if DEBUG_SCAN:
        dbg_sc = nc.dram_tensor("dbg_sc", [2, 22, 128, 256], F32, kind="ExternalOutput").ap()

    a_buf = nc.dram_tensor("a_buf", [HID2, TOK], F32).ap()
    ag_in = nc.dram_tensor("ag_in", [D, TOK], F32)
    ag_out = nc.dram_tensor("ag_out", [8 * D, TOK], F32, addr_space="Shared")
    a2a_in = nc.dram_tensor("a2a_in", [D, TOK], F32)
    a2a_out = nc.dram_tensor("a2a_out", [D, TOK], F32)

    with ExitStack() as ctx:
        e = Emit(nc, ctx)
        t, v, s, q, g = e.t, e.v, e.s, e.q, e.g
        _uid = [0]

        def uname(base):
            _uid[0] += 1
            return f"{base}_{_uid[0]}"

        def sb(name, shape):
            return ctx.enter_context(nc.sbuf_tensor(name, shape, F32))

        XACC = sb("XACC", [128, NCH, TOK])
        TRIU = sb("TRIU", [128, 128])
        TRIU2 = sb("TRIU2", [128, 256])
        IDENT = sb("IDENT", [128, 128])
        ONESC = sb("ONESC", [128, 1])
        ONESR = sb("ONESR", [1, 512])
        GNB = sb("GNB", [128, 256])
        MU_BC = sb("MU_BC", [128, TOK])
        R_BC = sb("R_BC", [128, TOK])
        CONSTB = sb("CONSTB", [128, 4])
        e.sq_guard = [0, 0]

        block = ctx.enter_context(nc.Block())

        @block.sync
        def _(_sync):
            e.inc(q.dma_start(out=XACC[:, :, :], in_=x_in.rearrange("(j p) t -> p j t", p=128)), "sd")
            e.inc(q.dma_start(out=TRIU[:, :], in_=triu_in[:, :]), "sd")
            e.inc(q.dma_start(out=TRIU2[:, :], in_=triu2_in[:, :]), "sd")
            e.inc(q.dma_start(out=IDENT[:, :], in_=ident_in[:, :]), "sd")
            e.inc(q.dma_start(out=GNB[:, :], in_=gnb[:, :]), "sd")
            e.wait(v, "sd")
            e.inc(v.memset(ONESC[:, :], 1.0), "sv")
            e.inc(v.memset(ONESR[:, :], 1.0), "sv")
            e.inc(g.memset(CONSTB[:, 0:1], -32.0), "sg", 1)
            e.inc(g.memset(CONSTB[:, 1:2], EPS), "sg", 1)
            e.inc(g.memset(CONSTB[:, 2:3], 1.5), "sg", 1)
            e.barrier()

            def emit_ln_stage():
                with ExitStack() as lctx, nc.named_scope(uname("LN")):
                    pR0 = lctx.enter_context(nc.psum_tensor(uname("pR0"), [1, 512], F32))
                    pR1 = lctx.enter_context(nc.psum_tensor(uname("pR1"), [1, 512], F32))
                    pBC = lctx.enter_context(nc.psum_tensor(uname("pBC"), [128, 512], F32))
                    XSQ = lctx.enter_context(nc.sbuf_tensor(uname("XSQ"), [128, 2, 512], F32))
                    MU = lctx.enter_context(nc.sbuf_tensor(uname("MU"), [1, TOK], F32))
                    RR = lctx.enter_context(nc.sbuf_tensor(uname("RR"), [1, TOK], F32))
                    T0 = lctx.enter_context(nc.sbuf_tensor(uname("T0"), [1, TOK], F32))
                    T1 = lctx.enter_context(nc.sbuf_tensor(uname("T1"), [1, TOK], F32))
                    T2 = lctx.enter_context(nc.sbuf_tensor(uname("T2"), [1, TOK], F32))
                    for vt in range(2):
                        sl = slice(512 * vt, 512 * (vt + 1))
                        for j in range(NCH):
                            if j >= 2:
                                e.wait(s, "sm", e.sq_guard[j % 2])
                            sa_c = e.inc(s.activation(XSQ[:, j % 2, :], XACC[:, j, sl], AF.Square), "sa")
                            t.matmul(pR0[:, :], ONESC[:, 0:1], XACC[:, j, sl],
                                     start=(j == 0), stop=(j == NCH - 1))
                            e.wait(t, "sa", sa_c)
                            mm2 = t.matmul(pR1[:, :], ONESC[:, 0:1], XSQ[:, j % 2, :],
                                           start=(j == 0), stop=(j == NCH - 1))
                            e.sq_guard[j % 2] = e.inc(mm2, "sm")
                        e.wait(s, "sm")
                        s.activation(MU[:, sl], pR0[:, :], AF.Copy, scale=1.0 / D)
                        sa_c = e.inc(s.activation(T0[:, sl], pR1[:, :], AF.Copy, scale=1.0 / D), "sa")
                        e.wait(t, "sa", sa_c)
                    e.wait(v, "sa")
                    e.inc(v.tensor_tensor(T1[:, :], MU[:, :], MU[:, :], ALU.mult), "sv")
                    sv_c = e.inc(v.tensor_tensor(T0[:, :], T0[:, :], T1[:, :], ALU.subtract), "sv")
                    e.wait(s, "sv", sv_c)
                    e.inc(s.activation(T2[:, :], T0[:, :], AF.Copy, bias=EPS), "sa")
                    sa_c = e.inc(s.activation(T1[:, :], T2[:, :], AF.Sqrt), "sa")
                    e.wait(v, "sa", sa_c)
                    e.inc(v.reciprocal(T0[:, :], T1[:, :]), "sv")
                    e.inc(v.tensor_tensor(T1[:, :], T0[:, :], T0[:, :], ALU.mult), "sv")
                    sv_c = e.inc(v.tensor_tensor(T1[:, :], T1[:, :], T2[:, :], ALU.mult), "sv")
                    e.wait(s, "sv", sv_c)
                    sa_c = e.inc(s.activation(T1[:, :], T1[:, :], AF.Copy, scale=-0.5, bias=1.5), "sa")
                    e.wait(v, "sa", sa_c)
                    e.inc(v.tensor_tensor(RR[:, :], T0[:, :], T1[:, :], ALU.mult), "sv")
                    e.wait(t, "sv")
                    for (row, dstbc) in ((MU, MU_BC), (RR, R_BC)):
                        for vt in range(2):
                            sl = slice(512 * vt, 512 * (vt + 1))
                            c1 = e.inc(t.matmul(pBC[:, :], ONESR[0:1, 0:128], row[:, sl],
                                                start=True, stop=True), "sm")
                            e.wait(v, "sm", c1)
                            cv = e.inc(v.tensor_copy(dstbc[:, sl], pBC[:, :]), "sv")
                            e.wait(t, "sv", cv)

            def norm_chunk(dst_ap, j):
                e.inc(v.tensor_tensor(dst_ap, XACC[:, j, :], MU_BC[:, :], ALU.subtract), "sv")
                return e.inc(v.tensor_tensor(dst_ap, dst_ap, R_BC[:, :], ALU.mult), "sv")

            def emit_ffn():
                with ExitStack() as fctx:
                    XL = fctx.enter_context(nc.sbuf_tensor(uname("XL"), [128, NCH, TOK], F32))
                    W0BC = fctx.enter_context(nc.sbuf_tensor(uname("W0BC"), [128, TOK], F32))
                    W1BC = fctx.enter_context(nc.sbuf_tensor(uname("W1BC"), [128, TOK], F32))
                    for j in range(NCH):
                        norm_chunk(XL[:, j, :], j)
                    with ExitStack() as rctx, nc.named_scope(uname("ROUTER")):
                        pDLr = rctx.enter_context(nc.psum_tensor(uname("pDLr"), [1, 512], F32))
                        pBC2 = rctx.enter_context(nc.psum_tensor(uname("pBC2"), [128, 512], F32))
                        WRT = rctx.enter_context(nc.sbuf_tensor(uname("WRT"), [128, NCH, 2], F32))
                        WRD = rctx.enter_context(nc.sbuf_tensor(uname("WRD"), [128, NCH], F32))
                        SR0 = rctx.enter_context(nc.sbuf_tensor(uname("SR0"), [1, TOK], F32))
                        SR1 = rctx.enter_context(nc.sbuf_tensor(uname("SR1"), [1, TOK], F32))
                        MR = rctx.enter_context(nc.sbuf_tensor(uname("MR"), [1, TOK], F32))
                        DL = rctx.enter_context(nc.sbuf_tensor(uname("DL"), [1, TOK], F32))
                        W0R = rctx.enter_context(nc.sbuf_tensor(uname("W0R"), [1, TOK], F32))
                        W1R = rctx.enter_context(nc.sbuf_tensor(uname("W1R"), [1, TOK], F32))
                        cd = e.inc(q.dma_start(out=WRT[:, :, :],
                                               in_=wrT.rearrange("(j p) c -> p j c", p=128)), "sd")
                        e.wait(v, "sd", cd)
                        sv_c = e.inc(v.tensor_tensor(WRD[:, :], WRT[:, :, 1], WRT[:, :, 0], ALU.subtract), "sv")
                        e.wait(t, "sv")
                        for vt in range(2):
                            sl = slice(512 * vt, 512 * (vt + 1))
                            for j in range(NCH):
                                mm = t.matmul(pDLr[:, :], WRD[:, j:j + 1], XL[:, j, sl],
                                              start=(j == 0), stop=(j == NCH - 1))
                            cm = e.inc(mm, "sm")
                            e.wait(s, "sm", cm)
                            sa_c = e.inc(s.copy(DL[:, sl], pDLr[:, :]), "sa")
                            e.wait(t, "sa", sa_c)
                        e.wait(s, "sa")
                        e.inc(s.activation(SR1[:, :], DL[:, :], AF.Sigmoid), "sa")
                        sa_c = e.inc(s.activation(SR0[:, :], DL[:, :], AF.Sigmoid, scale=-1.0), "sa")
                        e.wait(v, "sa", sa_c)
                        e.inc(v.tensor_scalar(MR[:, :], DL[:, :], 0.0, None, ALU.is_gt), "sv")
                        e.inc(v.tensor_tensor(W1R[:, :], MR[:, :], SR1[:, :], ALU.mult), "sv")
                        e.inc(v.tensor_scalar(W1R[:, :], W1R[:, :], 0.5, None, ALU.mult), "sv")
                        e.inc(v.tensor_scalar(MR[:, :], MR[:, :], -0.5, 0.5, ALU.mult, ALU.add), "sv")
                        e.inc(v.tensor_tensor(W0R[:, :], MR[:, :], SR0[:, :], ALU.mult), "sv")
                        e.wait(t, "sv")
                        for (row, dstbc) in ((W0R, W0BC), (W1R, W1BC)):
                            for vt in range(2):
                                sl = slice(512 * vt, 512 * (vt + 1))
                                c1 = e.inc(t.matmul(pBC2[:, :], ONESR[0:1, 0:128], row[:, sl],
                                                    start=True, stop=True), "sm")
                                e.wait(v, "sm", c1)
                                cv = e.inc(v.tensor_copy(dstbc[:, sl], pBC2[:, :]), "sv")
                                e.wait(t, "sv", cv)
                    e.barrier()
                    with ExitStack() as actx, nc.named_scope(uname("FFNA")):
                        WF = actx.enter_context(nc.sbuf_tensor(uname("WF"), [128, 2, NCH, 128], F32))
                        ASL = actx.enter_context(nc.sbuf_tensor(uname("ASL"), [128, 2, TOK], F32))
                        AB1 = actx.enter_context(nc.sbuf_tensor(uname("AB1"), [128, 2, 512], F32))
                        TT = actx.enter_context(nc.sbuf_tensor(uname("TT"), [128, 2, 512], F32))
                        pH = [actx.enter_context(nc.psum_tensor(uname(f"pH{i}"), [128, 512], F32)) for i in range(4)]
                        wf_ready = [0, 0]
                        wf_guard = [0, 0]
                        ph_free = [0, 0, 0, 0]
                        ab_guard = [0, 0]
                        asl_guard = [0, 0]
                        for k in range(128):
                            kb = k % 2
                            wbc = W0BC if k < 64 else W1BC
                            e.wait(q, "sm", wf_guard[kb])
                            wf_ready[kb] = e.inc(
                                q.dma_start(out=WF[:, kb, :, :],
                                            in_=wfcT[:, 128 * k:128 * (k + 1)]
                                            .rearrange("(j p) h -> p j h", p=128)), f"wf{kb}", 16)
                            for vt in range(2):
                                pb = 2 * kb + vt
                                sl = slice(512 * vt, 512 * (vt + 1))
                                e.wait(t, f"wf{kb}", wf_ready[kb])
                                e.wait(t, "sv", ph_free[pb])
                                for j in range(NCH):
                                    mm = t.matmul(pH[pb][:, :], WF[:, kb, j, :], XL[:, j, sl],
                                                  start=(j == 0), stop=(j == NCH - 1))
                                ph_mm = e.inc(mm, "sm")
                                if vt == 1:
                                    wf_guard[kb] = ph_mm
                                e.wait(s, "sm", ph_mm)
                                e.wait(s, "sv", ab_guard[vt])
                                sa1 = e.inc(s.activation(AB1[:, vt, :], pH[pb][:, :], AF.Abs, scale=0.25), "sa")
                                e.wait(v, "sa", sa1)
                                e.wait(v, f"as{kb}", asl_guard[kb])
                                sv1 = e.inc(v.scalar_tensor_tensor(TT[:, vt, :], pH[pb][:, :], 0.75,
                                                                   AB1[:, vt, :], ALU.mult, ALU.add), "sv")
                                ph_free[pb] = sv1
                                e.wait(s, "sv", sv1)
                                sa2 = e.inc(s.activation(TT[:, vt, :], TT[:, vt, :], AF.Square), "sa")
                                e.wait(v, "sa", sa2)
                                sv2 = e.inc(v.tensor_tensor(ASL[:, kb, sl], TT[:, vt, :], wbc[:, sl],
                                                            ALU.mult), "sv")
                                ab_guard[vt] = sv2
                            e.wait(g, "sv", sv2)
                            asl_guard[kb] = e.inc(
                                g.dma_start(out=a_buf[128 * k:128 * (k + 1), :], in_=ASL[:, kb, :]),
                                f"as{kb}", 16)
                    e.barrier()
                    with ExitStack() as bctx, nc.named_scope(uname("FFNB")):
                        WP = bctx.enter_context(nc.sbuf_tensor(uname("WP"), [128, 3, 1024], F32))
                        AT = bctx.enter_context(nc.sbuf_tensor(uname("AT"), [128, 3, 512], F32))
                        pO = [bctx.enter_context(nc.psum_tensor(uname(f"pO{i}"), [128, 512], F32)) for i in range(8)]
                        wp_ready = [0, 0, 0]
                        at_ready = [0, 0, 0]
                        buf_guard = [0, 0, 0]
                        po_free = 0
                        for d2h in range(2):
                            for th in range(2):
                                sl = slice(512 * th, 512 * (th + 1))
                                for k in range(128):
                                    kb = k % 3
                                    e.wait(q, "sm", buf_guard[kb])
                                    wp_ready[kb] = e.inc(
                                        q.dma_start(out=WP[:, kb, :],
                                                    in_=wprojT[128 * k:128 * (k + 1),
                                                               1024 * d2h:1024 * (d2h + 1)]), f"wp{kb}", 16)
                                    e.wait(g, "sm", buf_guard[kb])
                                    at_ready[kb] = e.inc(
                                        g.dma_start(out=AT[:, kb, :],
                                                    in_=a_buf[128 * k:128 * (k + 1), sl]), f"at{kb}", 16)
                                    e.wait(t, f"wp{kb}", wp_ready[kb])
                                    e.wait(t, f"at{kb}", at_ready[kb])
                                    if k == 0:
                                        e.wait(t, "sv", po_free)
                                    for c8 in range(8):
                                        mm = t.matmul(pO[c8][:, :], WP[:, kb, 128 * c8:128 * (c8 + 1)],
                                                      AT[:, kb, :], start=(k == 0), stop=(k == 127))
                                    buf_guard[kb] = e.inc(mm, "sm")
                                e.wait(v, "sm")
                                for c8 in range(8):
                                    j = 8 * d2h + c8
                                    iv = v.tensor_tensor(XACC[:, j, sl], XACC[:, j, sl], pO[c8][:, :], ALU.add)
                                    po_free = e.inc(iv, "sv")
                e.barrier()

            def emit_attn():
                with ExitStack() as nctx, nc.named_scope(uname("LN2AG")):
                    XLJ = nctx.enter_context(nc.sbuf_tensor(uname("XLJ"), [128, 2, TOK], F32))
                    dma_guard = [0, 0]
                    for j in range(NCH):
                        jb = j % 2
                        e.wait(v, f"xj{jb}", dma_guard[jb])
                        cv = norm_chunk(XLJ[:, jb, :], j)
                        e.wait(q, "sv", cv)
                        dma_guard[jb] = e.inc(
                            q.dma_start(out=ag_in.ap()[128 * j:128 * (j + 1), :], in_=XLJ[:, jb, :]), f"xj{jb}", 16)
                    e.wait(g, "xj0")
                    e.wait(g, "xj1")
                cc1 = e.inc(g.collective_compute(
                    "AllGather", ALU.bypass, replica_groups=[list(range(8))],
                    ins=[ag_in.ap().opt()], outs=[ag_out.ap().opt()]), "sc")
                e.wait(g, "sc", cc1)
                e.barrier()
                if ATTN_SUB < 2:
                    return

                with ExitStack() as sctx, nc.named_scope(uname("SCAN")):
                    def ssb(name, shape):
                        return sctx.enter_context(nc.sbuf_tensor(name, shape, F32))
                    WQb, WKb, WSb = ssb("WQb", [128, NCH, 128]), ssb("WKb", [128, NCH, 128]), ssb("WSb", [128, NCH, 128])
                    WVb, WGb = ssb("WVb", [128, NCH, 256]), ssb("WGb", [128, NCH, 256])
                    XLT = ssb("XLT", [128, 2, NCH, 128])
                    QF = ssb(uname("QF"), [64, 2, 2, 128])
                    KT = ssb(uname("KT"), [128, 2, 128])
                    VT = ssb(uname("VT"), [128, 2, 256])
                    GT = ssb(uname("GT"), [128, 2, 256])
                    PT = ssb(uname("PT"), [128, 2, 128])
                    KF = ssb(uname("KF"), [64, 2, 128])
                    PF = ssb(uname("PF"), [64, 2, 128])
                    SM, W2M = ssb(uname("SM"), [128, 256]), ssb(uname("W2M"), [128, 256])
                    OKN, EXPT = ssb(uname("OKN"), [128, 128]), ssb(uname("EXPT"), [128, 128])
                    PV2 = ssb(uname("PV2"), [128, 128])
                    PV2F = ssb(uname("PV2F"), [64, 2, 128])
                    CD, RD = ssb(uname("CD"), [128, 128]), ssb(uname("RD"), [128, 128])
                    ON = ssb(uname("ON"), [128, 2, 256])
                    OF = ssb(uname("OF"), [128, 256])
                    SQG = ssb(uname("SQG"), [128, 256])
                    SGT = ssb(uname("SGT"), [128, 256])
                    SG = ssb(uname("SG"), [128, 2, 256])
                    NM, RSm = ssb(uname("NM"), [128, 2]), ssb(uname("RSm"), [128, 2])
                    GSm, MSm = ssb(uname("GSm"), [128, 2]), ssb(uname("MSm"), [128, 2])
                    RN = ssb(uname("RN"), [128, 2])
                    RSW, RSR = ssb(uname("RSW"), [128, 64]), ssb(uname("RSR"), [128, 64])
                    HST = ssb(uname("HST"), [64, 2, 64])
                    HVST = ssb(uname("HVST"), [64, 2, 128])
                    D0R = ssb(uname("D0R"), [1, 128])
                    pAp = [sctx.enter_context(nc.psum_tensor(uname(f"pA{i}"), [128, 512], F32)) for i in range(2)]
                    pBp = [sctx.enter_context(nc.psum_tensor(uname(f"pB{i}"), [128, 512], F32)) for i in range(2)]
                    pC = sctx.enter_context(nc.psum_tensor(uname("pC"), [128, 512], F32))
                    pD = sctx.enter_context(nc.psum_tensor(uname("pD"), [128, 512], F32))
                    pE = sctx.enter_context(nc.psum_tensor(uname("pE"), [128, 512], F32))
                    pF = sctx.enter_context(nc.psum_tensor(uname("pF"), [128, 512], F32))
                    for (buf, wsrc) in ((WQb, wq), (WKb, wk), (WSb, ws), (WVb, wv), (WGb, wg)):
                        e.inc(q.dma_start(out=buf[:, :, :], in_=wsrc.rearrange("(j p) c -> p j c", p=128)), "sd")
                    e.wait(v, "sd")
                    e.inc(v.memset(RSW[:, :], 1.0), "sv")
                    xlt_ready = [0, 0]
                    proj_mm = [0, 0]
                    ev_b = [0, 0]
                    pt_c = [0, 0]
                    bank_free = [0, 0]
                    tile_free = [0, 0]
                    of_guard = 0
                    st_upd_prev = 0
                    d0_ev_prev = 0
                    sm_ev_prev = 0
                    w2_ev_prev = 0
                    of_c_prev = 0

                    def emit_xlt(tcx):
                        cbx = tcx % 2
                        blkx, subx = tcx // 8, tcx % 8
                        e.wait(q, "sm", proj_mm[cbx])
                        xlt_ready[cbx] = e.inc(
                            q.dma_start(out=XLT[:, cbx, :, :],
                                        in_=ag_out.ap()[2048 * blkx:2048 * (blkx + 1),
                                                        128 * subx:128 * (subx + 1)]
                                        .rearrange("(j p) t -> p j t", p=128)), f"xt{cbx}", 16)

                    def proj_gen(tcx):
                        par = tcx % 2
                        pA, pB = pAp[par], pBp[par]
                        e.wait(t, f"xt{par}", xlt_ready[par])
                        e.wait(t, "sv", bank_free[par])
                        for j in range(NCH):
                            st0 = (j == 0)
                            lst = (j == NCH - 1)
                            t.matmul(pA[:, 256:384], XLT[:, par, j, :], WKb[:, j, :], start=st0, stop=lst)
                            t.matmul(pA[:, 384:512], XLT[:, par, j, :], WSb[:, j, :], start=False, stop=lst)
                            t.matmul(pA[0:64, 0:128], WQb[:, j, 0:64], XLT[:, par, j, :], start=False, stop=lst)
                            t.matmul(pA[0:64, 128:256], WQb[:, j, 64:128], XLT[:, par, j, :], start=False, stop=lst)
                            t.matmul(pB[:, 0:256], XLT[:, par, j, :], WVb[:, j, :], start=st0, stop=lst)
                            mm = t.matmul(pB[:, 256:512], XLT[:, par, j, :], WGb[:, j, :], start=False, stop=lst)
                            if lst:
                                proj_mm[par] = e.inc(mm, "sm")
                            yield

                    def emit_evacs(tcx):
                        par = tcx % 2
                        pA, pB = pAp[par], pBp[par]
                        e.wait(v, "sm", proj_mm[par])
                        e.wait(v, "sm", tile_free[par])
                        e.inc(v.tensor_scalar(QF[:, par, 0, :], pA[0:64, 0:128], 0.125, None, ALU.mult), "sv")
                        e.inc(v.tensor_scalar(QF[:, par, 1, :], pA[0:64, 128:256], 0.125, None, ALU.mult), "sv")
                        e.inc(v.tensor_copy(KT[:, par, :], pA[:, 256:384]), "sv")
                        sv_cl = e.inc(v.tensor_scalar(PT[:, par, :], pA[:, 384:512], -32.0, 32.0,
                                                      ALU.max, ALU.min), "sv")
                        e.inc(v.tensor_copy(VT[:, par, :], pB[:, 0:256]), "sv")
                        ev_b[par] = e.inc(v.tensor_copy(GT[:, par, :], pB[:, 256:512]), "sv")
                        e.wait(s, "sv", ev_b[par])
                        sgt_c = e.inc(s.activation(SGT[:, :], GT[:, par, :], AF.Tanh, scale=0.5), "sa")
                        pt_c[par] = e.inc(s.activation(PT[:, par, :], PT[:, par, :], AF.Exp,
                                                       bias=CONSTB[:, 0:1]), "sa")
                        e.wait(v, "sa", sgt_c)
                        e.inc(v.tensor_scalar(SG[:, par, :], SGT[:, :], 0.5, 0.5, ALU.mult, ALU.add), "sv")
                        e.inc(v.tensor_tensor(SG[:, par, :], SG[:, par, :], GT[:, par, :], ALU.mult), "sv")

                    def emit_ofinal(tc):
                        nonlocal of_guard, of_c_prev
                        if tc < 0:
                            return
                        par = tc % 2
                        pA = pAp[par]
                        blk, sub = tc // 8, tc % 8
                        e.wait(t, "sv", ov_free_arr[par])
                        t.matmul(pF[:, 256:384], ON[:, par, 0:128], IDENT[:, :], start=False, stop=True,
                                 is_transpose=True)
                        otr = e.inc(t.matmul(pF[:, 384:512], ON[:, par, 128:256], IDENT[:, :], start=False,
                                             stop=True, is_transpose=True), "sm")
                        e.wait(v, "sm", otr)
                        e.wait(v, "sg", of_guard)
                        of_c = e.inc(v.tensor_copy(OF[:, :], pF[:, 256:512]), "sv")
                        of_c_prev = of_c
                        e.wait(g, "sv", of_c)
                        for h in range(2):
                            of_guard = e.inc(g.dma_start(
                                out=a2a_in.ap()[256 * blk + 128 * h:256 * blk + 128 * (h + 1),
                                                128 * sub:128 * (sub + 1)],
                                in_=OF[:, 128 * h:128 * (h + 1)]), "sg", 16)

                    def emit_tail(tc, projit):
                        nonlocal of_guard, st_upd_prev, d0_ev_prev, sm_ev_prev, w2_ev_prev, of_c_prev
                        par = tc % 2
                        pA = pAp[par]
                        blk, sub = tc // 8, tc % 8

                        def pj(n=1):
                            for _ in range(n):
                                try:
                                    next(projit)
                                except StopIteration:
                                    return
                        if tc % 32 == 0:
                            e.wait(v, "sm", st_guard_mm[0])
                            e.inc(v.memset(HST[:, :, :], 0.0), "sv")
                            e.inc(v.memset(HVST[:, :, :], 0.0), "sv")
                            sv_c = e.inc(v.memset(D0R[:, :], 0.0), "sv")
                            e.wait(t, "sv", sv_c)
                        pj(2)
                        e.wait(t, "sv", st_upd_prev)
                        e.wait(t, "sa", d0_ev_prev)
                        e.wait(t, "sv", ev_b[par])
                        e.wait(t, "sa", pt_c[par])
                        t.matmul(pA[0:64, 0:128], KT[:, par, 0:64], IDENT[:, :], start=True, stop=True,
                                 is_transpose=True)
                        t.matmul(pA[0:64, 128:256], KT[:, par, 64:128], IDENT[:, :], start=False, stop=True,
                                 is_transpose=True)
                        t.matmul(pA[0:64, 256:384], PT[:, par, 0:64], IDENT[:, :], start=False, stop=True,
                                 is_transpose=True)
                        tr_mm = e.inc(t.matmul(pA[0:64, 384:512], PT[:, par, 64:128], IDENT[:, :],
                                               start=False, stop=True, is_transpose=True), "sm")
                        e.wait(v, "sm", tr_mm)
                        e.inc(v.tensor_copy(KF[:, 0, :], pA[0:64, 0:128]), "sv")
                        e.inc(v.tensor_copy(KF[:, 1, :], pA[0:64, 128:256]), "sv")
                        e.inc(v.tensor_copy(PF[:, 0, :], pA[0:64, 256:384]), "sv")
                        tr_ev = e.inc(v.tensor_copy(PF[:, 1, :], pA[0:64, 384:512]), "sv")
                        emit_ofinal(tc - 1)
                        pj(2)
                        e.wait(t, "sv", sm_ev_prev)
                        e.wait(t, "sv", tr_ev)
                        for h in range(2):
                            hs = slice(64 * h, 64 * (h + 1))
                            t.matmul(pC[:, hs], TRIU[:, :], PT[:, par, hs], start=(h == 0), stop=False)
                            t.matmul(pC[:, hs], ONESR[0:1, 0:128], D0R[0:1, hs], start=False, stop=True)
                        for h in range(2):
                            hs = slice(64 * h, 64 * (h + 1))
                            d0s = slice(128 + 64 * h, 128 + 64 * (h + 1))
                            t.matmul(pC[0:1, d0s], ONESC[:, 0:1], PT[:, par, hs], start=False, stop=False)
                            mm = t.matmul(pC[0:1, d0s], ONESR[0:1, 0:1], D0R[0:1, hs], start=False, stop=True)
                        cumd_mm = e.inc(mm, "sm")
                        e.wait(v, "sm", cumd_mm)
                        e.inc(v.tensor_copy(CD[:, :], pC[:, 0:128]), "sv")
                        rd_ev = e.inc(v.reciprocal(RD[:, :], CD[:, :]), "sv")
                        e.wait(s, "sm", cumd_mm)
                        e.wait(s, "sv", rd_ev)
                        d0_ev = e.inc(s.copy(D0R[0:1, :], pC[0:1, 128:256]), "sa")
                        pj(3)
                        e.wait(t, "sv", rd_ev)
                        e.wait(t, "sa", d0_ev)
                        for h in range(2):
                            mm = t.matmul(pC[:, 256 + 128 * h:256 + 128 * (h + 1)],
                                          KF[:, h, :], QF[:, par, h, :], start=False, stop=True)
                        st_mm = e.inc(mm, "sm")
                        e.wait(v, "sm", st_mm)
                        sm_ev = e.inc(v.tensor_tensor(SM[:, :], pC[:, 256:512], TRIU2[:, :], ALU.mult), "sv")
                        sm_ev_prev = sm_ev
                        pj(2)
                        e.wait(t, "sv", w2_ev_prev)
                        e.wait(t, "sv", sm_ev)
                        for h in range(2):
                            hs = slice(64 * h, 64 * (h + 1))
                            t.matmul(pD[:, hs], QF[:, par, h, :], HST[:, h, :], start=(h == 0), stop=False)
                            mm = t.matmul(pD[:, hs], SM[:, 128 * h:128 * (h + 1)], PT[:, par, hs],
                                          start=False, stop=True)
                        ok_mm = e.inc(mm, "sm")
                        for h in range(2):
                            hs = slice(64 * h, 64 * (h + 1))
                            mm = t.matmul(pE[0:64, hs], KT[:, par, hs], PT[:, par, hs],
                                          start=(h == 0), stop=True)
                        h_mm = e.inc(mm, "sm")
                        e.wait(v, "sm", ok_mm)
                        e.inc(v.tensor_tensor(OKN[:, :], pD[:, 0:128], RD[:, :], ALU.mult), "sv")
                        nm_c = e.inc(v.reduce_max(NM[:, :], OKN.rearrange("p (g x) -> p g x", g=2),
                                                  axis=AX, negate=True), "sv")
                        e.wait(s, "sv", nm_c)
                        sa_c = 0
                        for h in range(2):
                            hs = slice(64 * h, 64 * (h + 1))
                            sa_c = e.inc(s.activation(EXPT[:, hs], OKN[:, hs], AF.Exp,
                                                      bias=NM[:, h:h + 1]), "sa")
                        e.wait(v, "sa", sa_c)
                        rs_c = e.inc(v.reduce_sum(RSm[:, :], EXPT.rearrange("p (g x) -> p g x", g=2),
                                                  axis=AX), "sv")
                        e.wait(s, "sv", rs_c)
                        sa_c = e.inc(s.copy(RSW[:, 0:2], RSm[:, :]), "sa")
                        e.wait(v, "sa", sa_c)
                        e.inc(v.reciprocal(RSR[:, :], RSW[:, :]), "sv")
                        e.inc(v.tensor_tensor(PV2.rearrange("p (g x) -> p g x", g=2),
                                              EXPT.rearrange("p (g x) -> p g x", g=2),
                                              RSR[:, 0:2].rearrange("p (g x) -> p g x", x=1)
                                              .broadcast_to((128, 2, 64)), ALU.mult), "sv")
                        pv2_c = e.inc(v.tensor_tensor(PV2[:, :], PV2[:, :], RD[:, :], ALU.mult), "sv")
                        pj(4)
                        e.wait(t, "sv", pv2_c)
                        t.matmul(pA[0:64, 0:128], PV2[:, 0:64], IDENT[:, :], start=True, stop=True,
                                 is_transpose=True)
                        tr2 = e.inc(t.matmul(pA[0:64, 128:256], PV2[:, 64:128], IDENT[:, :],
                                             start=False, stop=True, is_transpose=True), "sm")
                        e.wait(v, "sm", tr2)
                        e.inc(v.tensor_copy(PV2F[:, 0, :], pA[0:64, 0:128]), "sv")
                        pv2f_c = e.inc(v.tensor_copy(PV2F[:, 1, :], pA[0:64, 128:256]), "sv")
                        bank_free[par] = pv2f_c
                        pj(2)
                        e.wait(t, "sv", pv2f_c)
                        for h in range(2):
                            mm = t.matmul(pD[:, 128 + 128 * h:128 + 128 * (h + 1)],
                                          PF[:, h, :], PV2F[:, h, :], start=False, stop=True)
                        w2_mm = e.inc(mm, "sm")
                        e.wait(v, "sm", w2_mm)
                        w2_ev = e.inc(v.tensor_tensor(W2M[:, :], pD[:, 128:384], TRIU2[:, :], ALU.mult), "sv")
                        w2_ev_prev = w2_ev
                        pj(1)
                        e.wait(t, "sv", of_c_prev)
                        e.wait(t, "sv", w2_ev)
                        for h in range(2):
                            vs = slice(128 * h, 128 * (h + 1))
                            t.matmul(pF[:, vs], PV2F[:, h, :], HVST[:, h, :], start=(h == 0), stop=False)
                            mm = t.matmul(pF[:, vs], W2M[:, vs], VT[:, par, vs], start=False, stop=True)
                        ov_mm = e.inc(mm, "sm")
                        for h in range(2):
                            vs = slice(128 + 128 * h, 128 + 128 * (h + 1))
                            mm = t.matmul(pE[0:64, vs], PT[:, par, 64 * h:64 * (h + 1)],
                                          VT[:, par, 128 * h:128 * (h + 1)], start=False, stop=True)
                        hv_mm = e.inc(mm, "sm")
                        tile_free[par] = hv_mm
                        st_guard_mm[0] = hv_mm
                        e.wait(v, "sm", hv_mm)
                        e.inc(v.tensor_tensor(HST[:, :, :], HST[:, :, :], pE[0:64, 0:128], ALU.add), "sv")
                        st_upd = e.inc(v.tensor_tensor(HVST[:, :, :], HVST[:, :, :], pE[0:64, 128:384],
                                                       ALU.add), "sv")
                        st_upd_prev = st_upd
                        e.wait(s, "sm", ov_mm)
                        sq_c = e.inc(s.activation(SQG[:, :], pF[:, 0:256], AF.Square), "sa")
                        e.wait(v, "sa", sq_c)
                        gs_c = e.inc(v.reduce_sum(GSm[:, :], SQG.rearrange("p (g x) -> p g x", g=2),
                                                  axis=AX), "sv")
                        e.wait(s, "sv", gs_c)
                        e.inc(s.activation(MSm[:, :], GSm[:, :], AF.Copy, scale=1.0 / DV, bias=EPS), "sa")
                        e.inc(s.activation(RN[:, :], MSm[:, :], AF.Ln), "sa")
                        rw_c = e.inc(s.activation(RN[:, :], RN[:, :], AF.Exp, scale=-0.5), "sa")
                        e.wait(v, "sa", rw_c)
                        e.inc(v.tensor_tensor(ON[:, par, :].rearrange("p (g x) -> p g x", g=2),
                                              pF[:, 0:256].rearrange("p (g x) -> p g x", g=2),
                                              RN[:, 0:2].rearrange("p (g x) -> p g x", x=1)
                                              .broadcast_to((128, 2, 128)), ALU.mult), "sv")
                        e.inc(v.tensor_tensor(ON[:, par, :], ON[:, par, :], GNB[:, :], ALU.mult), "sv")
                        ov_free_arr[par] = e.inc(v.tensor_tensor(ON[:, par, :], ON[:, par, :], SG[:, par, :], ALU.mult), "sv")
                        pj(8)

                    st_guard_mm = [0]
                    ov_free_arr = [0, 0]
                    emit_xlt(0)
                    g0 = proj_gen(0)
                    for _ in g0:
                        pass
                    emit_evacs(0)
                    for tc in range(NTC):
                        if tc + 1 < NTC:
                            emit_xlt(tc + 1)
                            projit = proj_gen(tc + 1)
                        else:
                            projit = iter(())
                        emit_tail(tc, projit)
                        if tc + 1 < NTC:
                            emit_evacs(tc + 1)
                    emit_ofinal(NTC - 1)
                    e.wait(g, "sg")
                e.barrier()
                if SCAN_LVL < 8:
                    return
                with allow_ici_all_to_all():
                    cc2 = e.inc(g.collective_compute(
                        "AllToAll", ALU.bypass, replica_groups=[list(range(8))],
                        ins=[a2a_in.ap().opt()], outs=[a2a_out.ap().opt()]), "sc")
                e.wait(g, "sc", cc2)
                e.barrier()
                if ATTN_SUB < 3:
                    return
                with ExitStack() as wctx, nc.named_scope(uname("WO")):
                    OFM = wctx.enter_context(nc.sbuf_tensor(uname("OFM"), [128, NCH, TOK], F32))
                    WOB = wctx.enter_context(nc.sbuf_tensor(uname("WOB"), [128, 2, NCH, 128], F32))
                    pW = [wctx.enter_context(nc.psum_tensor(uname(f"pW{i}"), [128, 512], F32)) for i in range(4)]
                    cd = e.inc(q.dma_start(out=OFM[:, :, :],
                                           in_=a2a_out.ap().rearrange("(j p) t -> p j t", p=128)), "ofm", 16)
                    wo_ready = [0, 0]
                    wo_guard = [0, 0]
                    pw_free = [0, 0, 0, 0]
                    for d2 in range(NCH):
                        db = d2 % 2
                        e.wait(q, "sm", wo_guard[db])
                        wo_ready[db] = e.inc(
                            q.dma_start(out=WOB[:, db, :, :],
                                        in_=wo[:, 128 * d2:128 * (d2 + 1)]
                                        .rearrange("(j p) c -> p j c", p=128)), f"wo{db}", 16)
                        e.wait(t, f"wo{db}", wo_ready[db])
                        e.wait(t, "ofm", cd)
                        for th in range(2):
                            pb = 2 * db + th
                            sl = slice(512 * th, 512 * (th + 1))
                            e.wait(t, "sv", pw_free[pb])
                            for jv in range(NCH):
                                mm = t.matmul(pW[pb][:, :], WOB[:, db, jv, :], OFM[:, jv, sl],
                                              start=(jv == 0), stop=(jv == NCH - 1))
                            cm = e.inc(mm, "sm")
                            if th == 1:
                                wo_guard[db] = cm
                            e.wait(v, "sm", cm)
                            iv = v.tensor_tensor(XACC[:, d2, sl], XACC[:, d2, sl], pW[pb][:, :], ALU.add)
                            pw_free[pb] = e.inc(iv, "sv")
                e.barrier()

            if STAGE_FFN1:
                emit_ln_stage()
                emit_ffn()
            if STAGE_ATTN:
                emit_ln_stage()
                emit_attn()
            if STAGE_FFN2:
                emit_ln_stage()
                emit_ffn()

            e.barrier()
            cst = e.inc(q.dma_start(out=y_out.rearrange("(j p) t -> p j t", p=128),
                                    in_=XACC[:, :, :]), "sd")
            if DEBUG_DUMP:
                e.inc(q.dma_start(out=dbg_ai[:, :], in_=a2a_in.ap()[:, :]), "sd")
                e.inc(q.dma_start(out=dbg_ao[:, :], in_=a2a_out.ap()[:, :]), "sd")
            e.wait(q, "sd")

    return nc


_NC_CACHE = {}


def _get_nc():
    key = (STAGE_FFN1, STAGE_ATTN, STAGE_FFN2, ATTN_SUB, SCAN_LVL, DEBUG_DUMP, DEBUG_SCAN)
    if key not in _NC_CACHE:
        _NC_CACHE[key] = build_program()
    return _NC_CACHE[key]


def kernel(L0_attn_hidden_states, L0_attn_Wq, L0_attn_Wk, L0_attn_Wv, L0_attn_Wg,
           L0_attn_Ws, L0_attn_Wo, L0_attn_g_norm_weight, L0_ffn_w_router,
           L0_ffn_w_fc, L0_ffn_w_proj):
    x = np.asarray(L0_attn_hidden_states, np.float32).reshape(8192, 2048)
    wfcT_h = np.ascontiguousarray(
        np.transpose(np.asarray(L0_ffn_w_fc, np.float32), (2, 0, 1)).reshape(2048, 16384))
    wprojT_h = np.ascontiguousarray(
        np.transpose(np.asarray(L0_ffn_w_proj, np.float32), (0, 2, 1)).reshape(16384, 2048))
    wrT_h = np.ascontiguousarray(np.asarray(L0_ffn_w_router, np.float32).T)
    Wq = np.asarray(L0_attn_Wq, np.float32)
    Wk = np.asarray(L0_attn_Wk, np.float32)
    Ws = np.asarray(L0_attn_Ws, np.float32)
    Wv = np.asarray(L0_attn_Wv, np.float32)
    Wg = np.asarray(L0_attn_Wg, np.float32)
    Wo_h = np.ascontiguousarray(np.asarray(L0_attn_Wo, np.float32))
    gn = np.asarray(L0_attn_g_norm_weight, np.float32)
    gnb_h = np.ascontiguousarray(np.broadcast_to(np.tile(gn, 2)[None, :], (128, 256)))
    triu_h = np.triu(np.ones((128, 128), np.float32))
    ident_h = np.eye(128, dtype=np.float32)

    in_maps = []
    for c in range(8):
        in_maps.append({
            "x_in": np.ascontiguousarray(x[1024 * c:1024 * (c + 1), :].T),
            "wfcT": wfcT_h, "wprojT": wprojT_h, "wrT": wrT_h,
            "wq": np.ascontiguousarray(Wq[:, 128 * c:128 * (c + 1)]),
            "wk": np.ascontiguousarray(Wk[:, 128 * c:128 * (c + 1)]),
            "ws": np.ascontiguousarray(Ws[:, 128 * c:128 * (c + 1)]),
            "wv": np.ascontiguousarray(Wv[:, 256 * c:256 * (c + 1)]),
            "wg": np.ascontiguousarray(Wg[:, 256 * c:256 * (c + 1)]),
            "wo": Wo_h, "gnb": gnb_h, "triu_in": triu_h, "triu2_in": np.ascontiguousarray(np.tile(triu_h, (1, 2))), "ident_in": ident_h,
        })

    nc = _get_nc()
    res = run_bass_kernel_spmd(nc, in_maps, core_ids=list(range(8)))
    out = np.empty((8192, 2048), np.float32)
    for c in range(8):
        out[1024 * c:1024 * (c + 1), :] = res.results[c]["y"].T
    return out.reshape(2, 4096, 2048)


# version 12
# speedup vs baseline: 1.0539x; 1.0049x over previous
import numpy as np
from contextlib import ExitStack

import concourse.bass as bass
import concourse.mybir as mybir
from concourse.bass_utils import run_bass_kernel_spmd
from concourse.aot_env import allow_ici_all_to_all

F32 = mybir.dt.float32
AF = mybir.ActivationFunctionType
ALU = mybir.AluOpType
AX = mybir.AxisListType.X

D = 2048
NCH = 16
TOK = 1024
HID2 = 16384
DK, DV, M = 64, 128, 64
NTC = 64
EPS = 1e-5

STAGE_FFN1 = True
STAGE_ATTN = True
STAGE_FFN2 = True
ATTN_SUB = 3
SCAN_LVL = 8
DEBUG_DUMP = False
DEBUG_SCAN = False


class Emit:
    def __init__(self, nc, ctx):
        self.nc = nc
        self.t, self.v, self.s, self.q, self.g = nc.tensor, nc.vector, nc.scalar, nc.sync, nc.gpsimd
        self.sem, self.cnt = {}, {}
        for name in ["sd", "sg", "sm", "sv", "sa", "sc"]:
            self.sem[name] = ctx.enter_context(nc.semaphore(name))
            self.cnt[name] = 0
        for name in ["wf0", "wf1", "wp0", "wp1", "wp2", "at0", "at1", "at2",
                     "xt0", "xt1", "wo0", "wo1", "ofm", "xj0", "xj1", "as0", "as1"]:
            self.sem[name] = ctx.enter_context(nc.semaphore(name))
            self.cnt[name] = 0

    def inc(self, instr, name, amt=1):
        if name in ("sd", "sg") and amt == 1:
            amt = 16
        self.cnt[name] += amt
        instr.then_inc(self.sem[name], amt)
        return self.cnt[name]

    def wait(self, eng, name, value=None):
        v = self.cnt[name] if value is None else value
        if v > 0:
            eng.wait_ge(self.sem[name], v)

    def barrier(self):
        engs = {"sd": self.q, "sg": self.g, "sm": self.t, "sv": self.v, "sa": self.s}
        allsems = list(self.sem.keys())
        for ename, eng in engs.items():
            for sname in allsems:
                if sname != ename:
                    self.wait(eng, sname)


def build_program():
    nc = bass.Bass("TRN2", target_bir_lowering=False)

    x_in = nc.dram_tensor("x_in", [D, TOK], F32, kind="ExternalInput").ap()
    wfcT = nc.dram_tensor("wfcT", [D, HID2], F32, kind="ExternalInput").ap()
    wprojT = nc.dram_tensor("wprojT", [HID2, D], F32, kind="ExternalInput").ap()
    wrT = nc.dram_tensor("wrT", [D, 2], F32, kind="ExternalInput").ap()
    wq = nc.dram_tensor("wq", [D, 128], F32, kind="ExternalInput").ap()
    wk = nc.dram_tensor("wk", [D, 128], F32, kind="ExternalInput").ap()
    ws = nc.dram_tensor("ws", [D, 128], F32, kind="ExternalInput").ap()
    wv = nc.dram_tensor("wv", [D, 256], F32, kind="ExternalInput").ap()
    wg = nc.dram_tensor("wg", [D, 256], F32, kind="ExternalInput").ap()
    wo = nc.dram_tensor("wo", [D, D], F32, kind="ExternalInput").ap()
    gnb = nc.dram_tensor("gnb", [128, 256], F32, kind="ExternalInput").ap()
    triu_in = nc.dram_tensor("triu_in", [128, 128], F32, kind="ExternalInput").ap()
    triu2_in = nc.dram_tensor("triu2_in", [128, 256], F32, kind="ExternalInput").ap()
    ident_in = nc.dram_tensor("ident_in", [128, 128], F32, kind="ExternalInput").ap()

    y_out = nc.dram_tensor("y", [D, TOK], F32, kind="ExternalOutput").ap()
    if DEBUG_DUMP:
        dbg_ai = nc.dram_tensor("dbg_ai", [D, TOK], F32, kind="ExternalOutput").ap()
        dbg_ao = nc.dram_tensor("dbg_ao", [D, TOK], F32, kind="ExternalOutput").ap()
    if DEBUG_SCAN:
        dbg_sc = nc.dram_tensor("dbg_sc", [2, 22, 128, 256], F32, kind="ExternalOutput").ap()

    a_buf = nc.dram_tensor("a_buf", [HID2, TOK], F32).ap()
    ag_in = nc.dram_tensor("ag_in", [D, TOK], F32)
    ag_out = nc.dram_tensor("ag_out", [8 * D, TOK], F32, addr_space="Shared")
    a2a_in = nc.dram_tensor("a2a_in", [D, TOK], F32)
    a2a_out = nc.dram_tensor("a2a_out", [D, TOK], F32)

    with ExitStack() as ctx:
        e = Emit(nc, ctx)
        t, v, s, q, g = e.t, e.v, e.s, e.q, e.g
        _uid = [0]

        def uname(base):
            _uid[0] += 1
            return f"{base}_{_uid[0]}"

        def sb(name, shape):
            return ctx.enter_context(nc.sbuf_tensor(name, shape, F32))

        XACC = sb("XACC", [128, NCH, TOK])
        TRIU = sb("TRIU", [128, 128])
        TRIU2 = sb("TRIU2", [128, 256])
        IDENT = sb("IDENT", [128, 128])
        ONESC = sb("ONESC", [128, 1])
        ONESR = sb("ONESR", [1, 512])
        GNB = sb("GNB", [128, 256])
        MU_BC = sb("MU_BC", [128, TOK])
        R_BC = sb("R_BC", [128, TOK])
        CONSTB = sb("CONSTB", [128, 4])
        e.sq_guard = [0, 0]

        block = ctx.enter_context(nc.Block())

        @block.sync
        def _(_sync):
            e.inc(q.dma_start(out=XACC[:, :, :], in_=x_in.rearrange("(j p) t -> p j t", p=128)), "sd")
            e.inc(q.dma_start(out=TRIU[:, :], in_=triu_in[:, :]), "sd")
            e.inc(q.dma_start(out=TRIU2[:, :], in_=triu2_in[:, :]), "sd")
            e.inc(q.dma_start(out=IDENT[:, :], in_=ident_in[:, :]), "sd")
            e.inc(q.dma_start(out=GNB[:, :], in_=gnb[:, :]), "sd")
            e.wait(v, "sd")
            e.inc(v.memset(ONESC[:, :], 1.0), "sv")
            e.inc(v.memset(ONESR[:, :], 1.0), "sv")
            e.inc(g.memset(CONSTB[:, 0:1], -32.0), "sg", 1)
            e.inc(g.memset(CONSTB[:, 1:2], EPS), "sg", 1)
            e.inc(g.memset(CONSTB[:, 2:3], 1.5), "sg", 1)
            e.barrier()

            def emit_ln_stage():
                with ExitStack() as lctx, nc.named_scope(uname("LN")):
                    pR0 = lctx.enter_context(nc.psum_tensor(uname("pR0"), [1, 512], F32))
                    pR1 = lctx.enter_context(nc.psum_tensor(uname("pR1"), [1, 512], F32))
                    pBC = lctx.enter_context(nc.psum_tensor(uname("pBC"), [128, 512], F32))
                    XSQ = lctx.enter_context(nc.sbuf_tensor(uname("XSQ"), [128, 2, 512], F32))
                    MU = lctx.enter_context(nc.sbuf_tensor(uname("MU"), [1, TOK], F32))
                    RR = lctx.enter_context(nc.sbuf_tensor(uname("RR"), [1, TOK], F32))
                    T0 = lctx.enter_context(nc.sbuf_tensor(uname("T0"), [1, TOK], F32))
                    T1 = lctx.enter_context(nc.sbuf_tensor(uname("T1"), [1, TOK], F32))
                    T2 = lctx.enter_context(nc.sbuf_tensor(uname("T2"), [1, TOK], F32))
                    for vt in range(2):
                        sl = slice(512 * vt, 512 * (vt + 1))
                        for j in range(NCH):
                            if j >= 2:
                                e.wait(s, "sm", e.sq_guard[j % 2])
                            sa_c = e.inc(s.activation(XSQ[:, j % 2, :], XACC[:, j, sl], AF.Square), "sa")
                            t.matmul(pR0[:, :], ONESC[:, 0:1], XACC[:, j, sl],
                                     start=(j == 0), stop=(j == NCH - 1))
                            e.wait(t, "sa", sa_c)
                            mm2 = t.matmul(pR1[:, :], ONESC[:, 0:1], XSQ[:, j % 2, :],
                                           start=(j == 0), stop=(j == NCH - 1))
                            e.sq_guard[j % 2] = e.inc(mm2, "sm")
                        e.wait(s, "sm")
                        s.activation(MU[:, sl], pR0[:, :], AF.Copy, scale=1.0 / D)
                        sa_c = e.inc(s.activation(T0[:, sl], pR1[:, :], AF.Copy, scale=1.0 / D), "sa")
                        e.wait(t, "sa", sa_c)
                    e.wait(v, "sa")
                    e.inc(v.tensor_tensor(T1[:, :], MU[:, :], MU[:, :], ALU.mult), "sv")
                    sv_c = e.inc(v.tensor_tensor(T0[:, :], T0[:, :], T1[:, :], ALU.subtract), "sv")
                    e.wait(s, "sv", sv_c)
                    e.inc(s.activation(T2[:, :], T0[:, :], AF.Copy, bias=EPS), "sa")
                    sa_c = e.inc(s.activation(T1[:, :], T2[:, :], AF.Sqrt), "sa")
                    e.wait(v, "sa", sa_c)
                    e.inc(v.reciprocal(T0[:, :], T1[:, :]), "sv")
                    e.inc(v.tensor_tensor(T1[:, :], T0[:, :], T0[:, :], ALU.mult), "sv")
                    sv_c = e.inc(v.tensor_tensor(T1[:, :], T1[:, :], T2[:, :], ALU.mult), "sv")
                    e.wait(s, "sv", sv_c)
                    sa_c = e.inc(s.activation(T1[:, :], T1[:, :], AF.Copy, scale=-0.5, bias=1.5), "sa")
                    e.wait(v, "sa", sa_c)
                    e.inc(v.tensor_tensor(RR[:, :], T0[:, :], T1[:, :], ALU.mult), "sv")
                    e.wait(t, "sv")
                    for (row, dstbc) in ((MU, MU_BC), (RR, R_BC)):
                        for vt in range(2):
                            sl = slice(512 * vt, 512 * (vt + 1))
                            c1 = e.inc(t.matmul(pBC[:, :], ONESR[0:1, 0:128], row[:, sl],
                                                start=True, stop=True), "sm")
                            e.wait(v, "sm", c1)
                            cv = e.inc(v.tensor_copy(dstbc[:, sl], pBC[:, :]), "sv")
                            e.wait(t, "sv", cv)

            def norm_chunk(dst_ap, j):
                e.inc(v.tensor_tensor(dst_ap, XACC[:, j, :], MU_BC[:, :], ALU.subtract), "sv")
                return e.inc(v.tensor_tensor(dst_ap, dst_ap, R_BC[:, :], ALU.mult), "sv")

            def emit_ffn():
                with ExitStack() as fctx:
                    XL = fctx.enter_context(nc.sbuf_tensor(uname("XL"), [128, NCH, TOK], F32))
                    W0BC = fctx.enter_context(nc.sbuf_tensor(uname("W0BC"), [128, TOK], F32))
                    W1BC = fctx.enter_context(nc.sbuf_tensor(uname("W1BC"), [128, TOK], F32))
                    for j in range(NCH):
                        norm_chunk(XL[:, j, :], j)
                    WRT = fctx.enter_context(nc.sbuf_tensor(uname("WRT"), [128, NCH, 2], F32))
                    WRD = fctx.enter_context(nc.sbuf_tensor(uname("WRD"), [128, NCH], F32))
                    SR0 = fctx.enter_context(nc.sbuf_tensor(uname("SR0"), [1, TOK], F32))
                    SR1 = fctx.enter_context(nc.sbuf_tensor(uname("SR1"), [1, TOK], F32))
                    MR = fctx.enter_context(nc.sbuf_tensor(uname("MR"), [1, TOK], F32))
                    DL = fctx.enter_context(nc.sbuf_tensor(uname("DL"), [1, TOK], F32))
                    W0R = fctx.enter_context(nc.sbuf_tensor(uname("W0R"), [1, TOK], F32))
                    W1R = fctx.enter_context(nc.sbuf_tensor(uname("W1R"), [1, TOK], F32))
                    cd = e.inc(q.dma_start(out=WRT[:, :, :],
                                           in_=wrT.rearrange("(j p) c -> p j c", p=128)), "sd")
                    e.wait(v, "sd", cd)
                    wrd_c = e.inc(v.tensor_tensor(WRD[:, :], WRT[:, :, 1], WRT[:, :, 0], ALU.subtract), "sv")
                    with ExitStack() as actx, nc.named_scope(uname("FFNA")):
                        WF = actx.enter_context(nc.sbuf_tensor(uname("WF"), [128, 2, NCH, 128], F32))
                        ASL = actx.enter_context(nc.sbuf_tensor(uname("ASL"), [128, 2, TOK], F32))
                        AB1 = actx.enter_context(nc.sbuf_tensor(uname("AB1"), [128, 2, 512], F32))
                        TT = actx.enter_context(nc.sbuf_tensor(uname("TT"), [128, 2, 512], F32))
                        pH = [actx.enter_context(nc.psum_tensor(uname(f"pH{i}"), [128, 512], F32)) for i in range(4)]
                        pBC = [actx.enter_context(nc.psum_tensor(uname(f"pBC{i}"), [128, 512], F32)) for i in range(4)]
                        wf_ready = [0, 0]
                        wf_guard = [0, 0]
                        ph_free = [0, 0, 0, 0]
                        ab_guard = [0, 0]
                        asl_guard = [0, 0]
                        chunk_mm = {}

                        def emit_mm(k):
                            kb = k % 2
                            e.wait(q, "sm", wf_guard[kb])
                            wf_ready[kb] = e.inc(
                                q.dma_start(out=WF[:, kb, :, :],
                                            in_=wfcT[:, 128 * k:128 * (k + 1)]
                                            .rearrange("(j p) h -> p j h", p=128)), f"wf{kb}", 16)
                            for vt in range(2):
                                pb = 2 * kb + vt
                                sl = slice(512 * vt, 512 * (vt + 1))
                                e.wait(t, f"wf{kb}", wf_ready[kb])
                                e.wait(t, "sv", ph_free[pb])
                                for j in range(NCH):
                                    mm = t.matmul(pH[pb][:, :], WF[:, kb, j, :], XL[:, j, sl],
                                                  start=(j == 0), stop=(j == NCH - 1))
                                ph_mm = e.inc(mm, "sm")
                                chunk_mm[(k, vt)] = ph_mm
                                if vt == 1:
                                    wf_guard[kb] = ph_mm

                        def emit_evac(k):
                            kb = k % 2
                            wbc = W0BC if k < 64 else W1BC
                            for vt in range(2):
                                pb = 2 * kb + vt
                                sl = slice(512 * vt, 512 * (vt + 1))
                                ph_mm = chunk_mm[(k, vt)]
                                e.wait(s, "sm", ph_mm)
                                e.wait(s, "sv", ab_guard[vt])
                                sa1 = e.inc(s.activation(AB1[:, vt, :], pH[pb][:, :], AF.Abs, scale=0.25), "sa")
                                e.wait(v, "sa", sa1)
                                e.wait(v, f"as{kb}", asl_guard[kb])
                                sv1 = e.inc(v.scalar_tensor_tensor(TT[:, vt, :], pH[pb][:, :], 0.75,
                                                                   AB1[:, vt, :], ALU.mult, ALU.add), "sv")
                                ph_free[pb] = sv1
                                e.wait(s, "sv", sv1)
                                sa2 = e.inc(s.activation(TT[:, vt, :], TT[:, vt, :], AF.Square), "sa")
                                e.wait(v, "sa", sa2)
                                sv2 = e.inc(v.tensor_tensor(ASL[:, kb, sl], TT[:, vt, :], wbc[:, sl],
                                                            ALU.mult), "sv")
                                ab_guard[vt] = sv2
                            e.wait(g, "sv", sv2)
                            asl_guard[kb] = e.inc(
                                g.dma_start(out=a_buf[128 * k:128 * (k + 1), :], in_=ASL[:, kb, :]),
                                f"as{kb}", 16)

                        e.wait(t, "sv", wrd_c)
                        dl_cm = []
                        for vt in range(2):
                            sl = slice(512 * vt, 512 * (vt + 1))
                            for j in range(NCH):
                                mm = t.matmul(pBC[vt][0:1, :], WRD[:, j:j + 1], XL[:, j, sl],
                                              start=(j == 0), stop=(j == NCH - 1))
                            dl_cm.append(e.inc(mm, "sm"))
                        emit_mm(0)
                        emit_mm(1)
                        for vt in range(2):
                            sl = slice(512 * vt, 512 * (vt + 1))
                            e.wait(s, "sm", dl_cm[vt])
                            e.inc(s.copy(DL[:, sl], pBC[vt][0:1, :]), "sa")
                        e.wait(s, "sa")
                        e.inc(s.activation(SR1[:, :], DL[:, :], AF.Sigmoid), "sa")
                        sa_c = e.inc(s.activation(SR0[:, :], DL[:, :], AF.Sigmoid, scale=-1.0), "sa")
                        e.wait(v, "sa", sa_c)
                        e.inc(v.tensor_scalar(MR[:, :], DL[:, :], 0.0, None, ALU.is_gt), "sv")
                        e.inc(v.tensor_tensor(W1R[:, :], MR[:, :], SR1[:, :], ALU.mult), "sv")
                        e.inc(v.tensor_scalar(W1R[:, :], W1R[:, :], 0.5, None, ALU.mult), "sv")
                        e.inc(v.tensor_scalar(MR[:, :], MR[:, :], -0.5, 0.5, ALU.mult, ALU.add), "sv")
                        rt_c = e.inc(v.tensor_tensor(W0R[:, :], MR[:, :], SR0[:, :], ALU.mult), "sv")
                        e.wait(t, "sv", rt_c)
                        bc_specs = [(W0R, W0BC, 0), (W0R, W0BC, 1), (W1R, W1BC, 0), (W1R, W1BC, 1)]
                        bc_cm = []
                        for i, (row, dstbc, vt) in enumerate(bc_specs):
                            sl = slice(512 * vt, 512 * (vt + 1))
                            bc_cm.append(e.inc(t.matmul(pBC[i][:, :], ONESR[0:1, 0:128], row[:, sl],
                                                        start=True, stop=True), "sm"))
                        for i, (row, dstbc, vt) in enumerate(bc_specs):
                            sl = slice(512 * vt, 512 * (vt + 1))
                            e.wait(v, "sm", bc_cm[i])
                            e.inc(v.tensor_copy(dstbc[:, sl], pBC[i][:, :]), "sv")
                        emit_evac(0)
                        emit_evac(1)
                        for k in range(2, 128):
                            emit_mm(k)
                            emit_evac(k)
                    e.barrier()
                    with ExitStack() as bctx, nc.named_scope(uname("FFNB")):
                        WP = bctx.enter_context(nc.sbuf_tensor(uname("WP"), [128, 3, 1024], F32))
                        AT = bctx.enter_context(nc.sbuf_tensor(uname("AT"), [128, 3, 512], F32))
                        pO = [bctx.enter_context(nc.psum_tensor(uname(f"pO{i}"), [128, 512], F32)) for i in range(8)]
                        wp_ready = [0, 0, 0]
                        at_ready = [0, 0, 0]
                        buf_guard = [0, 0, 0]
                        po_free = 0
                        for d2h in range(2):
                            for th in range(2):
                                sl = slice(512 * th, 512 * (th + 1))
                                for k in range(128):
                                    kb = k % 3
                                    e.wait(q, "sm", buf_guard[kb])
                                    wp_ready[kb] = e.inc(
                                        q.dma_start(out=WP[:, kb, :],
                                                    in_=wprojT[128 * k:128 * (k + 1),
                                                               1024 * d2h:1024 * (d2h + 1)]), f"wp{kb}", 16)
                                    e.wait(g, "sm", buf_guard[kb])
                                    at_ready[kb] = e.inc(
                                        g.dma_start(out=AT[:, kb, :],
                                                    in_=a_buf[128 * k:128 * (k + 1), sl]), f"at{kb}", 16)
                                    e.wait(t, f"wp{kb}", wp_ready[kb])
                                    e.wait(t, f"at{kb}", at_ready[kb])
                                    if k == 0:
                                        e.wait(t, "sv", po_free)
                                    for c8 in range(8):
                                        mm = t.matmul(pO[c8][:, :], WP[:, kb, 128 * c8:128 * (c8 + 1)],
                                                      AT[:, kb, :], start=(k == 0), stop=(k == 127))
                                    buf_guard[kb] = e.inc(mm, "sm")
                                e.wait(v, "sm")
                                for c8 in range(8):
                                    j = 8 * d2h + c8
                                    iv = v.tensor_tensor(XACC[:, j, sl], XACC[:, j, sl], pO[c8][:, :], ALU.add)
                                    po_free = e.inc(iv, "sv")
                e.barrier()

            def emit_attn():
                with ExitStack() as nctx, nc.named_scope(uname("LN2AG")):
                    XLJ = nctx.enter_context(nc.sbuf_tensor(uname("XLJ"), [128, 2, TOK], F32))
                    dma_guard = [0, 0]
                    for j in range(NCH):
                        jb = j % 2
                        e.wait(v, f"xj{jb}", dma_guard[jb])
                        cv = norm_chunk(XLJ[:, jb, :], j)
                        e.wait(q, "sv", cv)
                        dma_guard[jb] = e.inc(
                            q.dma_start(out=ag_in.ap()[128 * j:128 * (j + 1), :], in_=XLJ[:, jb, :]), f"xj{jb}", 16)
                    e.wait(g, "xj0")
                    e.wait(g, "xj1")
                cc1 = e.inc(g.collective_compute(
                    "AllGather", ALU.bypass, replica_groups=[list(range(8))],
                    ins=[ag_in.ap().opt()], outs=[ag_out.ap().opt()]), "sc")
                e.wait(g, "sc", cc1)
                e.barrier()
                if ATTN_SUB < 2:
                    return

                with ExitStack() as sctx, nc.named_scope(uname("SCAN")):
                    def ssb(name, shape):
                        return sctx.enter_context(nc.sbuf_tensor(name, shape, F32))
                    WQb, WKb, WSb = ssb("WQb", [128, NCH, 128]), ssb("WKb", [128, NCH, 128]), ssb("WSb", [128, NCH, 128])
                    WVb, WGb = ssb("WVb", [128, NCH, 256]), ssb("WGb", [128, NCH, 256])
                    XLT = ssb("XLT", [128, 2, NCH, 128])
                    QF = ssb(uname("QF"), [64, 2, 2, 128])
                    KT = ssb(uname("KT"), [128, 2, 128])
                    VT = ssb(uname("VT"), [128, 2, 256])
                    GT = ssb(uname("GT"), [128, 2, 256])
                    PT = ssb(uname("PT"), [128, 2, 128])
                    KF = ssb(uname("KF"), [64, 2, 128])
                    PF = ssb(uname("PF"), [64, 2, 128])
                    SM, W2M = ssb(uname("SM"), [128, 256]), ssb(uname("W2M"), [128, 256])
                    OKN, EXPT = ssb(uname("OKN"), [128, 128]), ssb(uname("EXPT"), [128, 128])
                    PV2 = ssb(uname("PV2"), [128, 128])
                    PV2F = ssb(uname("PV2F"), [64, 2, 128])
                    CD, RD = ssb(uname("CD"), [128, 128]), ssb(uname("RD"), [128, 128])
                    ON = ssb(uname("ON"), [128, 2, 256])
                    OF = ssb(uname("OF"), [128, 256])
                    SQG = ssb(uname("SQG"), [128, 256])
                    SGT = ssb(uname("SGT"), [128, 256])
                    SG = ssb(uname("SG"), [128, 2, 256])
                    NM, RSm = ssb(uname("NM"), [128, 2]), ssb(uname("RSm"), [128, 2])
                    GSm, MSm = ssb(uname("GSm"), [128, 2]), ssb(uname("MSm"), [128, 2])
                    RN = ssb(uname("RN"), [128, 2])
                    RSW, RSR = ssb(uname("RSW"), [128, 64]), ssb(uname("RSR"), [128, 64])
                    HST = ssb(uname("HST"), [64, 2, 64])
                    HVST = ssb(uname("HVST"), [64, 2, 128])
                    D0R = ssb(uname("D0R"), [1, 128])
                    pAp = [sctx.enter_context(nc.psum_tensor(uname(f"pA{i}"), [128, 512], F32)) for i in range(2)]
                    pBp = [sctx.enter_context(nc.psum_tensor(uname(f"pB{i}"), [128, 512], F32)) for i in range(2)]
                    pC = sctx.enter_context(nc.psum_tensor(uname("pC"), [128, 512], F32))
                    pD = sctx.enter_context(nc.psum_tensor(uname("pD"), [128, 512], F32))
                    pE = sctx.enter_context(nc.psum_tensor(uname("pE"), [128, 512], F32))
                    pF = sctx.enter_context(nc.psum_tensor(uname("pF"), [128, 512], F32))
                    for (buf, wsrc) in ((WQb, wq), (WKb, wk), (WSb, ws), (WVb, wv), (WGb, wg)):
                        e.inc(q.dma_start(out=buf[:, :, :], in_=wsrc.rearrange("(j p) c -> p j c", p=128)), "sd")
                    e.wait(v, "sd")
                    e.inc(v.memset(RSW[:, :], 1.0), "sv")
                    xlt_ready = [0, 0]
                    proj_mm = [0, 0]
                    ev_b = [0, 0]
                    pt_c = [0, 0]
                    bank_free = [0, 0]
                    tile_free = [0, 0]
                    of_guard = 0
                    st_upd_prev = 0
                    d0_ev_prev = 0
                    sm_ev_prev = 0
                    w2_ev_prev = 0
                    of_c_prev = 0

                    def emit_xlt(tcx):
                        cbx = tcx % 2
                        blkx, subx = tcx // 8, tcx % 8
                        e.wait(q, "sm", proj_mm[cbx])
                        xlt_ready[cbx] = e.inc(
                            q.dma_start(out=XLT[:, cbx, :, :],
                                        in_=ag_out.ap()[2048 * blkx:2048 * (blkx + 1),
                                                        128 * subx:128 * (subx + 1)]
                                        .rearrange("(j p) t -> p j t", p=128)), f"xt{cbx}", 16)

                    def proj_gen(tcx):
                        par = tcx % 2
                        pA, pB = pAp[par], pBp[par]
                        e.wait(t, f"xt{par}", xlt_ready[par])
                        e.wait(t, "sv", bank_free[par])
                        for j in range(NCH):
                            st0 = (j == 0)
                            lst = (j == NCH - 1)
                            t.matmul(pA[:, 256:384], XLT[:, par, j, :], WKb[:, j, :], start=st0, stop=lst)
                            t.matmul(pA[:, 384:512], XLT[:, par, j, :], WSb[:, j, :], start=False, stop=lst)
                            t.matmul(pA[0:64, 0:128], WQb[:, j, 0:64], XLT[:, par, j, :], start=False, stop=lst)
                            t.matmul(pA[0:64, 128:256], WQb[:, j, 64:128], XLT[:, par, j, :], start=False, stop=lst)
                            t.matmul(pB[:, 0:256], XLT[:, par, j, :], WVb[:, j, :], start=st0, stop=lst)
                            mm = t.matmul(pB[:, 256:512], XLT[:, par, j, :], WGb[:, j, :], start=False, stop=lst)
                            if lst:
                                proj_mm[par] = e.inc(mm, "sm")
                            yield

                    def emit_evacs(tcx):
                        par = tcx % 2
                        pA, pB = pAp[par], pBp[par]
                        e.wait(v, "sm", proj_mm[par])
                        e.wait(v, "sm", tile_free[par])
                        e.inc(v.tensor_scalar(QF[:, par, 0, :], pA[0:64, 0:128], 0.125, None, ALU.mult), "sv")
                        e.inc(v.tensor_scalar(QF[:, par, 1, :], pA[0:64, 128:256], 0.125, None, ALU.mult), "sv")
                        e.inc(v.tensor_copy(KT[:, par, :], pA[:, 256:384]), "sv")
                        sv_cl = e.inc(v.tensor_scalar(PT[:, par, :], pA[:, 384:512], -32.0, 32.0,
                                                      ALU.max, ALU.min), "sv")
                        e.inc(v.tensor_copy(VT[:, par, :], pB[:, 0:256]), "sv")
                        ev_b[par] = e.inc(v.tensor_copy(GT[:, par, :], pB[:, 256:512]), "sv")
                        e.wait(s, "sv", ev_b[par])
                        sgt_c = e.inc(s.activation(SGT[:, :], GT[:, par, :], AF.Tanh, scale=0.5), "sa")
                        pt_c[par] = e.inc(s.activation(PT[:, par, :], PT[:, par, :], AF.Exp,
                                                       bias=CONSTB[:, 0:1]), "sa")
                        e.wait(v, "sa", sgt_c)
                        e.inc(v.tensor_scalar(SG[:, par, :], SGT[:, :], 0.5, 0.5, ALU.mult, ALU.add), "sv")
                        e.inc(v.tensor_tensor(SG[:, par, :], SG[:, par, :], GT[:, par, :], ALU.mult), "sv")

                    def emit_ofinal(tc):
                        nonlocal of_guard, of_c_prev
                        if tc < 0:
                            return
                        par = tc % 2
                        pA = pAp[par]
                        blk, sub = tc // 8, tc % 8
                        e.wait(t, "sv", ov_free_arr[par])
                        t.matmul(pF[:, 256:384], ON[:, par, 0:128], IDENT[:, :], start=False, stop=True,
                                 is_transpose=True)
                        otr = e.inc(t.matmul(pF[:, 384:512], ON[:, par, 128:256], IDENT[:, :], start=False,
                                             stop=True, is_transpose=True), "sm")
                        e.wait(v, "sm", otr)
                        e.wait(v, "sg", of_guard)
                        of_c = e.inc(v.tensor_copy(OF[:, :], pF[:, 256:512]), "sv")
                        of_c_prev = of_c
                        e.wait(g, "sv", of_c)
                        for h in range(2):
                            of_guard = e.inc(g.dma_start(
                                out=a2a_in.ap()[256 * blk + 128 * h:256 * blk + 128 * (h + 1),
                                                128 * sub:128 * (sub + 1)],
                                in_=OF[:, 128 * h:128 * (h + 1)]), "sg", 16)

                    def emit_tail(tc, projit):
                        nonlocal of_guard, st_upd_prev, d0_ev_prev, sm_ev_prev, w2_ev_prev, of_c_prev
                        par = tc % 2
                        pA = pAp[par]
                        blk, sub = tc // 8, tc % 8

                        def pj(n=1):
                            for _ in range(n):
                                try:
                                    next(projit)
                                except StopIteration:
                                    return
                        if tc % 32 == 0:
                            e.wait(v, "sm", st_guard_mm[0])
                            e.inc(v.memset(HST[:, :, :], 0.0), "sv")
                            e.inc(v.memset(HVST[:, :, :], 0.0), "sv")
                            sv_c = e.inc(v.memset(D0R[:, :], 0.0), "sv")
                            e.wait(t, "sv", sv_c)
                        pj(2)
                        e.wait(t, "sv", st_upd_prev)
                        e.wait(t, "sa", d0_ev_prev)
                        e.wait(t, "sv", ev_b[par])
                        e.wait(t, "sa", pt_c[par])
                        t.matmul(pA[0:64, 0:128], KT[:, par, 0:64], IDENT[:, :], start=True, stop=True,
                                 is_transpose=True)
                        t.matmul(pA[0:64, 128:256], KT[:, par, 64:128], IDENT[:, :], start=False, stop=True,
                                 is_transpose=True)
                        t.matmul(pA[0:64, 256:384], PT[:, par, 0:64], IDENT[:, :], start=False, stop=True,
                                 is_transpose=True)
                        tr_mm = e.inc(t.matmul(pA[0:64, 384:512], PT[:, par, 64:128], IDENT[:, :],
                                               start=False, stop=True, is_transpose=True), "sm")
                        e.wait(v, "sm", tr_mm)
                        e.inc(v.tensor_copy(KF[:, 0, :], pA[0:64, 0:128]), "sv")
                        e.inc(v.tensor_copy(KF[:, 1, :], pA[0:64, 128:256]), "sv")
                        e.inc(v.tensor_copy(PF[:, 0, :], pA[0:64, 256:384]), "sv")
                        tr_ev = e.inc(v.tensor_copy(PF[:, 1, :], pA[0:64, 384:512]), "sv")
                        emit_ofinal(tc - 1)
                        pj(2)
                        e.wait(t, "sv", sm_ev_prev)
                        e.wait(t, "sv", tr_ev)
                        for h in range(2):
                            hs = slice(64 * h, 64 * (h + 1))
                            t.matmul(pC[:, hs], TRIU[:, :], PT[:, par, hs], start=(h == 0), stop=False)
                            t.matmul(pC[:, hs], ONESR[0:1, 0:128], D0R[0:1, hs], start=False, stop=True)
                        for h in range(2):
                            hs = slice(64 * h, 64 * (h + 1))
                            d0s = slice(128 + 64 * h, 128 + 64 * (h + 1))
                            t.matmul(pC[0:1, d0s], ONESC[:, 0:1], PT[:, par, hs], start=False, stop=False)
                            mm = t.matmul(pC[0:1, d0s], ONESR[0:1, 0:1], D0R[0:1, hs], start=False, stop=True)
                        cumd_mm = e.inc(mm, "sm")
                        e.wait(v, "sm", cumd_mm)
                        e.inc(v.tensor_copy(CD[:, :], pC[:, 0:128]), "sv")
                        rd_ev = e.inc(v.reciprocal(RD[:, :], CD[:, :]), "sv")
                        e.wait(s, "sm", cumd_mm)
                        e.wait(s, "sv", rd_ev)
                        d0_ev = e.inc(s.copy(D0R[0:1, :], pC[0:1, 128:256]), "sa")
                        pj(3)
                        e.wait(t, "sv", rd_ev)
                        e.wait(t, "sa", d0_ev)
                        for h in range(2):
                            mm = t.matmul(pC[:, 256 + 128 * h:256 + 128 * (h + 1)],
                                          KF[:, h, :], QF[:, par, h, :], start=False, stop=True)
                        st_mm = e.inc(mm, "sm")
                        e.wait(v, "sm", st_mm)
                        sm_ev = e.inc(v.tensor_tensor(SM[:, :], pC[:, 256:512], TRIU2[:, :], ALU.mult), "sv")
                        sm_ev_prev = sm_ev
                        pj(2)
                        e.wait(t, "sv", w2_ev_prev)
                        e.wait(t, "sv", sm_ev)
                        for h in range(2):
                            hs = slice(64 * h, 64 * (h + 1))
                            t.matmul(pD[:, hs], QF[:, par, h, :], HST[:, h, :], start=(h == 0), stop=False)
                            mm = t.matmul(pD[:, hs], SM[:, 128 * h:128 * (h + 1)], PT[:, par, hs],
                                          start=False, stop=True)
                        ok_mm = e.inc(mm, "sm")
                        for h in range(2):
                            hs = slice(64 * h, 64 * (h + 1))
                            mm = t.matmul(pE[0:64, hs], KT[:, par, hs], PT[:, par, hs],
                                          start=(h == 0), stop=True)
                        h_mm = e.inc(mm, "sm")
                        e.wait(v, "sm", ok_mm)
                        e.inc(v.tensor_tensor(OKN[:, :], pD[:, 0:128], RD[:, :], ALU.mult), "sv")
                        nm_c = e.inc(v.reduce_max(NM[:, :], OKN.rearrange("p (g x) -> p g x", g=2),
                                                  axis=AX, negate=True), "sv")
                        e.wait(s, "sv", nm_c)
                        sa_c = 0
                        for h in range(2):
                            hs = slice(64 * h, 64 * (h + 1))
                            sa_c = e.inc(s.activation(EXPT[:, hs], OKN[:, hs], AF.Exp,
                                                      bias=NM[:, h:h + 1]), "sa")
                        e.wait(v, "sa", sa_c)
                        rs_c = e.inc(v.reduce_sum(RSm[:, :], EXPT.rearrange("p (g x) -> p g x", g=2),
                                                  axis=AX), "sv")
                        e.wait(s, "sv", rs_c)
                        sa_c = e.inc(s.copy(RSW[:, 0:2], RSm[:, :]), "sa")
                        e.wait(v, "sa", sa_c)
                        e.inc(v.reciprocal(RSR[:, :], RSW[:, :]), "sv")
                        e.inc(v.tensor_tensor(PV2.rearrange("p (g x) -> p g x", g=2),
                                              EXPT.rearrange("p (g x) -> p g x", g=2),
                                              RSR[:, 0:2].rearrange("p (g x) -> p g x", x=1)
                                              .broadcast_to((128, 2, 64)), ALU.mult), "sv")
                        pv2_c = e.inc(v.tensor_tensor(PV2[:, :], PV2[:, :], RD[:, :], ALU.mult), "sv")
                        pj(4)
                        e.wait(t, "sv", pv2_c)
                        t.matmul(pA[0:64, 0:128], PV2[:, 0:64], IDENT[:, :], start=True, stop=True,
                                 is_transpose=True)
                        tr2 = e.inc(t.matmul(pA[0:64, 128:256], PV2[:, 64:128], IDENT[:, :],
                                             start=False, stop=True, is_transpose=True), "sm")
                        e.wait(v, "sm", tr2)
                        e.inc(v.tensor_copy(PV2F[:, 0, :], pA[0:64, 0:128]), "sv")
                        pv2f_c = e.inc(v.tensor_copy(PV2F[:, 1, :], pA[0:64, 128:256]), "sv")
                        bank_free[par] = pv2f_c
                        pj(2)
                        e.wait(t, "sv", pv2f_c)
                        for h in range(2):
                            mm = t.matmul(pD[:, 128 + 128 * h:128 + 128 * (h + 1)],
                                          PF[:, h, :], PV2F[:, h, :], start=False, stop=True)
                        w2_mm = e.inc(mm, "sm")
                        e.wait(v, "sm", w2_mm)
                        w2_ev = e.inc(v.tensor_tensor(W2M[:, :], pD[:, 128:384], TRIU2[:, :], ALU.mult), "sv")
                        w2_ev_prev = w2_ev
                        pj(1)
                        e.wait(t, "sv", of_c_prev)
                        e.wait(t, "sv", w2_ev)
                        for h in range(2):
                            vs = slice(128 * h, 128 * (h + 1))
                            t.matmul(pF[:, vs], PV2F[:, h, :], HVST[:, h, :], start=(h == 0), stop=False)
                            mm = t.matmul(pF[:, vs], W2M[:, vs], VT[:, par, vs], start=False, stop=True)
                        ov_mm = e.inc(mm, "sm")
                        for h in range(2):
                            vs = slice(128 + 128 * h, 128 + 128 * (h + 1))
                            mm = t.matmul(pE[0:64, vs], PT[:, par, 64 * h:64 * (h + 1)],
                                          VT[:, par, 128 * h:128 * (h + 1)], start=False, stop=True)
                        hv_mm = e.inc(mm, "sm")
                        tile_free[par] = hv_mm
                        st_guard_mm[0] = hv_mm
                        e.wait(v, "sm", hv_mm)
                        e.inc(v.tensor_tensor(HST[:, :, :], HST[:, :, :], pE[0:64, 0:128], ALU.add), "sv")
                        st_upd = e.inc(v.tensor_tensor(HVST[:, :, :], HVST[:, :, :], pE[0:64, 128:384],
                                                       ALU.add), "sv")
                        st_upd_prev = st_upd
                        e.wait(s, "sm", ov_mm)
                        sq_c = e.inc(s.activation(SQG[:, :], pF[:, 0:256], AF.Square), "sa")
                        e.wait(v, "sa", sq_c)
                        gs_c = e.inc(v.reduce_sum(GSm[:, :], SQG.rearrange("p (g x) -> p g x", g=2),
                                                  axis=AX), "sv")
                        e.wait(s, "sv", gs_c)
                        e.inc(s.activation(MSm[:, :], GSm[:, :], AF.Copy, scale=1.0 / DV, bias=EPS), "sa")
                        e.inc(s.activation(RN[:, :], MSm[:, :], AF.Ln), "sa")
                        rw_c = e.inc(s.activation(RN[:, :], RN[:, :], AF.Exp, scale=-0.5), "sa")
                        e.wait(v, "sa", rw_c)
                        e.inc(v.tensor_tensor(ON[:, par, :].rearrange("p (g x) -> p g x", g=2),
                                              pF[:, 0:256].rearrange("p (g x) -> p g x", g=2),
                                              RN[:, 0:2].rearrange("p (g x) -> p g x", x=1)
                                              .broadcast_to((128, 2, 128)), ALU.mult), "sv")
                        e.inc(v.tensor_tensor(ON[:, par, :], ON[:, par, :], GNB[:, :], ALU.mult), "sv")
                        ov_free_arr[par] = e.inc(v.tensor_tensor(ON[:, par, :], ON[:, par, :], SG[:, par, :], ALU.mult), "sv")
                        pj(8)

                    st_guard_mm = [0]
                    ov_free_arr = [0, 0]
                    emit_xlt(0)
                    g0 = proj_gen(0)
                    for _ in g0:
                        pass
                    emit_evacs(0)
                    for tc in range(NTC):
                        if tc + 1 < NTC:
                            emit_xlt(tc + 1)
                            projit = proj_gen(tc + 1)
                        else:
                            projit = iter(())
                        emit_tail(tc, projit)
                        if tc + 1 < NTC:
                            emit_evacs(tc + 1)
                    emit_ofinal(NTC - 1)
                    e.wait(g, "sg")
                e.barrier()
                if SCAN_LVL < 8:
                    return
                with allow_ici_all_to_all():
                    cc2 = e.inc(g.collective_compute(
                        "AllToAll", ALU.bypass, replica_groups=[list(range(8))],
                        ins=[a2a_in.ap().opt()], outs=[a2a_out.ap().opt()]), "sc")
                e.wait(g, "sc", cc2)
                e.barrier()
                if ATTN_SUB < 3:
                    return
                with ExitStack() as wctx, nc.named_scope(uname("WO")):
                    OFM = wctx.enter_context(nc.sbuf_tensor(uname("OFM"), [128, NCH, TOK], F32))
                    WOB = wctx.enter_context(nc.sbuf_tensor(uname("WOB"), [128, 2, NCH, 128], F32))
                    pW = [wctx.enter_context(nc.psum_tensor(uname(f"pW{i}"), [128, 512], F32)) for i in range(4)]
                    cd = e.inc(q.dma_start(out=OFM[:, :, :],
                                           in_=a2a_out.ap().rearrange("(j p) t -> p j t", p=128)), "ofm", 16)
                    wo_ready = [0, 0]
                    wo_guard = [0, 0]
                    pw_free = [0, 0, 0, 0]
                    for d2 in range(NCH):
                        db = d2 % 2
                        e.wait(q, "sm", wo_guard[db])
                        wo_ready[db] = e.inc(
                            q.dma_start(out=WOB[:, db, :, :],
                                        in_=wo[:, 128 * d2:128 * (d2 + 1)]
                                        .rearrange("(j p) c -> p j c", p=128)), f"wo{db}", 16)
                        e.wait(t, f"wo{db}", wo_ready[db])
                        e.wait(t, "ofm", cd)
                        for th in range(2):
                            pb = 2 * db + th
                            sl = slice(512 * th, 512 * (th + 1))
                            e.wait(t, "sv", pw_free[pb])
                            for jv in range(NCH):
                                mm = t.matmul(pW[pb][:, :], WOB[:, db, jv, :], OFM[:, jv, sl],
                                              start=(jv == 0), stop=(jv == NCH - 1))
                            cm = e.inc(mm, "sm")
                            if th == 1:
                                wo_guard[db] = cm
                            e.wait(v, "sm", cm)
                            iv = v.tensor_tensor(XACC[:, d2, sl], XACC[:, d2, sl], pW[pb][:, :], ALU.add)
                            pw_free[pb] = e.inc(iv, "sv")
                e.barrier()

            if STAGE_FFN1:
                emit_ln_stage()
                emit_ffn()
            if STAGE_ATTN:
                emit_ln_stage()
                emit_attn()
            if STAGE_FFN2:
                emit_ln_stage()
                emit_ffn()

            e.barrier()
            cst = e.inc(q.dma_start(out=y_out.rearrange("(j p) t -> p j t", p=128),
                                    in_=XACC[:, :, :]), "sd")
            if DEBUG_DUMP:
                e.inc(q.dma_start(out=dbg_ai[:, :], in_=a2a_in.ap()[:, :]), "sd")
                e.inc(q.dma_start(out=dbg_ao[:, :], in_=a2a_out.ap()[:, :]), "sd")
            e.wait(q, "sd")

    return nc


_NC_CACHE = {}


def _get_nc():
    key = (STAGE_FFN1, STAGE_ATTN, STAGE_FFN2, ATTN_SUB, SCAN_LVL, DEBUG_DUMP, DEBUG_SCAN)
    if key not in _NC_CACHE:
        _NC_CACHE[key] = build_program()
    return _NC_CACHE[key]


def kernel(L0_attn_hidden_states, L0_attn_Wq, L0_attn_Wk, L0_attn_Wv, L0_attn_Wg,
           L0_attn_Ws, L0_attn_Wo, L0_attn_g_norm_weight, L0_ffn_w_router,
           L0_ffn_w_fc, L0_ffn_w_proj):
    x = np.asarray(L0_attn_hidden_states, np.float32).reshape(8192, 2048)
    wfcT_h = np.ascontiguousarray(
        np.transpose(np.asarray(L0_ffn_w_fc, np.float32), (2, 0, 1)).reshape(2048, 16384))
    wprojT_h = np.ascontiguousarray(
        np.transpose(np.asarray(L0_ffn_w_proj, np.float32), (0, 2, 1)).reshape(16384, 2048))
    wrT_h = np.ascontiguousarray(np.asarray(L0_ffn_w_router, np.float32).T)
    Wq = np.asarray(L0_attn_Wq, np.float32)
    Wk = np.asarray(L0_attn_Wk, np.float32)
    Ws = np.asarray(L0_attn_Ws, np.float32)
    Wv = np.asarray(L0_attn_Wv, np.float32)
    Wg = np.asarray(L0_attn_Wg, np.float32)
    Wo_h = np.ascontiguousarray(np.asarray(L0_attn_Wo, np.float32))
    gn = np.asarray(L0_attn_g_norm_weight, np.float32)
    gnb_h = np.ascontiguousarray(np.broadcast_to(np.tile(gn, 2)[None, :], (128, 256)))
    triu_h = np.triu(np.ones((128, 128), np.float32))
    ident_h = np.eye(128, dtype=np.float32)

    in_maps = []
    for c in range(8):
        in_maps.append({
            "x_in": np.ascontiguousarray(x[1024 * c:1024 * (c + 1), :].T),
            "wfcT": wfcT_h, "wprojT": wprojT_h, "wrT": wrT_h,
            "wq": np.ascontiguousarray(Wq[:, 128 * c:128 * (c + 1)]),
            "wk": np.ascontiguousarray(Wk[:, 128 * c:128 * (c + 1)]),
            "ws": np.ascontiguousarray(Ws[:, 128 * c:128 * (c + 1)]),
            "wv": np.ascontiguousarray(Wv[:, 256 * c:256 * (c + 1)]),
            "wg": np.ascontiguousarray(Wg[:, 256 * c:256 * (c + 1)]),
            "wo": Wo_h, "gnb": gnb_h, "triu_in": triu_h, "triu2_in": np.ascontiguousarray(np.tile(triu_h, (1, 2))), "ident_in": ident_h,
        })

    nc = _get_nc()
    res = run_bass_kernel_spmd(nc, in_maps, core_ids=list(range(8)))
    out = np.empty((8192, 2048), np.float32)
    for c in range(8):
        out[1024 * c:1024 * (c + 1), :] = res.results[c]["y"].T
    return out.reshape(2, 4096, 2048)


# version 13 (final)
# speedup vs baseline: 1.0617x; 1.0074x over previous
import numpy as np
from contextlib import ExitStack

import concourse.bass as bass
import concourse.mybir as mybir
from concourse.bass_utils import run_bass_kernel_spmd
from concourse.aot_env import allow_ici_all_to_all

F32 = mybir.dt.float32
AF = mybir.ActivationFunctionType
ALU = mybir.AluOpType
AX = mybir.AxisListType.X

D = 2048
NCH = 16
TOK = 1024
HID2 = 16384
DK, DV, M = 64, 128, 64
NTC = 64
EPS = 1e-5

STAGE_FFN1 = True
STAGE_ATTN = True
STAGE_FFN2 = True
ATTN_SUB = 3
SCAN_LVL = 8
DEBUG_DUMP = False
DEBUG_SCAN = False


class Emit:
    def __init__(self, nc, ctx):
        self.nc = nc
        self.t, self.v, self.s, self.q, self.g = nc.tensor, nc.vector, nc.scalar, nc.sync, nc.gpsimd
        self.sem, self.cnt = {}, {}
        for name in ["sd", "sg", "sm", "sv", "sa", "sc"]:
            self.sem[name] = ctx.enter_context(nc.semaphore(name))
            self.cnt[name] = 0
        for name in ["wf0", "wf1", "wp0", "wp1", "wp2", "at0", "at1", "at2",
                     "xt0", "xt1", "wo0", "wo1", "ofm", "xj0", "xj1", "as0", "as1"]:
            self.sem[name] = ctx.enter_context(nc.semaphore(name))
            self.cnt[name] = 0

    def inc(self, instr, name, amt=1):
        if name in ("sd", "sg") and amt == 1:
            amt = 16
        self.cnt[name] += amt
        instr.then_inc(self.sem[name], amt)
        return self.cnt[name]

    def wait(self, eng, name, value=None):
        v = self.cnt[name] if value is None else value
        if v > 0:
            eng.wait_ge(self.sem[name], v)

    def barrier(self):
        engs = {"sd": self.q, "sg": self.g, "sm": self.t, "sv": self.v, "sa": self.s}
        allsems = list(self.sem.keys())
        for ename, eng in engs.items():
            for sname in allsems:
                if sname != ename:
                    self.wait(eng, sname)


def build_program():
    nc = bass.Bass("TRN2", target_bir_lowering=False)

    x_in = nc.dram_tensor("x_in", [D, TOK], F32, kind="ExternalInput").ap()
    wfcT = nc.dram_tensor("wfcT", [D, HID2], F32, kind="ExternalInput").ap()
    wprojT = nc.dram_tensor("wprojT", [HID2, D], F32, kind="ExternalInput").ap()
    wrT = nc.dram_tensor("wrT", [D, 2], F32, kind="ExternalInput").ap()
    wq = nc.dram_tensor("wq", [D, 128], F32, kind="ExternalInput").ap()
    wk = nc.dram_tensor("wk", [D, 128], F32, kind="ExternalInput").ap()
    ws = nc.dram_tensor("ws", [D, 128], F32, kind="ExternalInput").ap()
    wv = nc.dram_tensor("wv", [D, 256], F32, kind="ExternalInput").ap()
    wg = nc.dram_tensor("wg", [D, 256], F32, kind="ExternalInput").ap()
    wo = nc.dram_tensor("wo", [D, D], F32, kind="ExternalInput").ap()
    gnb = nc.dram_tensor("gnb", [128, 256], F32, kind="ExternalInput").ap()
    triu_in = nc.dram_tensor("triu_in", [128, 128], F32, kind="ExternalInput").ap()
    triu2_in = nc.dram_tensor("triu2_in", [128, 256], F32, kind="ExternalInput").ap()
    ident_in = nc.dram_tensor("ident_in", [128, 128], F32, kind="ExternalInput").ap()

    y_out = nc.dram_tensor("y", [D, TOK], F32, kind="ExternalOutput").ap()
    if DEBUG_DUMP:
        dbg_ai = nc.dram_tensor("dbg_ai", [D, TOK], F32, kind="ExternalOutput").ap()
        dbg_ao = nc.dram_tensor("dbg_ao", [D, TOK], F32, kind="ExternalOutput").ap()
    if DEBUG_SCAN:
        dbg_sc = nc.dram_tensor("dbg_sc", [2, 22, 128, 256], F32, kind="ExternalOutput").ap()

    a_buf = nc.dram_tensor("a_buf", [HID2, TOK], F32).ap()
    ag_inA = nc.dram_tensor("ag_inA", [D, 512], F32)
    ag_inB = nc.dram_tensor("ag_inB", [D, 512], F32)
    ag_outA = nc.dram_tensor("ag_outA", [8 * D, 512], F32, addr_space="Shared")
    ag_outB = nc.dram_tensor("ag_outB", [8 * D, 512], F32, addr_space="Shared")
    a2a_in = nc.dram_tensor("a2a_in", [D, TOK], F32)
    a2a_out = nc.dram_tensor("a2a_out", [D, TOK], F32)

    with ExitStack() as ctx:
        e = Emit(nc, ctx)
        t, v, s, q, g = e.t, e.v, e.s, e.q, e.g
        _uid = [0]

        def uname(base):
            _uid[0] += 1
            return f"{base}_{_uid[0]}"

        def sb(name, shape):
            return ctx.enter_context(nc.sbuf_tensor(name, shape, F32))

        XACC = sb("XACC", [128, NCH, TOK])
        TRIU = sb("TRIU", [128, 128])
        TRIU2 = sb("TRIU2", [128, 256])
        IDENT = sb("IDENT", [128, 128])
        ONESC = sb("ONESC", [128, 1])
        ONESR = sb("ONESR", [1, 512])
        GNB = sb("GNB", [128, 256])
        MU_BC = sb("MU_BC", [128, TOK])
        R_BC = sb("R_BC", [128, TOK])
        CONSTB = sb("CONSTB", [128, 4])
        e.sq_guard = [0, 0]

        block = ctx.enter_context(nc.Block())

        @block.sync
        def _(_sync):
            e.inc(q.dma_start(out=XACC[:, :, :], in_=x_in.rearrange("(j p) t -> p j t", p=128)), "sd")
            e.inc(q.dma_start(out=TRIU[:, :], in_=triu_in[:, :]), "sd")
            e.inc(q.dma_start(out=TRIU2[:, :], in_=triu2_in[:, :]), "sd")
            e.inc(q.dma_start(out=IDENT[:, :], in_=ident_in[:, :]), "sd")
            e.inc(q.dma_start(out=GNB[:, :], in_=gnb[:, :]), "sd")
            e.wait(v, "sd")
            e.inc(v.memset(ONESC[:, :], 1.0), "sv")
            e.inc(v.memset(ONESR[:, :], 1.0), "sv")
            e.inc(g.memset(CONSTB[:, 0:1], -32.0), "sg", 1)
            e.inc(g.memset(CONSTB[:, 1:2], EPS), "sg", 1)
            e.inc(g.memset(CONSTB[:, 2:3], 1.5), "sg", 1)
            e.barrier()

            def emit_ln_stage():
                with ExitStack() as lctx, nc.named_scope(uname("LN")):
                    pR0 = lctx.enter_context(nc.psum_tensor(uname("pR0"), [1, 512], F32))
                    pR1 = lctx.enter_context(nc.psum_tensor(uname("pR1"), [1, 512], F32))
                    pBC = lctx.enter_context(nc.psum_tensor(uname("pBC"), [128, 512], F32))
                    XSQ = lctx.enter_context(nc.sbuf_tensor(uname("XSQ"), [128, 2, 512], F32))
                    MU = lctx.enter_context(nc.sbuf_tensor(uname("MU"), [1, TOK], F32))
                    RR = lctx.enter_context(nc.sbuf_tensor(uname("RR"), [1, TOK], F32))
                    T0 = lctx.enter_context(nc.sbuf_tensor(uname("T0"), [1, TOK], F32))
                    T1 = lctx.enter_context(nc.sbuf_tensor(uname("T1"), [1, TOK], F32))
                    T2 = lctx.enter_context(nc.sbuf_tensor(uname("T2"), [1, TOK], F32))
                    for vt in range(2):
                        sl = slice(512 * vt, 512 * (vt + 1))
                        for j in range(NCH):
                            if j >= 2:
                                e.wait(s, "sm", e.sq_guard[j % 2])
                            sa_c = e.inc(s.activation(XSQ[:, j % 2, :], XACC[:, j, sl], AF.Square), "sa")
                            t.matmul(pR0[:, :], ONESC[:, 0:1], XACC[:, j, sl],
                                     start=(j == 0), stop=(j == NCH - 1))
                            e.wait(t, "sa", sa_c)
                            mm2 = t.matmul(pR1[:, :], ONESC[:, 0:1], XSQ[:, j % 2, :],
                                           start=(j == 0), stop=(j == NCH - 1))
                            e.sq_guard[j % 2] = e.inc(mm2, "sm")
                        e.wait(s, "sm")
                        s.activation(MU[:, sl], pR0[:, :], AF.Copy, scale=1.0 / D)
                        sa_c = e.inc(s.activation(T0[:, sl], pR1[:, :], AF.Copy, scale=1.0 / D), "sa")
                        e.wait(t, "sa", sa_c)
                    e.wait(v, "sa")
                    e.inc(v.tensor_tensor(T1[:, :], MU[:, :], MU[:, :], ALU.mult), "sv")
                    sv_c = e.inc(v.tensor_tensor(T0[:, :], T0[:, :], T1[:, :], ALU.subtract), "sv")
                    e.wait(s, "sv", sv_c)
                    e.inc(s.activation(T2[:, :], T0[:, :], AF.Copy, bias=EPS), "sa")
                    sa_c = e.inc(s.activation(T1[:, :], T2[:, :], AF.Sqrt), "sa")
                    e.wait(v, "sa", sa_c)
                    e.inc(v.reciprocal(T0[:, :], T1[:, :]), "sv")
                    e.inc(v.tensor_tensor(T1[:, :], T0[:, :], T0[:, :], ALU.mult), "sv")
                    sv_c = e.inc(v.tensor_tensor(T1[:, :], T1[:, :], T2[:, :], ALU.mult), "sv")
                    e.wait(s, "sv", sv_c)
                    sa_c = e.inc(s.activation(T1[:, :], T1[:, :], AF.Copy, scale=-0.5, bias=1.5), "sa")
                    e.wait(v, "sa", sa_c)
                    e.inc(v.tensor_tensor(RR[:, :], T0[:, :], T1[:, :], ALU.mult), "sv")
                    e.wait(t, "sv")
                    for (row, dstbc) in ((MU, MU_BC), (RR, R_BC)):
                        for vt in range(2):
                            sl = slice(512 * vt, 512 * (vt + 1))
                            c1 = e.inc(t.matmul(pBC[:, :], ONESR[0:1, 0:128], row[:, sl],
                                                start=True, stop=True), "sm")
                            e.wait(v, "sm", c1)
                            cv = e.inc(v.tensor_copy(dstbc[:, sl], pBC[:, :]), "sv")
                            e.wait(t, "sv", cv)

            def norm_chunk(dst_ap, j):
                e.inc(v.tensor_tensor(dst_ap, XACC[:, j, :], MU_BC[:, :], ALU.subtract), "sv")
                return e.inc(v.tensor_tensor(dst_ap, dst_ap, R_BC[:, :], ALU.mult), "sv")

            def emit_ffn():
                with ExitStack() as fctx:
                    XL = fctx.enter_context(nc.sbuf_tensor(uname("XL"), [128, NCH, TOK], F32))
                    W0BC = fctx.enter_context(nc.sbuf_tensor(uname("W0BC"), [128, TOK], F32))
                    W1BC = fctx.enter_context(nc.sbuf_tensor(uname("W1BC"), [128, TOK], F32))
                    for j in range(NCH):
                        norm_chunk(XL[:, j, :], j)
                    WRT = fctx.enter_context(nc.sbuf_tensor(uname("WRT"), [128, NCH, 2], F32))
                    WRD = fctx.enter_context(nc.sbuf_tensor(uname("WRD"), [128, NCH], F32))
                    SR0 = fctx.enter_context(nc.sbuf_tensor(uname("SR0"), [1, TOK], F32))
                    SR1 = fctx.enter_context(nc.sbuf_tensor(uname("SR1"), [1, TOK], F32))
                    MR = fctx.enter_context(nc.sbuf_tensor(uname("MR"), [1, TOK], F32))
                    DL = fctx.enter_context(nc.sbuf_tensor(uname("DL"), [1, TOK], F32))
                    W0R = fctx.enter_context(nc.sbuf_tensor(uname("W0R"), [1, TOK], F32))
                    W1R = fctx.enter_context(nc.sbuf_tensor(uname("W1R"), [1, TOK], F32))
                    cd = e.inc(q.dma_start(out=WRT[:, :, :],
                                           in_=wrT.rearrange("(j p) c -> p j c", p=128)), "sd")
                    e.wait(v, "sd", cd)
                    wrd_c = e.inc(v.tensor_tensor(WRD[:, :], WRT[:, :, 1], WRT[:, :, 0], ALU.subtract), "sv")
                    with ExitStack() as actx, nc.named_scope(uname("FFNA")):
                        WF = actx.enter_context(nc.sbuf_tensor(uname("WF"), [128, 2, NCH, 128], F32))
                        ASL = actx.enter_context(nc.sbuf_tensor(uname("ASL"), [128, 2, TOK], F32))
                        AB1 = actx.enter_context(nc.sbuf_tensor(uname("AB1"), [128, 2, 512], F32))
                        TT = actx.enter_context(nc.sbuf_tensor(uname("TT"), [128, 2, 512], F32))
                        pH = [actx.enter_context(nc.psum_tensor(uname(f"pH{i}"), [128, 512], F32)) for i in range(4)]
                        pBC = [actx.enter_context(nc.psum_tensor(uname(f"pBC{i}"), [128, 512], F32)) for i in range(4)]
                        wf_ready = [0, 0]
                        wf_guard = [0, 0]
                        ph_free = [0, 0, 0, 0]
                        ab_guard = [0, 0]
                        asl_guard = [0, 0]
                        chunk_mm = {}

                        def emit_mm(k):
                            kb = k % 2
                            e.wait(q, "sm", wf_guard[kb])
                            wf_ready[kb] = e.inc(
                                q.dma_start(out=WF[:, kb, :, :],
                                            in_=wfcT[:, 128 * k:128 * (k + 1)]
                                            .rearrange("(j p) h -> p j h", p=128)), f"wf{kb}", 16)
                            for vt in range(2):
                                pb = 2 * kb + vt
                                sl = slice(512 * vt, 512 * (vt + 1))
                                e.wait(t, f"wf{kb}", wf_ready[kb])
                                e.wait(t, "sv", ph_free[pb])
                                for j in range(NCH):
                                    mm = t.matmul(pH[pb][:, :], WF[:, kb, j, :], XL[:, j, sl],
                                                  start=(j == 0), stop=(j == NCH - 1))
                                ph_mm = e.inc(mm, "sm")
                                chunk_mm[(k, vt)] = ph_mm
                                if vt == 1:
                                    wf_guard[kb] = ph_mm

                        def emit_evac(k):
                            kb = k % 2
                            wbc = W0BC if k < 64 else W1BC
                            for vt in range(2):
                                pb = 2 * kb + vt
                                sl = slice(512 * vt, 512 * (vt + 1))
                                ph_mm = chunk_mm[(k, vt)]
                                e.wait(s, "sm", ph_mm)
                                e.wait(s, "sv", ab_guard[vt])
                                sa1 = e.inc(s.activation(AB1[:, vt, :], pH[pb][:, :], AF.Abs, scale=0.25), "sa")
                                e.wait(v, "sa", sa1)
                                e.wait(v, f"as{kb}", asl_guard[kb])
                                sv1 = e.inc(v.scalar_tensor_tensor(TT[:, vt, :], pH[pb][:, :], 0.75,
                                                                   AB1[:, vt, :], ALU.mult, ALU.add), "sv")
                                ph_free[pb] = sv1
                                e.wait(s, "sv", sv1)
                                sa2 = e.inc(s.activation(TT[:, vt, :], TT[:, vt, :], AF.Square), "sa")
                                e.wait(v, "sa", sa2)
                                sv2 = e.inc(v.tensor_tensor(ASL[:, kb, sl], TT[:, vt, :], wbc[:, sl],
                                                            ALU.mult), "sv")
                                ab_guard[vt] = sv2
                            e.wait(g, "sv", sv2)
                            asl_guard[kb] = e.inc(
                                g.dma_start(out=a_buf[128 * k:128 * (k + 1), :], in_=ASL[:, kb, :]),
                                f"as{kb}", 16)

                        e.wait(t, "sv", wrd_c)
                        dl_cm = []
                        for vt in range(2):
                            sl = slice(512 * vt, 512 * (vt + 1))
                            for j in range(NCH):
                                mm = t.matmul(pBC[vt][0:1, :], WRD[:, j:j + 1], XL[:, j, sl],
                                              start=(j == 0), stop=(j == NCH - 1))
                            dl_cm.append(e.inc(mm, "sm"))
                        emit_mm(0)
                        emit_mm(1)
                        for vt in range(2):
                            sl = slice(512 * vt, 512 * (vt + 1))
                            e.wait(s, "sm", dl_cm[vt])
                            e.inc(s.copy(DL[:, sl], pBC[vt][0:1, :]), "sa")
                        e.wait(s, "sa")
                        e.inc(s.activation(SR1[:, :], DL[:, :], AF.Sigmoid), "sa")
                        sa_c = e.inc(s.activation(SR0[:, :], DL[:, :], AF.Sigmoid, scale=-1.0), "sa")
                        e.wait(v, "sa", sa_c)
                        e.inc(v.tensor_scalar(MR[:, :], DL[:, :], 0.0, None, ALU.is_gt), "sv")
                        e.inc(v.tensor_tensor(W1R[:, :], MR[:, :], SR1[:, :], ALU.mult), "sv")
                        e.inc(v.tensor_scalar(W1R[:, :], W1R[:, :], 0.5, None, ALU.mult), "sv")
                        e.inc(v.tensor_scalar(MR[:, :], MR[:, :], -0.5, 0.5, ALU.mult, ALU.add), "sv")
                        rt_c = e.inc(v.tensor_tensor(W0R[:, :], MR[:, :], SR0[:, :], ALU.mult), "sv")
                        e.wait(t, "sv", rt_c)
                        bc_specs = [(W0R, W0BC, 0), (W0R, W0BC, 1), (W1R, W1BC, 0), (W1R, W1BC, 1)]
                        bc_cm = []
                        for i, (row, dstbc, vt) in enumerate(bc_specs):
                            sl = slice(512 * vt, 512 * (vt + 1))
                            bc_cm.append(e.inc(t.matmul(pBC[i][:, :], ONESR[0:1, 0:128], row[:, sl],
                                                        start=True, stop=True), "sm"))
                        for i, (row, dstbc, vt) in enumerate(bc_specs):
                            sl = slice(512 * vt, 512 * (vt + 1))
                            e.wait(v, "sm", bc_cm[i])
                            e.inc(v.tensor_copy(dstbc[:, sl], pBC[i][:, :]), "sv")
                        emit_evac(0)
                        emit_evac(1)
                        for k in range(2, 128):
                            emit_mm(k)
                            emit_evac(k)
                    e.barrier()
                    with ExitStack() as bctx, nc.named_scope(uname("FFNB")):
                        WP = bctx.enter_context(nc.sbuf_tensor(uname("WP"), [128, 3, 1024], F32))
                        AT = bctx.enter_context(nc.sbuf_tensor(uname("AT"), [128, 3, 512], F32))
                        pO = [bctx.enter_context(nc.psum_tensor(uname(f"pO{i}"), [128, 512], F32)) for i in range(8)]
                        wp_ready = [0, 0, 0]
                        at_ready = [0, 0, 0]
                        buf_guard = [0, 0, 0]
                        po_free = 0
                        for d2h in range(2):
                            for th in range(2):
                                sl = slice(512 * th, 512 * (th + 1))
                                for k in range(128):
                                    kb = k % 3
                                    e.wait(q, "sm", buf_guard[kb])
                                    wp_ready[kb] = e.inc(
                                        q.dma_start(out=WP[:, kb, :],
                                                    in_=wprojT[128 * k:128 * (k + 1),
                                                               1024 * d2h:1024 * (d2h + 1)]), f"wp{kb}", 16)
                                    e.wait(g, "sm", buf_guard[kb])
                                    at_ready[kb] = e.inc(
                                        g.dma_start(out=AT[:, kb, :],
                                                    in_=a_buf[128 * k:128 * (k + 1), sl]), f"at{kb}", 16)
                                    e.wait(t, f"wp{kb}", wp_ready[kb])
                                    e.wait(t, f"at{kb}", at_ready[kb])
                                    if k == 0:
                                        e.wait(t, "sv", po_free)
                                    for c8 in range(8):
                                        mm = t.matmul(pO[c8][:, :], WP[:, kb, 128 * c8:128 * (c8 + 1)],
                                                      AT[:, kb, :], start=(k == 0), stop=(k == 127))
                                    buf_guard[kb] = e.inc(mm, "sm")
                                e.wait(v, "sm")
                                for c8 in range(8):
                                    j = 8 * d2h + c8
                                    iv = v.tensor_tensor(XACC[:, j, sl], XACC[:, j, sl], pO[c8][:, :], ALU.add)
                                    po_free = e.inc(iv, "sv")
                e.barrier()

            def emit_attn():
                with ExitStack() as nctx, nc.named_scope(uname("LN2AG")):
                    XLJ = nctx.enter_context(nc.sbuf_tensor(uname("XLJ"), [128, 2, TOK], F32))
                    dma_guard = [0, 0]
                    for j in range(NCH):
                        jb = j % 2
                        e.wait(v, f"xj{jb}", dma_guard[jb])
                        cv = norm_chunk(XLJ[:, jb, :], j)
                        e.wait(q, "sv", cv)
                        e.inc(q.dma_start(out=ag_inA.ap()[128 * j:128 * (j + 1), :],
                                          in_=XLJ[:, jb, 0:512]), f"xj{jb}", 16)
                        dma_guard[jb] = e.inc(
                            q.dma_start(out=ag_inB.ap()[128 * j:128 * (j + 1), :],
                                        in_=XLJ[:, jb, 512:1024]), f"xj{jb}", 16)
                    e.wait(g, "xj0")
                    e.wait(g, "xj1")
                cc_a = e.inc(g.collective_compute(
                    "AllGather", ALU.bypass, replica_groups=[list(range(8))],
                    ins=[ag_inA.ap().opt()], outs=[ag_outA.ap().opt()]), "sc")
                cc_b = e.inc(g.collective_compute(
                    "AllGather", ALU.bypass, replica_groups=[list(range(8))],
                    ins=[ag_inB.ap().opt()], outs=[ag_outB.ap().opt()]), "sc")
                if ATTN_SUB < 2:
                    return

                with ExitStack() as sctx, nc.named_scope(uname("SCAN")):
                    def ssb(name, shape):
                        return sctx.enter_context(nc.sbuf_tensor(name, shape, F32))
                    WQb, WKb, WSb = ssb("WQb", [128, NCH, 128]), ssb("WKb", [128, NCH, 128]), ssb("WSb", [128, NCH, 128])
                    WVb, WGb = ssb("WVb", [128, NCH, 256]), ssb("WGb", [128, NCH, 256])
                    XLT = ssb("XLT", [128, 2, NCH, 128])
                    QF = ssb(uname("QF"), [64, 2, 2, 128])
                    KT = ssb(uname("KT"), [128, 2, 128])
                    VT = ssb(uname("VT"), [128, 2, 256])
                    GT = ssb(uname("GT"), [128, 2, 256])
                    PT = ssb(uname("PT"), [128, 2, 128])
                    KF = ssb(uname("KF"), [64, 2, 128])
                    PF = ssb(uname("PF"), [64, 2, 128])
                    SM, W2M = ssb(uname("SM"), [128, 256]), ssb(uname("W2M"), [128, 256])
                    OKN, EXPT = ssb(uname("OKN"), [128, 128]), ssb(uname("EXPT"), [128, 128])
                    PV2 = ssb(uname("PV2"), [128, 128])
                    PV2F = ssb(uname("PV2F"), [64, 2, 128])
                    CD, RD = ssb(uname("CD"), [128, 128]), ssb(uname("RD"), [128, 128])
                    ON = ssb(uname("ON"), [128, 2, 256])
                    OF = ssb(uname("OF"), [128, 256])
                    SQG = ssb(uname("SQG"), [128, 256])
                    SGT = ssb(uname("SGT"), [128, 256])
                    SG = ssb(uname("SG"), [128, 2, 256])
                    NM, RSm = ssb(uname("NM"), [128, 2]), ssb(uname("RSm"), [128, 2])
                    GSm, MSm = ssb(uname("GSm"), [128, 2]), ssb(uname("MSm"), [128, 2])
                    RN = ssb(uname("RN"), [128, 2])
                    RSW, RSR = ssb(uname("RSW"), [128, 64]), ssb(uname("RSR"), [128, 64])
                    HST = ssb(uname("HST"), [64, 2, 64])
                    HVST = ssb(uname("HVST"), [64, 2, 128])
                    D0R = ssb(uname("D0R"), [1, 128])
                    pAp = [sctx.enter_context(nc.psum_tensor(uname(f"pA{i}"), [128, 512], F32)) for i in range(2)]
                    pBp = [sctx.enter_context(nc.psum_tensor(uname(f"pB{i}"), [128, 512], F32)) for i in range(2)]
                    pC = sctx.enter_context(nc.psum_tensor(uname("pC"), [128, 512], F32))
                    pD = sctx.enter_context(nc.psum_tensor(uname("pD"), [128, 512], F32))
                    pE = sctx.enter_context(nc.psum_tensor(uname("pE"), [128, 512], F32))
                    pF = sctx.enter_context(nc.psum_tensor(uname("pF"), [128, 512], F32))
                    e.wait(q, "xj0")
                    e.wait(q, "xj1")
                    for (buf, wsrc) in ((WQb, wq), (WKb, wk), (WSb, ws), (WVb, wv), (WGb, wg)):
                        e.inc(q.dma_start(out=buf[:, :, :], in_=wsrc.rearrange("(j p) c -> p j c", p=128)), "sd")
                    e.wait(v, "sd")
                    e.inc(v.memset(RSW[:, :], 1.0), "sv")
                    xlt_ready = [0, 0]
                    proj_mm = [0, 0]
                    ev_b = [0, 0]
                    pt_c = [0, 0]
                    bank_free = [0, 0]
                    tile_free = [0, 0]
                    of_guard = 0
                    st_upd_prev = 0
                    d0_ev_prev = 0
                    sm_ev_prev = 0
                    w2_ev_prev = 0
                    of_c_prev = 0

                    def emit_xlt(tcx):
                        cbx = tcx % 2
                        blkx, subx = tcx // 8, tcx % 8
                        agt, agcc = (ag_outA, cc_a) if subx < 4 else (ag_outB, cc_b)
                        e.wait(q, "sc", agcc)
                        e.wait(q, "sm", proj_mm[cbx])
                        xlt_ready[cbx] = e.inc(
                            q.dma_start(out=XLT[:, cbx, :, :],
                                        in_=agt.ap()[2048 * blkx:2048 * (blkx + 1),
                                                     128 * (subx % 4):128 * (subx % 4 + 1)]
                                        .rearrange("(j p) t -> p j t", p=128)), f"xt{cbx}", 16)

                    def proj_gen(tcx):
                        par = tcx % 2
                        pA, pB = pAp[par], pBp[par]
                        e.wait(t, f"xt{par}", xlt_ready[par])
                        e.wait(t, "sv", bank_free[par])
                        for j in range(NCH):
                            st0 = (j == 0)
                            lst = (j == NCH - 1)
                            t.matmul(pA[:, 256:384], XLT[:, par, j, :], WKb[:, j, :], start=st0, stop=lst)
                            t.matmul(pA[:, 384:512], XLT[:, par, j, :], WSb[:, j, :], start=False, stop=lst)
                            t.matmul(pA[0:64, 0:128], WQb[:, j, 0:64], XLT[:, par, j, :], start=False, stop=lst)
                            t.matmul(pA[0:64, 128:256], WQb[:, j, 64:128], XLT[:, par, j, :], start=False, stop=lst)
                            t.matmul(pB[:, 0:256], XLT[:, par, j, :], WVb[:, j, :], start=st0, stop=lst)
                            mm = t.matmul(pB[:, 256:512], XLT[:, par, j, :], WGb[:, j, :], start=False, stop=lst)
                            if lst:
                                proj_mm[par] = e.inc(mm, "sm")
                            yield

                    def emit_evacs(tcx):
                        par = tcx % 2
                        pA, pB = pAp[par], pBp[par]
                        e.wait(v, "sm", proj_mm[par])
                        e.wait(v, "sm", tile_free[par])
                        e.inc(v.tensor_scalar(QF[:, par, 0, :], pA[0:64, 0:128], 0.125, None, ALU.mult), "sv")
                        e.inc(v.tensor_scalar(QF[:, par, 1, :], pA[0:64, 128:256], 0.125, None, ALU.mult), "sv")
                        e.inc(v.tensor_copy(KT[:, par, :], pA[:, 256:384]), "sv")
                        sv_cl = e.inc(v.tensor_scalar(PT[:, par, :], pA[:, 384:512], -32.0, 32.0,
                                                      ALU.max, ALU.min), "sv")
                        e.inc(v.tensor_copy(VT[:, par, :], pB[:, 0:256]), "sv")
                        ev_b[par] = e.inc(v.tensor_copy(GT[:, par, :], pB[:, 256:512]), "sv")
                        e.wait(s, "sv", ev_b[par])
                        sgt_c = e.inc(s.activation(SGT[:, :], GT[:, par, :], AF.Tanh, scale=0.5), "sa")
                        pt_c[par] = e.inc(s.activation(PT[:, par, :], PT[:, par, :], AF.Exp,
                                                       bias=CONSTB[:, 0:1]), "sa")
                        e.wait(v, "sa", sgt_c)
                        e.inc(v.tensor_scalar(SG[:, par, :], SGT[:, :], 0.5, 0.5, ALU.mult, ALU.add), "sv")
                        e.inc(v.tensor_tensor(SG[:, par, :], SG[:, par, :], GT[:, par, :], ALU.mult), "sv")

                    def emit_ofinal(tc):
                        nonlocal of_guard, of_c_prev
                        if tc < 0:
                            return
                        par = tc % 2
                        pA = pAp[par]
                        blk, sub = tc // 8, tc % 8
                        e.wait(t, "sv", ov_free_arr[par])
                        t.matmul(pF[:, 256:384], ON[:, par, 0:128], IDENT[:, :], start=False, stop=True,
                                 is_transpose=True)
                        otr = e.inc(t.matmul(pF[:, 384:512], ON[:, par, 128:256], IDENT[:, :], start=False,
                                             stop=True, is_transpose=True), "sm")
                        e.wait(v, "sm", otr)
                        e.wait(v, "sg", of_guard)
                        of_c = e.inc(v.tensor_copy(OF[:, :], pF[:, 256:512]), "sv")
                        of_c_prev = of_c
                        e.wait(g, "sv", of_c)
                        for h in range(2):
                            of_guard = e.inc(g.dma_start(
                                out=a2a_in.ap()[256 * blk + 128 * h:256 * blk + 128 * (h + 1),
                                                128 * sub:128 * (sub + 1)],
                                in_=OF[:, 128 * h:128 * (h + 1)]), "sg", 16)

                    def emit_tail(tc, projit):
                        nonlocal of_guard, st_upd_prev, d0_ev_prev, sm_ev_prev, w2_ev_prev, of_c_prev
                        par = tc % 2
                        pA = pAp[par]
                        blk, sub = tc // 8, tc % 8

                        def pj(n=1):
                            for _ in range(n):
                                try:
                                    next(projit)
                                except StopIteration:
                                    return
                        if tc % 32 == 0:
                            e.wait(v, "sm", st_guard_mm[0])
                            e.inc(v.memset(HST[:, :, :], 0.0), "sv")
                            e.inc(v.memset(HVST[:, :, :], 0.0), "sv")
                            sv_c = e.inc(v.memset(D0R[:, :], 0.0), "sv")
                            e.wait(t, "sv", sv_c)
                        pj(2)
                        e.wait(t, "sv", st_upd_prev)
                        e.wait(t, "sa", d0_ev_prev)
                        e.wait(t, "sv", ev_b[par])
                        e.wait(t, "sa", pt_c[par])
                        t.matmul(pA[0:64, 0:128], KT[:, par, 0:64], IDENT[:, :], start=True, stop=True,
                                 is_transpose=True)
                        t.matmul(pA[0:64, 128:256], KT[:, par, 64:128], IDENT[:, :], start=False, stop=True,
                                 is_transpose=True)
                        t.matmul(pA[0:64, 256:384], PT[:, par, 0:64], IDENT[:, :], start=False, stop=True,
                                 is_transpose=True)
                        tr_mm = e.inc(t.matmul(pA[0:64, 384:512], PT[:, par, 64:128], IDENT[:, :],
                                               start=False, stop=True, is_transpose=True), "sm")
                        e.wait(v, "sm", tr_mm)
                        e.inc(v.tensor_copy(KF[:, 0, :], pA[0:64, 0:128]), "sv")
                        e.inc(v.tensor_copy(KF[:, 1, :], pA[0:64, 128:256]), "sv")
                        e.inc(v.tensor_copy(PF[:, 0, :], pA[0:64, 256:384]), "sv")
                        tr_ev = e.inc(v.tensor_copy(PF[:, 1, :], pA[0:64, 384:512]), "sv")
                        emit_ofinal(tc - 1)
                        pj(2)
                        e.wait(t, "sv", sm_ev_prev)
                        e.wait(t, "sv", tr_ev)
                        for h in range(2):
                            hs = slice(64 * h, 64 * (h + 1))
                            t.matmul(pC[:, hs], TRIU[:, :], PT[:, par, hs], start=(h == 0), stop=False)
                            t.matmul(pC[:, hs], ONESR[0:1, 0:128], D0R[0:1, hs], start=False, stop=True)
                        for h in range(2):
                            hs = slice(64 * h, 64 * (h + 1))
                            d0s = slice(128 + 64 * h, 128 + 64 * (h + 1))
                            t.matmul(pC[0:1, d0s], ONESC[:, 0:1], PT[:, par, hs], start=False, stop=False)
                            mm = t.matmul(pC[0:1, d0s], ONESR[0:1, 0:1], D0R[0:1, hs], start=False, stop=True)
                        cumd_mm = e.inc(mm, "sm")
                        e.wait(v, "sm", cumd_mm)
                        e.inc(v.tensor_copy(CD[:, :], pC[:, 0:128]), "sv")
                        rd_ev = e.inc(v.reciprocal(RD[:, :], CD[:, :]), "sv")
                        e.wait(s, "sm", cumd_mm)
                        e.wait(s, "sv", rd_ev)
                        d0_ev = e.inc(s.copy(D0R[0:1, :], pC[0:1, 128:256]), "sa")
                        pj(3)
                        e.wait(t, "sv", rd_ev)
                        e.wait(t, "sa", d0_ev)
                        for h in range(2):
                            mm = t.matmul(pC[:, 256 + 128 * h:256 + 128 * (h + 1)],
                                          KF[:, h, :], QF[:, par, h, :], start=False, stop=True)
                        st_mm = e.inc(mm, "sm")
                        e.wait(v, "sm", st_mm)
                        sm_ev = e.inc(v.tensor_tensor(SM[:, :], pC[:, 256:512], TRIU2[:, :], ALU.mult), "sv")
                        sm_ev_prev = sm_ev
                        pj(2)
                        e.wait(t, "sv", w2_ev_prev)
                        e.wait(t, "sv", sm_ev)
                        for h in range(2):
                            hs = slice(64 * h, 64 * (h + 1))
                            t.matmul(pD[:, hs], QF[:, par, h, :], HST[:, h, :], start=(h == 0), stop=False)
                            mm = t.matmul(pD[:, hs], SM[:, 128 * h:128 * (h + 1)], PT[:, par, hs],
                                          start=False, stop=True)
                        ok_mm = e.inc(mm, "sm")
                        for h in range(2):
                            hs = slice(64 * h, 64 * (h + 1))
                            mm = t.matmul(pE[0:64, hs], KT[:, par, hs], PT[:, par, hs],
                                          start=(h == 0), stop=True)
                        h_mm = e.inc(mm, "sm")
                        e.wait(v, "sm", ok_mm)
                        e.inc(v.tensor_tensor(OKN[:, :], pD[:, 0:128], RD[:, :], ALU.mult), "sv")
                        nm_c = e.inc(v.reduce_max(NM[:, :], OKN.rearrange("p (g x) -> p g x", g=2),
                                                  axis=AX, negate=True), "sv")
                        e.wait(s, "sv", nm_c)
                        sa_c = 0
                        for h in range(2):
                            hs = slice(64 * h, 64 * (h + 1))
                            sa_c = e.inc(s.activation(EXPT[:, hs], OKN[:, hs], AF.Exp,
                                                      bias=NM[:, h:h + 1]), "sa")
                        e.wait(v, "sa", sa_c)
                        rs_c = e.inc(v.reduce_sum(RSm[:, :], EXPT.rearrange("p (g x) -> p g x", g=2),
                                                  axis=AX), "sv")
                        e.wait(s, "sv", rs_c)
                        sa_c = e.inc(s.copy(RSW[:, 0:2], RSm[:, :]), "sa")
                        e.wait(v, "sa", sa_c)
                        e.inc(v.reciprocal(RSR[:, :], RSW[:, :]), "sv")
                        e.inc(v.tensor_tensor(PV2.rearrange("p (g x) -> p g x", g=2),
                                              EXPT.rearrange("p (g x) -> p g x", g=2),
                                              RSR[:, 0:2].rearrange("p (g x) -> p g x", x=1)
                                              .broadcast_to((128, 2, 64)), ALU.mult), "sv")
                        pv2_c = e.inc(v.tensor_tensor(PV2[:, :], PV2[:, :], RD[:, :], ALU.mult), "sv")
                        pj(4)
                        e.wait(t, "sv", pv2_c)
                        t.matmul(pA[0:64, 0:128], PV2[:, 0:64], IDENT[:, :], start=True, stop=True,
                                 is_transpose=True)
                        tr2 = e.inc(t.matmul(pA[0:64, 128:256], PV2[:, 64:128], IDENT[:, :],
                                             start=False, stop=True, is_transpose=True), "sm")
                        e.wait(v, "sm", tr2)
                        e.inc(v.tensor_copy(PV2F[:, 0, :], pA[0:64, 0:128]), "sv")
                        pv2f_c = e.inc(v.tensor_copy(PV2F[:, 1, :], pA[0:64, 128:256]), "sv")
                        bank_free[par] = pv2f_c
                        pj(2)
                        e.wait(t, "sv", pv2f_c)
                        for h in range(2):
                            mm = t.matmul(pD[:, 128 + 128 * h:128 + 128 * (h + 1)],
                                          PF[:, h, :], PV2F[:, h, :], start=False, stop=True)
                        w2_mm = e.inc(mm, "sm")
                        e.wait(v, "sm", w2_mm)
                        w2_ev = e.inc(v.tensor_tensor(W2M[:, :], pD[:, 128:384], TRIU2[:, :], ALU.mult), "sv")
                        w2_ev_prev = w2_ev
                        pj(1)
                        e.wait(t, "sv", of_c_prev)
                        e.wait(t, "sv", w2_ev)
                        for h in range(2):
                            vs = slice(128 * h, 128 * (h + 1))
                            t.matmul(pF[:, vs], PV2F[:, h, :], HVST[:, h, :], start=(h == 0), stop=False)
                            mm = t.matmul(pF[:, vs], W2M[:, vs], VT[:, par, vs], start=False, stop=True)
                        ov_mm = e.inc(mm, "sm")
                        for h in range(2):
                            vs = slice(128 + 128 * h, 128 + 128 * (h + 1))
                            mm = t.matmul(pE[0:64, vs], PT[:, par, 64 * h:64 * (h + 1)],
                                          VT[:, par, 128 * h:128 * (h + 1)], start=False, stop=True)
                        hv_mm = e.inc(mm, "sm")
                        tile_free[par] = hv_mm
                        st_guard_mm[0] = hv_mm
                        e.wait(v, "sm", hv_mm)
                        e.inc(v.tensor_tensor(HST[:, :, :], HST[:, :, :], pE[0:64, 0:128], ALU.add), "sv")
                        st_upd = e.inc(v.tensor_tensor(HVST[:, :, :], HVST[:, :, :], pE[0:64, 128:384],
                                                       ALU.add), "sv")
                        st_upd_prev = st_upd
                        e.wait(s, "sm", ov_mm)
                        sq_c = e.inc(s.activation(SQG[:, :], pF[:, 0:256], AF.Square), "sa")
                        e.wait(v, "sa", sq_c)
                        gs_c = e.inc(v.reduce_sum(GSm[:, :], SQG.rearrange("p (g x) -> p g x", g=2),
                                                  axis=AX), "sv")
                        e.wait(s, "sv", gs_c)
                        e.inc(s.activation(MSm[:, :], GSm[:, :], AF.Copy, scale=1.0 / DV, bias=EPS), "sa")
                        e.inc(s.activation(RN[:, :], MSm[:, :], AF.Ln), "sa")
                        rw_c = e.inc(s.activation(RN[:, :], RN[:, :], AF.Exp, scale=-0.5), "sa")
                        e.wait(v, "sa", rw_c)
                        e.inc(v.tensor_tensor(ON[:, par, :].rearrange("p (g x) -> p g x", g=2),
                                              pF[:, 0:256].rearrange("p (g x) -> p g x", g=2),
                                              RN[:, 0:2].rearrange("p (g x) -> p g x", x=1)
                                              .broadcast_to((128, 2, 128)), ALU.mult), "sv")
                        e.inc(v.tensor_tensor(ON[:, par, :], ON[:, par, :], GNB[:, :], ALU.mult), "sv")
                        ov_free_arr[par] = e.inc(v.tensor_tensor(ON[:, par, :], ON[:, par, :], SG[:, par, :], ALU.mult), "sv")
                        pj(8)

                    st_guard_mm = [0]
                    ov_free_arr = [0, 0]
                    emit_xlt(0)
                    g0 = proj_gen(0)
                    for _ in g0:
                        pass
                    emit_evacs(0)
                    for tc in range(NTC):
                        if tc + 1 < NTC:
                            emit_xlt(tc + 1)
                            projit = proj_gen(tc + 1)
                        else:
                            projit = iter(())
                        emit_tail(tc, projit)
                        if tc + 1 < NTC:
                            emit_evacs(tc + 1)
                    emit_ofinal(NTC - 1)
                    e.wait(g, "sg")
                e.barrier()
                if SCAN_LVL < 8:
                    return
                with allow_ici_all_to_all():
                    cc2 = e.inc(g.collective_compute(
                        "AllToAll", ALU.bypass, replica_groups=[list(range(8))],
                        ins=[a2a_in.ap().opt()], outs=[a2a_out.ap().opt()]), "sc")
                e.wait(g, "sc", cc2)
                e.barrier()
                if ATTN_SUB < 3:
                    return
                with ExitStack() as wctx, nc.named_scope(uname("WO")):
                    OFM = wctx.enter_context(nc.sbuf_tensor(uname("OFM"), [128, NCH, TOK], F32))
                    WOB = wctx.enter_context(nc.sbuf_tensor(uname("WOB"), [128, 2, NCH, 128], F32))
                    pW = [wctx.enter_context(nc.psum_tensor(uname(f"pW{i}"), [128, 512], F32)) for i in range(4)]
                    cd = e.inc(q.dma_start(out=OFM[:, :, :],
                                           in_=a2a_out.ap().rearrange("(j p) t -> p j t", p=128)), "ofm", 16)
                    wo_ready = [0, 0]
                    wo_guard = [0, 0]
                    pw_free = [0, 0, 0, 0]
                    for d2 in range(NCH):
                        db = d2 % 2
                        e.wait(q, "sm", wo_guard[db])
                        wo_ready[db] = e.inc(
                            q.dma_start(out=WOB[:, db, :, :],
                                        in_=wo[:, 128 * d2:128 * (d2 + 1)]
                                        .rearrange("(j p) c -> p j c", p=128)), f"wo{db}", 16)
                        e.wait(t, f"wo{db}", wo_ready[db])
                        e.wait(t, "ofm", cd)
                        for th in range(2):
                            pb = 2 * db + th
                            sl = slice(512 * th, 512 * (th + 1))
                            e.wait(t, "sv", pw_free[pb])
                            for jv in range(NCH):
                                mm = t.matmul(pW[pb][:, :], WOB[:, db, jv, :], OFM[:, jv, sl],
                                              start=(jv == 0), stop=(jv == NCH - 1))
                            cm = e.inc(mm, "sm")
                            if th == 1:
                                wo_guard[db] = cm
                            e.wait(v, "sm", cm)
                            iv = v.tensor_tensor(XACC[:, d2, sl], XACC[:, d2, sl], pW[pb][:, :], ALU.add)
                            pw_free[pb] = e.inc(iv, "sv")
                e.barrier()

            if STAGE_FFN1:
                emit_ln_stage()
                emit_ffn()
            if STAGE_ATTN:
                emit_ln_stage()
                emit_attn()
            if STAGE_FFN2:
                emit_ln_stage()
                emit_ffn()

            e.barrier()
            cst = e.inc(q.dma_start(out=y_out.rearrange("(j p) t -> p j t", p=128),
                                    in_=XACC[:, :, :]), "sd")
            if DEBUG_DUMP:
                e.inc(q.dma_start(out=dbg_ai[:, :], in_=a2a_in.ap()[:, :]), "sd")
                e.inc(q.dma_start(out=dbg_ao[:, :], in_=a2a_out.ap()[:, :]), "sd")
            e.wait(q, "sd")

    return nc


_NC_CACHE = {}


def _get_nc():
    key = (STAGE_FFN1, STAGE_ATTN, STAGE_FFN2, ATTN_SUB, SCAN_LVL, DEBUG_DUMP, DEBUG_SCAN)
    if key not in _NC_CACHE:
        _NC_CACHE[key] = build_program()
    return _NC_CACHE[key]


def kernel(L0_attn_hidden_states, L0_attn_Wq, L0_attn_Wk, L0_attn_Wv, L0_attn_Wg,
           L0_attn_Ws, L0_attn_Wo, L0_attn_g_norm_weight, L0_ffn_w_router,
           L0_ffn_w_fc, L0_ffn_w_proj):
    x = np.asarray(L0_attn_hidden_states, np.float32).reshape(8192, 2048)
    wfcT_h = np.ascontiguousarray(
        np.transpose(np.asarray(L0_ffn_w_fc, np.float32), (2, 0, 1)).reshape(2048, 16384))
    wprojT_h = np.ascontiguousarray(
        np.transpose(np.asarray(L0_ffn_w_proj, np.float32), (0, 2, 1)).reshape(16384, 2048))
    wrT_h = np.ascontiguousarray(np.asarray(L0_ffn_w_router, np.float32).T)
    Wq = np.asarray(L0_attn_Wq, np.float32)
    Wk = np.asarray(L0_attn_Wk, np.float32)
    Ws = np.asarray(L0_attn_Ws, np.float32)
    Wv = np.asarray(L0_attn_Wv, np.float32)
    Wg = np.asarray(L0_attn_Wg, np.float32)
    Wo_h = np.ascontiguousarray(np.asarray(L0_attn_Wo, np.float32))
    gn = np.asarray(L0_attn_g_norm_weight, np.float32)
    gnb_h = np.ascontiguousarray(np.broadcast_to(np.tile(gn, 2)[None, :], (128, 256)))
    triu_h = np.triu(np.ones((128, 128), np.float32))
    ident_h = np.eye(128, dtype=np.float32)

    in_maps = []
    for c in range(8):
        in_maps.append({
            "x_in": np.ascontiguousarray(x[1024 * c:1024 * (c + 1), :].T),
            "wfcT": wfcT_h, "wprojT": wprojT_h, "wrT": wrT_h,
            "wq": np.ascontiguousarray(Wq[:, 128 * c:128 * (c + 1)]),
            "wk": np.ascontiguousarray(Wk[:, 128 * c:128 * (c + 1)]),
            "ws": np.ascontiguousarray(Ws[:, 128 * c:128 * (c + 1)]),
            "wv": np.ascontiguousarray(Wv[:, 256 * c:256 * (c + 1)]),
            "wg": np.ascontiguousarray(Wg[:, 256 * c:256 * (c + 1)]),
            "wo": Wo_h, "gnb": gnb_h, "triu_in": triu_h, "triu2_in": np.ascontiguousarray(np.tile(triu_h, (1, 2))), "ident_in": ident_h,
        })

    nc = _get_nc()
    res = run_bass_kernel_spmd(nc, in_maps, core_ids=list(range(8)))
    out = np.empty((8192, 2048), np.float32)
    for c in range(8):
        out[1024 * c:1024 * (c + 1), :] = res.results[c]["y"].T
    return out.reshape(2, 4096, 2048)
